# version 02
# baseline (speedup 1.0000x reference)
import zlib
import numpy as np
import ml_dtypes
import concourse.bass as bass
import concourse.mybir as mybir
from concourse.bass_utils import run_bass_kernel_spmd

F32 = mybir.dt.float32
BF16 = mybir.dt.bfloat16
AF = mybir.ActivationFunctionType
ALU = mybir.AluOpType

N_CORES = 8
B_PER_CORE = 2
T = 2048
D = 1024
DC = 8
D_INNER = 2048
HEADS = 16
DH = 64
WIN = 256
NW = 8
HID = 5632
HC = 22
HIDH = 2816
MCH = 512
NMC = 4

MILESTONE = "full"
PARANOID = True
RELAX_SAME = ("tensor",)
DEBUG_DUMPS = []


class _Op:
    __slots__ = ("eng", "fn", "deps", "is_dma", "inc_no", "gidx", "fam")

    def __init__(self, eng, fn, deps, is_dma, gidx, fam=None):
        self.eng, self.fn, self.deps, self.is_dma = eng, fn, deps, is_dma
        self.inc_no = None
        self.gidx = gidx
        self.fam = fam


class Prog:
    ENGINES = ("tensor", "vector", "scalar", "gpsimd", "sync")

    def __init__(self, nc):
        self.nc = nc
        self.ops = {e: [] for e in self.ENGINES}
        self.gcount = 0
        self.regions = {}
        self.last_fam_dma = {}

    def add(self, eng, fn, reads=(), writes=(), deps=(), is_dma=False, fam=None):
        dl = list(deps)
        for r in reads:
            ent = self.regions.get(r)
            if ent is not None and ent[0] is not None:
                dl.append(ent[0])
        for w in writes:
            ent = self.regions.get(w)
            if ent is not None:
                if ent[0] is not None:
                    dl.append(ent[0])
                dl.extend(ent[1])
        if is_dma:
            fam = fam or (writes[0] if writes else "misc")
            nslots = {"sync": 44, "gpsimd": 20, "scalar": 12,
                      "vector": 4, "tensor": 4}[eng]
            fam = f"d{zlib.crc32((eng + ':' + fam).encode()) % nslots}"
            prev = self.last_fam_dma.get((eng, fam))
            if prev is not None:
                dl.append(prev)
        op = _Op(eng, fn, dl, is_dma, self.gcount, fam)
        self.gcount += 1
        self.ops[eng].append(op)
        if is_dma:
            self.last_fam_dma[(eng, fam)] = op
        for r in reads:
            self.regions.setdefault(r, [None, []])[1].append(op)
        for w in writes:
            self.regions[w] = [op, []]
        return op

    def dma(self, eng, fn, reads=(), writes=(), deps=(), fam=None):
        return self.add(eng, fn, reads, writes, deps, is_dma=True, fam=fam)

    def finalize_and_emit(self):
        nc = self.nc
        needed = set()
        for e in self.ENGINES:
            for op in self.ops[e]:
                if op.is_dma:
                    needed.add(id(op))
                for d in op.deps:
                    if d.eng != op.eng or d.is_dma or \
                       (PARANOID and d.eng not in RELAX_SAME):
                        needed.add(id(d))
        sems, counters = {}, {}

        def op_key(op):
            return (op.eng, op.fam) if op.is_dma else (op.eng, None)

        for e in self.ENGINES:
            for op in self.ops[e]:
                if id(op) in needed:
                    key = op_key(op)
                    if key not in sems:
                        sems[key] = nc.alloc_semaphore(
                            f"s_{key[0]}_{key[1] or 'c'}")
                        counters[key] = 0
                    counters[key] += 1
                    op.inc_no = counters[key]
        for e in self.ENGINES:
            for op in self.ops[e]:
                for d in op.deps:
                    if d.gidx >= op.gidx:
                        raise RuntimeError(f"dep ordering violation on {e}")

        def mk(engine_name):
            def _run(eng):
                waited = {}
                for op in self.ops[engine_name]:
                    for d in op.deps:
                        if d.eng == engine_name and not d.is_dma and \
                           ((not PARANOID) or engine_name in RELAX_SAME):
                            continue
                        key = (d.eng, d.fam) if d.is_dma else (d.eng, None)
                        sem = sems[key]
                        val = d.inc_no * (16 if d.is_dma else 1)
                        if waited.get(key, -1) >= val:
                            continue
                        eng.wait_ge(sem, val)
                        waited[key] = val
                    ins = op.fn(eng)
                    if op.inc_no is not None:
                        key = (engine_name, op.fam) if op.is_dma else (engine_name, None)
                        ins.then_inc(sems[key], 16 if op.is_dma else 1)
                if engine_name == "gpsimd":
                    for key, cnt in counters.items():
                        if key[1] is not None:
                            if waited.get(key, -1) < cnt * 16:
                                eng.wait_ge(sems[key], cnt * 16)
            return _run

        with self.nc.Block() as block:
            block.tensor(mk("tensor"))
            block.vector(mk("vector"))
            block.scalar(mk("scalar"))
            block.gpsimd(mk("gpsimd"))
            block.sync(mk("sync"))
        return sum(len(v) for v in self.ops.values())


def _host_consts():
    c = {}
    half = np.arange(0, DH, 2, dtype=np.float64)
    inv_freq = 1.0 / 10000.0 ** (half / DH)
    t = np.arange(T, dtype=np.float64)
    freqs = np.concatenate([t[:, None] * inv_freq[None, :]] * 2, -1)
    base_scale = (half + 0.4 * DH) / (1.4 * DH)
    power = (t - T // 2) / (WIN // 2)
    scale = np.concatenate([base_scale[None, :] ** power[:, None]] * 2, -1)
    cos, sin = np.cos(freqs), np.sin(freqs)
    inv = 1.0 / scale
    for name, tab in (("cs_q", cos * scale), ("ss_q", sin * scale),
                      ("cs_k", cos * inv), ("ss_k", sin * inv)):
        fm = np.ascontiguousarray(tab.T)
        fm = np.concatenate([fm, fm], 0)
        c[name] = fm.astype(ml_dtypes.bfloat16)
    jl = np.arange(512)[:, None]
    il = np.arange(256)[None, :]
    allowed = (jl - il >= 0) & (jl - il <= WIN)
    mask = np.where(allowed, 0.0, -1e30).astype(np.float32)
    c["maskT"] = np.ascontiguousarray(
        mask.reshape(4, 128, 256).transpose(1, 0, 2)).astype(ml_dtypes.bfloat16)
    h2 = np.zeros((128, 2), np.float32)
    h2[0:64, 0] = 1.0
    h2[64:128, 1] = 1.0
    c["h2"] = h2.astype(ml_dtypes.bfloat16)
    R = np.zeros((128, 128), np.float32)
    for h in range(2):
        o = h * 64
        for j in range(32):
            R[o + j + 32, o + j] = -1.0
        for j in range(32, 64):
            R[o + j - 32, o + j] = 1.0
    c["rperm"] = R.astype(ml_dtypes.bfloat16)
    c["ones_bf"] = np.ones((128, 1), np.float32).astype(ml_dtypes.bfloat16)
    c["ones_f"] = np.ones((128, 1), np.float32)
    c["onesrow_f"] = np.ones((1, 128), np.float32)
    c["onesrow_bf"] = np.ones((1, 128), np.float32).astype(ml_dtypes.bfloat16)
    c["ident_bf"] = np.eye(128, dtype=np.float32).astype(ml_dtypes.bfloat16)
    h2t = np.zeros((2, 128), np.float32)
    h2t[0, 0:64] = 1.0
    h2t[1, 64:128] = 1.0
    c["h2t"] = h2t.astype(ml_dtypes.bfloat16)
    return c


def _pp(vec):
    return np.ascontiguousarray(np.asarray(vec).reshape(DC, 128).T)


def _prep_weights(iw):
    w = {}
    w["W_in"] = np.asarray(iw["L0_attn_W_in"])
    w["W_out"] = np.asarray(iw["L0_attn_W_out"])
    w["W_dt"] = np.asarray(iw["L0_attn_W_dt"])
    xd = np.asarray(iw["L0_attn_W_xdbl"])
    xdn = np.zeros((D, 128), np.float32)
    xdn[:, 0:16] = xd[:, 64:80]
    xdn[:, 32:48] = xd[:, 80:96]
    xdn[:, 64:128] = xd[:, 0:64]
    w["W_xdbl"] = xdn
    cxw = np.asarray(iw["L0_attn_conv_x_w"])[:, 0, :]
    czw = np.asarray(iw["L0_attn_conv_z_w"])[:, 0, :]
    cxw2 = np.stack([_pp(cxw[k]) for k in range(4)], axis=2)
    czw2 = np.stack([_pp(czw[k]) for k in range(4)], axis=2)
    mv = np.zeros((128, 112), np.float32)
    mv[:, 0:32] = cxw2.reshape(128, 32)
    mv[:, 32:64] = czw2.reshape(128, 32)
    mv[:, 64:72] = _pp(iw["L0_attn_conv_x_b"])
    mv[:, 72:80] = _pp(iw["L0_attn_conv_z_b"])
    mv[:, 80:88] = _pp(iw["L0_attn_inv_dt"])
    mv[:, 96:104] = _pp(iw["L0_attn_D"])
    mv[:, 104:112] = _pp(iw["L0_attn_b_out"])
    w["mvecs"] = mv
    w["Wqkv"] = np.asarray(iw["L0_ffn_Wqkv"])
    w["Wo"] = np.asarray(iw["L0_ffn_Wo"])
    W1 = np.asarray(iw["L0_ffn_W1"])
    b1 = np.asarray(iw["L0_ffn_b1"])
    W1n = np.zeros((6, D, HID), np.float32)
    b1n = np.zeros((6, HID), np.float32)
    for ci in range(HC):
        lo, hi = 128 * ci, min(128 * ci + 128, 2730)
        if lo < 2730:
            n = hi - lo
            W1n[:, :, 256 * ci:256 * ci + n] = W1[:, :, lo:hi]
            W1n[:, :, 256 * ci + 128:256 * ci + 128 + n] = W1[:, :, 2730 + lo:2730 + hi]
            b1n[:, 256 * ci:256 * ci + n] = b1[:, lo:hi]
            b1n[:, 256 * ci + 128:256 * ci + 128 + n] = b1[:, 2730 + lo:2730 + hi]
    w["W1"] = W1n
    b1r = b1n.reshape(6, HC, 2, 128)
    w["b1a"] = np.ascontiguousarray(b1r[:, :, 0, :].transpose(0, 2, 1))
    w["b1g"] = np.ascontiguousarray(b1r[:, :, 1, :].transpose(0, 2, 1))
    W2 = np.asarray(iw["L0_ffn_W2"])
    W2n = np.zeros((6, HIDH, D), np.float32)
    W2n[:, 0:2730, :] = W2
    w["W2"] = W2n
    qs = np.tile(np.asarray(iw["L0_ffn_q_scale"])[:, None, :], (1, HEADS, 1)).reshape(6, D)
    ks = np.tile(np.asarray(iw["L0_ffn_k_scale"])[:, None, :], (1, HEADS, 1)).reshape(6, D)
    lv = np.zeros((6, 128, 48), np.float32)
    for l in range(6):
        lv[l, :, 0:8] = _pp(qs[l])
        lv[l, :, 8:16] = _pp(ks[l])
        lv[l, :, 16:24] = _pp(np.asarray(iw["L0_ffn_ln_g"])[l])
        lv[l, :, 24:32] = _pp(np.asarray(iw["L0_ffn_ln_b"])[l])
        lv[l, :, 32:40] = _pp(np.asarray(iw["L0_ffn_b2"])[l])
    w["lvecs"] = lv
    lr = np.zeros((6, 2, D), np.float32)
    lr[:, 0, :] = np.asarray(iw["L0_ffn_ln_g"])
    lr[:, 1, :] = np.asarray(iw["L0_ffn_ln_b"])
    w["lrows2"] = lr
    return w


MF_XF = 0
MF_PRE = 4160
MF_ZPRE = 8320
MF_XS = 12480
MF_RES = 16576
MF_BREP = 20672
MF_CREP = 21184
MF_P = 21696
MF_W = 22720
MF_BC = 23232
MF_MISC = 23744
AF_COLS = 26304
BF_XB = 0
BF_MISC = 16512

MB_WROT = 0
MB_WDT = 2048
MB_WXD = 3072
MB_XN = 4096
MB_ZBF = 8256
MB_XSBF = 12416
MB_DTL = 13440
MB_YBF = 13952
BB_WROT = 0
BB_WOROT = 2048
BB_XBF = 4096
BB_Q = 8256
BB_K = 12352
BB_KCUR = BB_K + 2048
BB_V = 18496
BB_VCUR = BB_V + 2048
BB_PT = 24640
BB_AFM = 26688
BB_MISC = 28736
FB_W1 = 0
FB_W2 = 4096
FB_M = 6144
FB_GELU = 17408
FB_XBF2 = 18432
FB_XN = 22592
AB_COLS = 37952
AB_TAB = AB_COLS - 4 * T
BF_QN = 16512



def build_program():
    nc = bass.Bass("TRN2", target_bir_lowering=False)
    p = Prog(nc)

    def dtin(name, shape, dtype=F32):
        return nc.dram_tensor(name, list(shape), dtype, kind="ExternalInput").ap()

    x_fm = dtin("x_fm", [B_PER_CORE, D, T])
    W_in_d = dtin("W_in", [D, D_INNER])
    W_out_d = dtin("W_out", [D_INNER, D])
    W_dt_d = dtin("W_dt", [64, D])
    W_xdbl_d = dtin("W_xdbl", [D, 128])
    mvecs_d = dtin("mvecs", [128, 112])
    Wqkv_d = dtin("Wqkv", [6, D, 3 * D])
    Wo_d = dtin("Wo", [6, D, D])
    W1_d = dtin("W1", [6, D, HID])
    W2_d = dtin("W2", [6, HIDH, D])
    b1a_d = dtin("b1a", [6, 128, HC])
    b1g_d = dtin("b1g", [6, 128, HC])
    lvecs_d = dtin("lvecs", [6, 128, 48])
    tabs_d = {nm: dtin(nm, [128, T], BF16) for nm in ("cs_q", "ss_q", "cs_k", "ss_k")}
    maskT_d = dtin("maskT", [128, 4, WIN], BF16)
    h2_d = dtin("h2", [128, 2], BF16)
    rperm_d = dtin("rperm", [128, 128], BF16)
    ones_bf_d = dtin("ones_bf", [128, 1], BF16)
    ones_f_d = dtin("ones_f", [128, 1])
    onesrow_f_d = dtin("onesrow_f", [1, 128])
    onesrow_bf_d = dtin("onesrow_bf", [1, 128], BF16)
    ident_bf_d = dtin("ident_bf", [128, 128], BF16)
    h2t_d = dtin("h2t", [2, 128], BF16)
    lrows2_d = dtin("lrows2", [6, 2, D])

    out_fm = nc.dram_tensor("out_fm", [B_PER_CORE, D, T], F32,
                            kind="ExternalOutput").ap()
    s_scr = nc.dram_tensor("s_scr", [D, B_PER_CORE * T], F32, kind="Internal").ap()
    bc_scr = nc.dram_tensor("bc_scr", [2, 48, MCH], F32, kind="Internal").ap()
    wbf = {}
    for name, shape in (("W_in", [D, D_INNER]), ("W_out", [D_INNER, D]),
                        ("W_dt", [64, D]), ("W_xdbl", [D, 128]),
                        ("Wqkv", [6, D, 3 * D]), ("Wo", [6, D, D]),
                        ("W1", [6, D, HID]), ("W2", [6, HIDH, D])):
        wbf[name] = nc.dram_tensor("bf_" + name, list(shape), BF16,
                                   kind="Internal").ap()

    dbg_d = nc.dram_tensor("dbg", [128, 65536], F32, kind="ExternalOutput").ap()
    dbg_b_d = nc.dram_tensor("dbgb", [128, 16384], BF16, kind="ExternalOutput").ap()
    dbg_state = {"off": 0, "map": {}, "offb": 0, "mapb": {}}

    af = nc.alloc_sbuf_tensor("af", [128, AF_COLS], F32)
    ab = nc.alloc_sbuf_tensor("ab", [128, AB_COLS], BF16)
    maskT = nc.alloc_sbuf_tensor("sb_maskT", [128, 4 * WIN], BF16)
    h2 = nc.alloc_sbuf_tensor("sb_h2", [128, 2], BF16)
    rperm = nc.alloc_sbuf_tensor("sb_rperm", [128, 128], BF16)
    ones_bf = nc.alloc_sbuf_tensor("sb_ones_bf", [128, 1], BF16)
    ones_f = nc.alloc_sbuf_tensor("sb_ones_f", [128, 1], F32)
    onesrow_f = nc.alloc_sbuf_tensor("sb_onesrow_f", [1, 128], F32)
    onesrow_bf = nc.alloc_sbuf_tensor("sb_onesrow_bf", [1, 128], BF16)
    ident_bf = nc.alloc_sbuf_tensor("sb_ident_bf", [128, 128], BF16)
    mvecs = nc.alloc_sbuf_tensor("sb_mvecs", [128, 112], F32)
    lvecs = nc.alloc_sbuf_tensor("sb_lvecs", [128, 48], F32)
    b1ab = nc.alloc_sbuf_tensor("sb_b1ab", [128, 2 * HC], F32)
    h2t = nc.alloc_sbuf_tensor("sb_h2t", [2, 128], BF16)
    lrows2 = nc.alloc_sbuf_tensor("sb_lrows2", [2, D], F32)
    lrows2_bf = nc.alloc_sbuf_tensor("sb_lrows2_bf", [2, D], BF16)
    rq_bf = nc.alloc_sbuf_tensor("sb_rq_bf", [128, 128], BF16)
    rk_bf = nc.alloc_sbuf_tensor("sb_rk_bf", [128, 128], BF16)
    murs_row = nc.alloc_sbuf_tensor("sb_murs_row", [2, 512], BF16)
    rs_row = nc.alloc_sbuf_tensor("sb_rs_row", [1, 512], BF16)
    small = nc.alloc_sbuf_tensor("sb_small", [128, 8], F32)
    wstage = [nc.alloc_sbuf_tensor(f"wstage{i}", [128, 1024], F32) for i in range(2)]
    castb = [nc.alloc_sbuf_tensor(f"castb{i}", [128, 1024], BF16) for i in range(2)]
    PS = [nc.alloc_psum_tensor(f"ps{i}", [128, 512], F32) for i in range(8)]

    def dbg(tag, ap_fn, cols, reads, rows=128):
        if not DEBUG_DUMPS or tag not in DEBUG_DUMPS:
            return
        off = dbg_state["off"]
        assert off + cols <= 65536, "dbg overflow"
        dbg_state["map"][tag] = (off, cols)
        dbg_state["off"] = off + cols
        p.dma("gpsimd", lambda e, off=off, cols=cols, rows=rows: e.dma_start(
            out=dbg_d[0:rows, off:off + cols], in_=ap_fn()),
            reads=reads, writes=(f"dbg_{tag}",))

    def dbgb(tag, ap_fn, cols, reads, rows=128):
        if not DEBUG_DUMPS or tag not in DEBUG_DUMPS:
            return
        off = dbg_state["offb"]
        assert off + cols <= 16384, "dbgb overflow"
        dbg_state["mapb"][tag] = (off, cols)
        dbg_state["offb"] = off + cols
        p.dma("gpsimd", lambda e, off=off, cols=cols, rows=rows: e.dma_start(
            out=dbg_b_d[0:rows, off:off + cols], in_=ap_fn()),
            reads=reads, writes=(f"dbgb_{tag}",))

    def A(off, length):
        return af[:, off:off + length]

    def Ap(p0, np_, off, length):
        return af[p0:p0 + np_, off:off + length]

    def B_(off, length):
        return ab[:, off:off + length]

    def Bp(p0, np_, off, length):
        return ab[p0:p0 + np_, off:off + length]

    def RA(off, length):
        return tuple(f"af{i}" for i in range(off // 512, (off + length + 511) // 512))

    def RB(off, length):
        return tuple(f"ab{i}" for i in range(off // 512, (off + length + 511) // 512))

    cast_ctr = [0]

    def cast_weight(src_ap, dst_ap, rows, cols, rkey):
        nstr = (rows + 127) // 128
        for si in range(nstr):
            r0 = si * 128
            nr = min(128, rows - r0)
            coff = 0
            while coff < cols:
                cw = min(1024, cols - coff)
                sl = cast_ctr[0] % 2
                cast_ctr[0] += 1
                p.dma("sync",
                      lambda e, s=src_ap, r0=r0, nr=nr, co=coff, cw=cw, sl=sl:
                      e.dma_start(out=wstage[sl][0:nr, 0:cw],
                                  in_=s[r0:r0 + nr, co:co + cw]),
                      writes=(f"wstage{sl}",))
                p.add("scalar",
                      lambda e, nr=nr, cw=cw, sl=sl:
                      e.activation(castb[sl][0:nr, 0:cw], wstage[sl][0:nr, 0:cw],
                                   AF.Copy),
                      reads=(f"wstage{sl}",), writes=(f"castb{sl}",))
                p.dma("gpsimd",
                      lambda e, d=dst_ap, r0=r0, nr=nr, co=coff, cw=cw, sl=sl:
                      e.dma_start(out=d[r0:r0 + nr, co:co + cw],
                                  in_=castb[sl][0:nr, 0:cw]),
                      reads=(f"castb{sl}",), writes=(rkey,))
                coff += cw

    def phase_w():
        cast_weight(W_in_d, wbf["W_in"], D, D_INNER, "wbf_W_in")
        cast_weight(W_out_d, wbf["W_out"], D_INNER, D, "wbf_W_out")
        cast_weight(W_dt_d, wbf["W_dt"], 64, D, "wbf_W_dt")
        cast_weight(W_xdbl_d, wbf["W_xdbl"], D, 128, "wbf_W_xdbl")
        if MILESTONE != "mamba":
            for l in range(6):
                cast_weight(Wqkv_d[l], wbf["Wqkv"][l], D, 3 * D, f"wbf_Wqkv{l}")
                cast_weight(Wo_d[l], wbf["Wo"][l], D, D, f"wbf_Wo{l}")
                cast_weight(W1_d[l], wbf["W1"][l], D, HID, f"wbf_W1{l}")
                cast_weight(W2_d[l], wbf["W2"][l], HIDH, D, f"wbf_W2{l}")

    def phase_consts():
        p.dma("sync", lambda e: e.dma_start(
            out=maskT[:, :], in_=maskT_d.rearrange("p k w -> p (k w)")),
            writes=("maskT",))
        for t_, nm in ((h2, "h2"), (rperm, "rperm"), (ones_bf, "ones_bf"),
                       (h2t, "h2t"),
                       (ones_f, "ones_f"), (ident_bf, "ident_bf"),
                       (mvecs, "mvecs"), (onesrow_f, "onesrow_f"),
                       (onesrow_bf, "onesrow_bf")):
            src = {"h2": h2_d, "rperm": rperm_d, "ones_bf": ones_bf_d,
                   "ones_f": ones_f_d, "ident_bf": ident_bf_d,
                   "mvecs": mvecs_d, "onesrow_f": onesrow_f_d,
                   "onesrow_bf": onesrow_bf_d, "h2t": h2t_d}[nm]
            p.dma("sync", lambda e, t_=t_, src=src: e.dma_start(
                out=t_[:, :], in_=src), writes=(nm,))
        for i, nm in enumerate(("cs_q", "ss_q", "cs_k", "ss_k")):
            p.dma("sync", lambda e, nm=nm, i=i: e.dma_start(
                out=ab[:, AB_TAB + i * T:AB_TAB + (i + 1) * T], in_=tabs_d[nm]),
                writes=(f"tab_{nm}",))
        p.add("vector", lambda e: e.tensor_scalar(
            mvecs[:, 88:96], mvecs[:, 80:88], 2.0, None, ALU.mult),
            reads=("mvecs",), writes=("mvecs",))
        p.add("vector", lambda e: e.memset(small[:, 0:1], 1e-5), writes=("eps",))
        p.add("vector", lambda e: e.memset(small[:, 2:3], 1.0), writes=("one1",))
        p.add("vector", lambda e: e.memset(murs_row[0:2, :], -1.0),
              writes=("mursrow1", "mursrow0"))

    SC = MF_MISC + 128
    STASH = MF_MISC + 2304

    def m_ln_chunk(b, c):
        t0 = c * MCH
        hal = 4 if c < NMC - 1 else 0
        nt = MCH + hal
        for dc in range(DC):
            p.dma("sync", lambda e, b=b, dc=dc, t0=t0, nt=nt:
                  e.dma_start(out=A(MF_XF + dc * 520, nt),
                              in_=x_fm[b, dc * 128:(dc + 1) * 128, t0:t0 + nt]),
                  writes=RA(MF_XF + dc * 520, nt))
        for dc in range(DC):
            p.add("vector", lambda e, dc=dc, nt=nt:
                  e.tensor_copy(Bp(0, 128, MB_XN + dc * 520, nt),
                                A(MF_XF + dc * 520, nt)),
                  reads=RA(MF_XF + dc * 520, nt),
                  writes=RB(MB_XN + dc * 520, nt))
        for dc in range(DC):
            p.add("scalar", lambda e, dc=dc, nt=nt:
                  e.activation(Bp(0, 128, MB_ZBF + dc * 520, nt),
                               A(MF_XF + dc * 520, nt), AF.Square),
                  reads=RA(MF_XF + dc * 520, nt),
                  writes=RB(MB_ZBF + dc * 520, nt))
        for dc in range(DC):
            p.add("tensor", lambda e, dc=dc: e.matmul(
                PS[5][0:1, 0:512], ones_bf.ap(), Bp(0, 128, MB_XN + dc * 520, 512),
                start=(dc == 0), stop=(dc == DC - 1)),
                reads=RB(MB_XN + dc * 520, 512) + ("ones_bf",), writes=("psb5",))
        for dc in range(DC):
            p.add("tensor", lambda e, dc=dc: e.matmul(
                PS[6][0:1, 0:512], ones_bf.ap(), Bp(0, 128, MB_ZBF + dc * 520, 512),
                start=(dc == 0), stop=(dc == DC - 1)),
                reads=RB(MB_ZBF + dc * 520, 512) + ("ones_bf",), writes=("psb6",))
        if hal:
            for dc in range(DC):
                p.add("tensor", lambda e, dc=dc, hal=hal: e.matmul(
                    PS[7][0:1, 0:hal], ones_bf.ap(),
                    Bp(0, 128, MB_XN + dc * 520 + 512, hal),
                    start=(dc == 0), stop=(dc == DC - 1)),
                    reads=RB(MB_XN + dc * 520 + 512, hal) + ("ones_bf",),
                    writes=("psb7",))
            for dc in range(DC):
                p.add("tensor", lambda e, dc=dc, hal=hal: e.matmul(
                    PS[7][0:1, 8:8 + hal], ones_bf.ap(),
                    Bp(0, 128, MB_ZBF + dc * 520 + 512, hal),
                    start=(dc == 0), stop=(dc == DC - 1)),
                    reads=RB(MB_ZBF + dc * 520 + 512, hal) + ("ones_bf",),
                    writes=("psb7",))
        p.add("vector", lambda e: e.tensor_scalar(
            Ap(0, 1, SC, 512), PS[5][0:1, 0:512], 1.0 / D, None, ALU.mult),
            reads=("psb5",), writes=RA(SC, 512))
        p.add("scalar", lambda e: e.activation(
            Ap(0, 1, SC + 512, 512), Ap(0, 1, SC, 512), AF.Square),
            reads=RA(SC, 512), writes=RA(SC + 512, 512))
        p.add("vector", lambda e: e.scalar_tensor_tensor(
            Ap(0, 1, SC + 1024, 512), PS[6][0:1, 0:512], 1.0 / D,
            Ap(0, 1, SC + 512, 512), ALU.mult, ALU.subtract),
            reads=("psb6",) + RA(SC + 512, 512), writes=RA(SC + 1024, 512))
        p.add("scalar", lambda e: e.activation(
            Ap(0, 1, SC + 1536, 512), Ap(0, 1, SC + 1024, 512), AF.Ln,
            bias=small[0:1, 0:1]),
            reads=RA(SC + 1024, 512) + ("eps",), writes=RA(SC + 1536, 512))
        p.add("scalar", lambda e: e.activation(
            Ap(0, 1, SC + 1024, 512), Ap(0, 1, SC + 1536, 512), AF.Exp,
            scale=-0.5),
            reads=RA(SC + 1536, 512), writes=RA(SC + 1024, 512))
        p.add("vector", lambda e: e.tensor_mul(
            Ap(0, 1, SC, 512), Ap(0, 1, SC, 512), Ap(0, 1, SC + 1024, 512)),
            reads=RA(SC, 512) + RA(SC + 1024, 512), writes=RA(SC, 512))
        if hal:
            p.add("vector", lambda e, hal=hal: e.tensor_scalar(
                Ap(0, 1, SC + 2048, hal), PS[7][0:1, 0:hal], 1.0 / D, None,
                ALU.mult), reads=("psb7",), writes=RA(SC + 2048, hal))
            p.add("scalar", lambda e, hal=hal: e.activation(
                Ap(0, 1, SC + 2056, hal), Ap(0, 1, SC + 2048, hal), AF.Square),
                reads=RA(SC + 2048, hal), writes=RA(SC + 2056, hal))
            p.add("vector", lambda e, hal=hal: e.scalar_tensor_tensor(
                Ap(0, 1, SC + 2064, hal), PS[7][0:1, 8:8 + hal], 1.0 / D,
                Ap(0, 1, SC + 2056, hal), ALU.mult, ALU.subtract),
                reads=("psb7",) + RA(SC + 2056, hal), writes=RA(SC + 2064, hal))
            p.add("scalar", lambda e, hal=hal: e.activation(
                Ap(0, 1, SC + 2072, hal), Ap(0, 1, SC + 2064, hal), AF.Ln,
                bias=small[0:1, 0:1]),
                reads=RA(SC + 2064, hal) + ("eps",), writes=RA(SC + 2072, hal))
            p.add("scalar", lambda e, hal=hal: e.activation(
                Ap(0, 1, SC + 2064, hal), Ap(0, 1, SC + 2072, hal), AF.Exp,
                scale=-0.5),
                reads=RA(SC + 2072, hal), writes=RA(SC + 2064, hal))
            p.add("vector", lambda e, hal=hal: e.tensor_mul(
                Ap(0, 1, SC + 2048, hal), Ap(0, 1, SC + 2048, hal),
                Ap(0, 1, SC + 2064, hal)),
                reads=RA(SC + 2048, hal) + RA(SC + 2064, hal),
                writes=RA(SC + 2048, hal))
        p.add("tensor", lambda e: e.matmul(
            PS[5][:, 0:512], onesrow_f[0:1, :], Ap(0, 1, SC + 1024, 512),
            start=True, stop=True),
            reads=RA(SC + 1024, 512) + ("onesrow_f",), writes=("psb5",))
        p.add("tensor", lambda e: e.matmul(
            PS[6][:, 0:512], onesrow_f[0:1, :], Ap(0, 1, SC, 512),
            start=True, stop=True),
            reads=RA(SC, 512) + ("onesrow_f",), writes=("psb6",))
        if hal:
            p.add("tensor", lambda e, hal=hal: e.matmul(
                PS[7][:, 16:16 + hal], onesrow_f[0:1, :], Ap(0, 1, SC + 2064, hal),
                start=True, stop=True),
                reads=RA(SC + 2064, hal) + ("onesrow_f",), writes=("psb7",))
            p.add("tensor", lambda e, hal=hal: e.matmul(
                PS[7][:, 24:24 + hal], onesrow_f[0:1, :], Ap(0, 1, SC + 2048, hal),
                start=True, stop=True),
                reads=RA(SC + 2048, hal) + ("onesrow_f",), writes=("psb7",))
        for dc in range(DC):
            p.add("vector", lambda e, dc=dc: e.tensor_mul(
                A(MF_XF + dc * 520, 512), A(MF_XF + dc * 520, 512),
                PS[5][:, 0:512]),
                reads=("psb5",) + RA(MF_XF + dc * 520, 512),
                writes=RA(MF_XF + dc * 520, 512))
            p.add("vector", lambda e, dc=dc: e.tensor_tensor(
                Bp(0, 128, MB_XN + dc * 520, 512), A(MF_XF + dc * 520, 512),
                PS[6][:, 0:512], ALU.subtract),
                reads=("psb6",) + RA(MF_XF + dc * 520, 512),
                writes=RB(MB_XN + dc * 520, 512))
            if hal:
                p.add("vector", lambda e, dc=dc, hal=hal: e.tensor_mul(
                    A(MF_XF + dc * 520 + 512, hal),
                    A(MF_XF + dc * 520 + 512, hal), PS[7][:, 16:16 + hal]),
                    reads=("psb7",) + RA(MF_XF + dc * 520 + 512, hal),
                    writes=RA(MF_XF + dc * 520 + 512, hal))
                p.add("vector", lambda e, dc=dc, hal=hal: e.tensor_tensor(
                    Bp(0, 128, MB_XN + dc * 520 + 512, hal),
                    A(MF_XF + dc * 520 + 512, hal),
                    PS[7][:, 24:24 + hal], ALU.subtract),
                    reads=("psb7",) + RA(MF_XF + dc * 520 + 512, hal),
                    writes=RB(MB_XN + dc * 520 + 512, hal))

    def m_xz_chunk(b, c):
        hal = 4 if c < NMC - 1 else 0
        if c == 0:
            for dc in range(DC):
                p.add("vector", lambda e, dc=dc: e.memset(
                    A(MF_PRE + dc * 520, 2), 0.0), writes=RA(MF_PRE + dc * 520, 2))
                p.add("vector", lambda e, dc=dc: e.memset(
                    A(MF_ZPRE + dc * 520, 2), 0.0), writes=RA(MF_ZPRE + dc * 520, 2))
        else:
            for dc in range(DC):
                p.add("vector", lambda e, dc=dc: e.tensor_copy(
                    A(MF_PRE + dc * 520, 2), A(STASH + dc * 2, 2)),
                    reads=RA(STASH + dc * 2, 2), writes=RA(MF_PRE + dc * 520, 2))
                p.add("vector", lambda e, dc=dc: e.tensor_copy(
                    A(MF_ZPRE + dc * 520, 2), A(STASH + 16 + dc * 2, 2)),
                    reads=RA(STASH + 16 + dc * 2, 2),
                    writes=RA(MF_ZPRE + dc * 520, 2))
        for o in range(16):
            slot = o % 2
            pb = o % 2
            p.dma("sync", lambda e, o=o, slot=slot: e.dma_start(
                out=B_(MB_WROT + slot * 1024, 1024).rearrange(
                    "p (c m) -> p c m", m=128),
                in_=wbf["W_in"].rearrange("(c p) m -> p c m", p=128)[
                    :, :, o * 128:(o + 1) * 128]),
                reads=("wbf_W_in",), writes=RB(MB_WROT + slot * 1024, 1024))
            for dc in range(DC):
                p.add("tensor", lambda e, dc=dc, slot=slot, pb=pb: e.matmul(
                    PS[pb][:, 0:512],
                    B_(MB_WROT + slot * 1024 + dc * 128, 128),
                    Bp(0, 128, MB_XN + dc * 520, 512),
                    start=(dc == 0), stop=(dc == DC - 1)),
                    reads=RB(MB_WROT + slot * 1024, 1024) +
                          RB(MB_XN + dc * 520, 512),
                    writes=(f"psb{pb}",))
            if hal:
                for dc in range(DC):
                    p.add("tensor", lambda e, o=o, dc=dc, slot=slot, hal=hal:
                          e.matmul(PS[2][:, o * 4:o * 4 + hal],
                                   B_(MB_WROT + slot * 1024 + dc * 128, 128),
                                   Bp(0, 128, MB_XN + dc * 520 + 512, hal),
                                   start=(dc == 0), stop=(dc == DC - 1)),
                          reads=RB(MB_WROT + slot * 1024, 1024) +
                                RB(MB_XN + dc * 520 + 512, hal),
                          writes=("psb2",))
            base = MF_PRE if o < 8 else MF_ZPRE
            dcc = o % 8
            p.add("vector", lambda e, base=base, dcc=dcc, pb=pb: e.tensor_copy(
                A(base + dcc * 520 + 2, 512), PS[pb][:, 0:512]),
                reads=(f"psb{pb}",), writes=RA(base + dcc * 520 + 2, 512))
            if hal:
                p.add("vector", lambda e, base=base, dcc=dcc, o=o, hal=hal:
                      e.tensor_copy(A(base + dcc * 520 + 514, hal),
                                    PS[2][:, o * 4:o * 4 + hal]),
                      reads=("psb2",), writes=RA(base + dcc * 520 + 514, hal))
            else:
                p.add("vector", lambda e, base=base, dcc=dcc: e.memset(
                    A(base + dcc * 520 + 514, 4), 0.0),
                    writes=RA(base + dcc * 520 + 514, 4))
        for dc in range(DC):
            p.add("vector", lambda e, dc=dc: e.tensor_copy(
                A(STASH + dc * 2, 2), A(MF_PRE + dc * 520 + 512, 2)),
                reads=RA(MF_PRE + dc * 520 + 512, 2),
                writes=RA(STASH + dc * 2, 2))
            p.add("vector", lambda e, dc=dc: e.tensor_copy(
                A(STASH + 16 + dc * 2, 2), A(MF_ZPRE + dc * 520 + 512, 2)),
                reads=RA(MF_ZPRE + dc * 520 + 512, 2),
                writes=RA(STASH + 16 + dc * 2, 2))

    def m_conv_chunk(b, c):
        for dc in range(DC):
            src_ = MF_PRE + dc * 520
            dst = MF_XS + dc * 512
            p.add("vector", lambda e, src_=src_, dst=dst, dc=dc: e.tensor_scalar(
                A(dst, 512), A(src_ + 1, 512), mvecs[:, dc * 4:dc * 4 + 1],
                None, ALU.mult),
                reads=RA(src_, 520) + ("mvecs",), writes=RA(dst, 512))
            for k in range(1, 4):
                p.add("vector", lambda e, src_=src_, dst=dst, dc=dc, k=k:
                      e.scalar_tensor_tensor(
                          A(dst, 512), A(src_ + 1 + k, 512),
                          mvecs[:, dc * 4 + k:dc * 4 + k + 1],
                          A(dst, 512), ALU.mult, ALU.add),
                      reads=RA(src_, 520) + RA(dst, 512) + ("mvecs",),
                      writes=RA(dst, 512))
            p.add("scalar", lambda e, dst=dst, dc=dc: e.activation(
                A(dst, 512), A(dst, 512), AF.Silu, bias=mvecs[:, 64 + dc:65 + dc]),
                reads=RA(dst, 512) + ("mvecs",), writes=RA(dst, 512))
            p.add("vector", lambda e, dst=dst, dc=dc: e.tensor_copy(
                Bp(0, 128, MB_XSBF + (dc % 2) * 512, 512), A(dst, 512)),
                reads=RA(dst, 512), writes=RB(MB_XSBF + (dc % 2) * 512, 512))
            p.add("tensor", lambda e, dc=dc: e.matmul(
                PS[3][:, 0:512], Bp(0, 128, MB_WXD + dc * 128, 128),
                Bp(0, 128, MB_XSBF + (dc % 2) * 512, 512),
                start=(dc == 0), stop=(dc == DC - 1)),
                reads=RB(MB_WXD, 1024) + RB(MB_XSBF + (dc % 2) * 512, 512),
                writes=("psb3",))
        for dc in range(DC):
            src_ = MF_ZPRE + dc * 520
            dst = MF_RES + dc * 512
            p.add("vector", lambda e, src_=src_, dst=dst, dc=dc: e.tensor_scalar(
                A(dst, 512), A(src_ + 1, 512),
                mvecs[:, 32 + dc * 4:32 + dc * 4 + 1], None, ALU.mult),
                reads=RA(src_, 520) + ("mvecs",), writes=RA(dst, 512))
            for k in range(1, 4):
                p.add("vector", lambda e, src_=src_, dst=dst, dc=dc, k=k:
                      e.scalar_tensor_tensor(
                          A(dst, 512), A(src_ + 1 + k, 512),
                          mvecs[:, 32 + dc * 4 + k:32 + dc * 4 + k + 1],
                          A(dst, 512), ALU.mult, ALU.add),
                      reads=RA(src_, 520) + RA(dst, 512) + ("mvecs",),
                      writes=RA(dst, 512))
            p.add("scalar", lambda e, dst=dst, dc=dc: e.activation(
                Bp(0, 128, MB_ZBF + dc * 512, 512), A(dst, 512), AF.Silu,
                bias=mvecs[:, 72 + dc:73 + dc]),
                reads=RA(dst, 512) + ("mvecs",), writes=RB(MB_ZBF + dc * 512, 512))

    def m_xdbl_dt_chunk(b, c):
        p.add("vector", lambda e: e.tensor_copy(
            Ap(0, 48, MF_BC, 512), PS[3][0:48, 0:512]),
            reads=("psb3",), writes=RA(MF_BC, 512))
        p.add("scalar", lambda e: e.activation(
            Bp(0, 64, MB_DTL, 512), PS[3][64:128, 0:512], AF.Copy),
            reads=("psb3",), writes=RB(MB_DTL, 512))
        p.dma("gpsimd", lambda e, c=c: e.dma_start(
            out=bc_scr[c % 2], in_=Ap(0, 48, MF_BC, 512)),
            reads=RA(MF_BC, 512), writes=(f"bcscr{c % 2}",))
        for o in range(DC):
            p.add("tensor", lambda e, o=o: e.matmul(
                PS[4][:, 0:512], Bp(0, 64, MB_WDT + o * 128, 128),
                Bp(0, 64, MB_DTL, 512), start=True, stop=True),
                reads=RB(MB_WDT, 1024) + RB(MB_DTL, 512), writes=("psb4",))
            p.add("vector", lambda e, o=o: e.tensor_scalar(
                A(MF_PRE + o * 512, 512), PS[4][:, 0:512],
                mvecs[:, 88 + o:89 + o], None, ALU.add),
                reads=("psb4", "mvecs"), writes=RA(MF_PRE + o * 512, 512))
            p.add("scalar", lambda e, o=o: e.activation(
                A(MF_W, 512), A(MF_PRE + o * 512, 512), AF.Abs),
                reads=RA(MF_PRE + o * 512, 512), writes=RA(MF_W, 512))
            p.add("scalar", lambda e, o=o: e.activation(
                A(MF_W, 512), A(MF_W, 512), AF.Exp, scale=-1.0),
                reads=RA(MF_W, 512), writes=RA(MF_W, 512))
            p.add("scalar", lambda e, o=o: e.activation(
                A(MF_W, 512), A(MF_W, 512), AF.Ln, bias=small[:, 2:3]),
                reads=RA(MF_W, 512) + ("one1",), writes=RA(MF_W, 512))
            p.add("vector", lambda e, o=o: e.tensor_scalar(
                A(MF_P, 512), A(MF_PRE + o * 512, 512), 0.0, None, ALU.max),
                reads=RA(MF_PRE + o * 512, 512), writes=RA(MF_P, 512))
            p.add("vector", lambda e, o=o: e.tensor_add(
                A(MF_PRE + o * 512, 512), A(MF_P, 512), A(MF_W, 512)),
                reads=RA(MF_P, 512) + RA(MF_W, 512),
                writes=RA(MF_PRE + o * 512, 512))
            p.add("vector", lambda e, o=o: e.tensor_mul(
                A(MF_ZPRE + o * 512, 512), A(MF_PRE + o * 512, 512),
                A(MF_XS + o * 512, 512)),
                reads=RA(MF_PRE + o * 512, 512) + RA(MF_XS + o * 512, 512),
                writes=RA(MF_ZPRE + o * 512, 512))

    def m_scan_chunk(b, c):
        for n in range(16):
            p.dma("scalar", lambda e, c=c, n=n: e.dma_start(
                out=A(MF_BREP, 512),
                in_=bc_scr[c % 2, n:n + 1, :].partition_broadcast(128).squeeze(1)),
                reads=(f"bcscr{c % 2}",), writes=RA(MF_BREP, 512))
            p.dma("scalar", lambda e, c=c, n=n: e.dma_start(
                out=A(MF_CREP, 512),
                in_=bc_scr[c % 2, 32 + n:33 + n, :]
                .partition_broadcast(128).squeeze(1)),
                reads=(f"bcscr{c % 2}",), writes=RA(MF_CREP, 512))
            for dc in range(DC):
                pslot = (n * DC + dc) % 2
                p.add("scalar", lambda e, n=n, dc=dc, pslot=pslot: e.activation(
                    A(MF_P + pslot * 512, 512), A(MF_PRE + dc * 512, 512),
                    AF.Exp, scale=-(n + 1.0)),
                    reads=RA(MF_PRE + dc * 512, 512),
                    writes=RA(MF_P + pslot * 512, 512))
                p.add("vector", lambda e, dc=dc: e.tensor_mul(
                    A(MF_W, 512), A(MF_ZPRE + dc * 512, 512), A(MF_BREP, 512)),
                    reads=RA(MF_ZPRE + dc * 512, 512) + RA(MF_BREP, 512),
                    writes=RA(MF_W, 512))
                p.add("vector", lambda e, n=n, dc=dc, pslot=pslot:
                      e.tensor_tensor_scan(
                          A(MF_RES + dc * 512, 512), A(MF_P + pslot * 512, 512),
                          A(MF_W, 512),
                          af[:, MF_MISC + n * 8 + dc:MF_MISC + n * 8 + dc + 1],
                          ALU.mult, ALU.add),
                      reads=RA(MF_P + pslot * 512, 512) + RA(MF_W, 512) +
                            RA(MF_MISC, 128),
                      writes=RA(MF_RES + dc * 512, 512))
                p.add("vector", lambda e, dc=dc: e.tensor_mul(
                    Bp(0, 128, MB_XSBF + (dc % 2) * 512, 512),
                    A(MF_RES + dc * 512, 512), A(MF_CREP, 512)),
                    reads=RA(MF_RES + dc * 512, 512) + RA(MF_CREP, 512),
                    writes=RB(MB_XSBF + (dc % 2) * 512, 512))
                p.add("tensor", lambda e, n=n, dc=dc: e.matmul(
                    PS[dc][:, 0:512], ident_bf.ap(),
                    Bp(0, 128, MB_XSBF + (dc % 2) * 512, 512),
                    start=(n == 0), stop=False),
                    reads=RB(MB_XSBF + (dc % 2) * 512, 512) + ("ident_bf",),
                    writes=(f"psb{dc}",))
            p.add("vector", lambda e, n=n: e.tensor_copy(
                af[:, MF_MISC + n * 8:MF_MISC + n * 8 + 8],
                af[:, MF_RES:MF_RES + 4096].rearrange(
                    "p (ci t) -> p ci t", t=512)[:, :, 511]),
                reads=RA(MF_RES, 4096),
                writes=RA(MF_MISC, 128))
        for dc in range(DC):
            p.add("vector", lambda e, dc=dc: e.tensor_scalar(
                Bp(0, 128, MB_XSBF + (dc % 2) * 512, 512),
                A(MF_XS + dc * 512, 512),
                mvecs[:, 96 + dc:97 + dc], None, ALU.mult),
                reads=RA(MF_XS + dc * 512, 512) + ("mvecs",),
                writes=RB(MB_XSBF + (dc % 2) * 512, 512))
            p.add("tensor", lambda e, dc=dc: e.matmul(
                PS[dc][:, 0:512], ident_bf.ap(),
                Bp(0, 128, MB_XSBF + (dc % 2) * 512, 512),
                start=False, stop=True),
                reads=RB(MB_XSBF + (dc % 2) * 512, 512) + ("ident_bf",),
                writes=(f"psb{dc}",))
            p.add("scalar", lambda e, dc=dc: e.activation(
                Bp(0, 128, MB_YBF + dc * 512, 512), PS[dc][:, 0:512], AF.Copy),
                reads=(f"psb{dc}",), writes=RB(MB_YBF + dc * 512, 512))

    def m_wout_chunk(b, c):
        t0 = c * MCH
        for k in range(16):
            slot = k % 2
            p.dma("sync", lambda e, k=k, slot=slot: e.dma_start(
                out=B_(MB_WROT + slot * 1024, 1024),
                in_=wbf["W_out"][k * 128:(k + 1) * 128, :]),
                reads=("wbf_W_out",), writes=RB(MB_WROT + slot * 1024, 1024))
            rhs_off = (MB_YBF + k * 512) if k < 8 else (MB_ZBF + (k - 8) * 512)
            for o in range(DC):
                p.add("tensor", lambda e, k=k, o=o, slot=slot, rhs_off=rhs_off:
                      e.matmul(PS[o][:, 0:512],
                               B_(MB_WROT + slot * 1024 + o * 128, 128),
                               Bp(0, 128, rhs_off, 512),
                               start=(k == 0), stop=(k == 15)),
                      reads=RB(MB_WROT + slot * 1024, 1024) + RB(rhs_off, 512),
                      writes=(f"psb{o}",))
        for o in range(DC):
            p.add("vector", lambda e, o=o: e.tensor_scalar(
                A(MF_RES + o * 512, 512), PS[o][:, 0:512],
                mvecs[:, 104 + o:105 + o], None, ALU.add),
                reads=(f"psb{o}", "mvecs"), writes=RA(MF_RES + o * 512, 512))
            p.dma("gpsimd", lambda e, o=o, b=b, t0=t0: e.dma_start(
                out=s_scr[o * 128:(o + 1) * 128, b * T + t0:b * T + t0 + 512],
                in_=A(MF_RES + o * 512, 512)),
                reads=RA(MF_RES + o * 512, 512), writes=(f"sst{b}_{c}_{o}",))

    def phase_mamba():
        p.dma("sync", lambda e: e.dma_start(
            out=Bp(0, 64, MB_WDT, 1024), in_=wbf["W_dt"]),
            reads=("wbf_W_dt",), writes=RB(MB_WDT, 1024))
        for k in range(DC):
            p.dma("sync", lambda e, k=k: e.dma_start(
                out=Bp(0, 128, MB_WXD + k * 128, 128),
                in_=wbf["W_xdbl"][k * 128:(k + 1) * 128, :]),
                reads=("wbf_W_xdbl",), writes=RB(MB_WXD + k * 128, 128))
        for b in range(B_PER_CORE):
            p.add("vector", lambda e: e.memset(
                af[:, MF_MISC:MF_MISC + 128], 0.0),
                writes=RA(MF_MISC, 128))
            for c in range(NMC):
                m_ln_chunk(b, c)
                m_xz_chunk(b, c)
                m_conv_chunk(b, c)
                m_xdbl_dt_chunk(b, c)
                m_scan_chunk(b, c)
                m_wout_chunk(b, c)

    def phase_out_mamba():
        outs = []
        for b in range(B_PER_CORE):
            for o in range(DC):
                op = p.dma("gpsimd", lambda e, b=b, o=o: e.dma_start(
                    out=out_fm[b, o * 128:(o + 1) * 128, :],
                    in_=s_scr[o * 128:(o + 1) * 128, b * T:(b + 1) * T]),
                    reads=tuple(f"sst{b}_{c}_{o}" for c in range(NMC)),
                    writes=(f"oo{b}_{o}",))
                outs.append(op)
        p.add("gpsimd", lambda e: e.engine_nop(), deps=outs)



    BB_LAYERS = int(globals().get("BB_NLAYERS", 6))

    def phase_w_bb():
        for l in range(BB_LAYERS):
            cast_weight(Wqkv_d[l], wbf["Wqkv"][l], D, 3 * D, f"wbf_Wqkv{l}")
            cast_weight(Wo_d[l], wbf["Wo"][l], D, D, f"wbf_Wo{l}")
            cast_weight(W1_d[l], wbf["W1"][l], D, HID, f"wbf_W1{l}")
            cast_weight(W2_d[l], wbf["W2"][l], HIDH, D, f"wbf_W2{l}")

    XBO = BF_XB

    def xb_off(dc, t):
        return XBO + dc * 2064 + 1 + t

    def bb_init(b):
        for dc in range(DC):
            p.add("vector", lambda e, dc=dc: e.memset(
                A(XBO + dc * 2064, 1), 0.0), writes=RA(XBO + dc * 2064, 1))
        for tt in range(4):
            t0 = tt * 512
            for dc in range(DC):
                p.dma("sync", lambda e, b=b, dc=dc, t0=t0: e.dma_start(
                    out=A(xb_off(dc, t0), 512),
                    in_=x_fm[b, dc * 128:(dc + 1) * 128, t0:t0 + 512]),
                    writes=RA(xb_off(dc, t0), 512))
            for dc in range(DC):
                p.add("vector", lambda e, dc=dc, t0=t0: e.tensor_copy(
                    Bp(0, 128, FB_M + dc * 512, 512), A(xb_off(dc, t0), 512)),
                    reads=RA(xb_off(dc, t0), 512),
                    writes=RB(FB_M + dc * 512, 512))
                p.add("scalar", lambda e, dc=dc, t0=t0: e.activation(
                    Bp(0, 128, FB_XN + dc * 512, 512), A(xb_off(dc, t0), 512),
                    AF.Square),
                    reads=RA(xb_off(dc, t0), 512),
                    writes=RB(FB_XN + dc * 512, 512))
            for dc in range(DC):
                p.add("tensor", lambda e, dc=dc: e.matmul(
                    PS[5][0:1, 0:512], ones_bf.ap(), Bp(0, 128, FB_M + dc * 512, 512),
                    start=(dc == 0), stop=(dc == DC - 1)),
                    reads=RB(FB_M + dc * 512, 512) + ("ones_bf",), writes=("psb5",))
                p.add("tensor", lambda e, dc=dc: e.matmul(
                    PS[6][0:1, 0:512], ones_bf.ap(), Bp(0, 128, FB_XN + dc * 512, 512),
                    start=(dc == 0), stop=(dc == DC - 1)),
                    reads=RB(FB_XN + dc * 512, 512) + ("ones_bf",), writes=("psb6",))
            ln_rows_512()
            p.add("tensor", lambda e: e.matmul(
                PS[5][:, 0:512], onesrow_f[0:1, :], Ap(0, 1, SC + 1024, 512),
                start=True, stop=True),
                reads=RA(SC + 1024, 512) + ("onesrow_f",), writes=("psb5",))
            p.add("tensor", lambda e: e.matmul(
                PS[6][:, 0:512], onesrow_f[0:1, :], Ap(0, 1, SC, 512),
                start=True, stop=True),
                reads=RA(SC, 512) + ("onesrow_f",), writes=("psb6",))
            for dc in range(DC):
                p.add("vector", lambda e, dc=dc, t0=t0: e.tensor_mul(
                    A(xb_off(dc, t0), 512), A(xb_off(dc, t0), 512), PS[5][:, 0:512]),
                    reads=("psb5",) + RA(xb_off(dc, t0), 512),
                    writes=RA(xb_off(dc, t0), 512))
                p.add("vector", lambda e, dc=dc, t0=t0: e.tensor_tensor(
                    A(xb_off(dc, t0), 512), A(xb_off(dc, t0), 512), PS[6][:, 0:512],
                    ALU.subtract),
                    reads=("psb6",) + RA(xb_off(dc, t0), 512),
                    writes=RA(xb_off(dc, t0), 512))

    def ln_rows_512():
        p.add("vector", lambda e: e.tensor_scalar(
            Ap(0, 1, SC, 512), PS[5][0:1, 0:512], 1.0 / D, None, ALU.mult),
            reads=("psb5",), writes=RA(SC, 512))
        p.add("scalar", lambda e: e.activation(
            Ap(0, 1, SC + 512, 512), Ap(0, 1, SC, 512), AF.Square),
            reads=RA(SC, 512), writes=RA(SC + 512, 512))
        p.add("vector", lambda e: e.scalar_tensor_tensor(
            Ap(0, 1, SC + 1024, 512), PS[6][0:1, 0:512], 1.0 / D,
            Ap(0, 1, SC + 512, 512), ALU.mult, ALU.subtract),
            reads=("psb6",) + RA(SC + 512, 512), writes=RA(SC + 1024, 512))
        p.add("scalar", lambda e: e.activation(
            Ap(0, 1, SC + 1536, 512), Ap(0, 1, SC + 1024, 512), AF.Ln,
            bias=small[0:1, 0:1]),
            reads=RA(SC + 1024, 512) + ("eps",), writes=RA(SC + 1536, 512))
        p.add("scalar", lambda e: e.activation(
            Ap(0, 1, SC + 1024, 512), Ap(0, 1, SC + 1536, 512), AF.Exp,
            scale=-0.5),
            reads=RA(SC + 1536, 512), writes=RA(SC + 1024, 512))
        p.add("vector", lambda e: e.tensor_mul(
            Ap(0, 1, SC, 512), Ap(0, 1, SC, 512), Ap(0, 1, SC + 1024, 512)),
            reads=RA(SC, 512) + RA(SC + 1024, 512), writes=RA(SC, 512))

    def bb_layer_setup(l):
        p.dma("sync", lambda e, l=l: e.dma_start(
            out=lvecs[:, :], in_=lvecs_d[l]), writes=("lvecs",))
        p.dma("sync", lambda e, l=l: e.dma_start(
            out=b1ab[:, 0:HC], in_=b1a_d[l]), writes=("b1ab_a",))
        p.dma("sync", lambda e, l=l: e.dma_start(
            out=b1ab[:, HC:2 * HC], in_=b1g_d[l]), writes=("b1ab_g",))
        p.dma("sync", lambda e, l=l: e.dma_start(
            out=lrows2[:, :], in_=lrows2_d[l]), writes=("lrows2",))
        p.add("vector", lambda e: e.tensor_copy(lrows2_bf[:, :], lrows2[:, :]),
              reads=("lrows2",), writes=("lrows2_bf",))
        for i, nm in enumerate(("cs_q", "ss_q", "cs_k", "ss_k")):
            p.dma("sync", lambda e, nm=nm, i=i: e.dma_start(
                out=ab[:, AB_TAB + i * T:AB_TAB + (i + 1) * T], in_=tabs_d[nm]),
                writes=RB(AB_TAB + i * T, T))
        p.add("vector", lambda e: e.tensor_scalar(
            ab[:, AB_TAB:AB_TAB + T], ab[:, AB_TAB:AB_TAB + T],
            lvecs[:, 0:1], 8.0, ALU.mult, ALU.mult),
            reads=RB(AB_TAB, T) + ("lvecs",), writes=RB(AB_TAB, T))
        p.add("vector", lambda e: e.tensor_scalar(
            ab[:, AB_TAB + 2 * T:AB_TAB + 3 * T], ab[:, AB_TAB + 2 * T:AB_TAB + 3 * T],
            lvecs[:, 8:9], None, ALU.mult),
            reads=RB(AB_TAB + 2 * T, T) + ("lvecs",), writes=RB(AB_TAB + 2 * T, T))
        p.add("vector", lambda e: e.tensor_scalar(
            rq_bf[:, :], rperm[:, :], lvecs[:, 0:1], 8.0, ALU.mult, ALU.mult),
            reads=("rperm", "lvecs"), writes=("rq_bf",))
        p.add("vector", lambda e: e.tensor_scalar(
            rk_bf[:, :], rperm[:, :], lvecs[:, 8:9], None, ALU.mult),
            reads=("rperm", "lvecs"), writes=("rk_bf",))

    def xbf_slice(base, dc, shifted_halo=True):
        off = base + dc * 520 + (0 if dc >= 4 else 1)
        return off

    def bb_attn_group(b, l, g):
        t0 = g * 512
        if g == 0:
            p.add("vector", lambda e: e.memset(
                af[0:1, 0:1] if False else ab[:, BB_XBF:BB_XBF + 1], 0.0),
                writes=RB(BB_XBF, 1))
            for dc in range(DC):
                p.add("vector", lambda e, dc=dc: e.memset(
                    ab[:, BB_XBF + dc * 520:BB_XBF + dc * 520 + 1], 0.0),
                    writes=RB(BB_XBF + dc * 520, 1))
        else:
            p.add("vector", lambda e: e.tensor_copy(
                ab[:, BB_XBF:BB_XBF + 8 * 520].rearrange(
                    "p (c t) -> p c t", t=520)[:, :, 0:1].squeeze(2)
                if False else
                ab[:, BB_XBF:BB_XBF + 8 * 520].rearrange(
                    "p (c t) -> p c t", t=520)[:, :, 0],
                ab[:, BB_XBF:BB_XBF + 8 * 520].rearrange(
                    "p (c t) -> p c t", t=520)[:, :, 512]),
                reads=RB(BB_XBF, 8 * 520), writes=RB(BB_XBF, 8 * 520))
        for dc in range(DC):
            p.add("vector", lambda e, dc=dc, t0=t0: e.tensor_copy(
                Bp(0, 128, BB_XBF + dc * 520 + 1, 512),
                A(xb_off(dc, t0), 512)),
                reads=RA(xb_off(dc, t0), 512),
                writes=RB(BB_XBF + dc * 520 + 1, 512))
        SCR = BB_AFM
        for o in range(16):
            slot = o % 2
            pb = o % 2
            p.dma("sync", lambda e, o=o, l=l, slot=slot: e.dma_start(
                out=B_(BB_WROT + slot * 1024, 1024).rearrange(
                    "p (c m) -> p c m", m=128),
                in_=wbf["Wqkv"][l].rearrange("(c p) m -> p c m", p=128)[
                    :, :, o * 128:(o + 1) * 128]),
                reads=(f"wbf_Wqkv{l}",), writes=RB(BB_WROT + slot * 1024, 1024))
            for dc in range(DC):
                p.add("tensor", lambda e, dc=dc, slot=slot, pb=pb: e.matmul(
                    PS[pb][:, 0:512],
                    B_(BB_WROT + slot * 1024 + dc * 128, 128),
                    Bp(0, 128, xbf_slice(BB_XBF, dc), 512),
                    start=(dc == 0), stop=(dc == DC - 1)),
                    reads=RB(BB_WROT + slot * 1024, 1024) +
                          RB(BB_XBF + dc * 520, 520),
                    writes=(f"psb{pb}",))
            isq = o < 8
            p.add("scalar", lambda e, pb=pb, slot=slot: e.activation(
                Bp(0, 128, BB_PT + slot * 1024, 512), PS[pb][:, 0:512], AF.Square),
                reads=(f"psb{pb}",), writes=RB(BB_PT + slot * 1024, 512))
            p.add("tensor", lambda e, slot=slot: e.matmul(
                PS[2][0:2, 0:512], h2.ap(), Bp(0, 128, BB_PT + slot * 1024, 512),
                start=True, stop=True),
                reads=RB(BB_PT + slot * 1024, 512) + ("h2",), writes=("psb2",))
            p.add("scalar", lambda e: e.activation(
                Ap(0, 2, SC + 1536, 512), PS[2][0:2, 0:512], AF.Ln),
                reads=("psb2",), writes=RA(SC + 1536, 512))
            p.add("scalar", lambda e: e.activation(
                murs_row[0:2, :] if False else murs_row[0:2, 0:512],
                Ap(0, 2, SC + 1536, 512), AF.Exp, scale=-0.5),
                reads=RA(SC + 1536, 512), writes=("mursrow0",))
            p.add("tensor", lambda e: e.matmul(
                PS[3][:, 0:512], h2t.ap(), murs_row[0:2, 0:512],
                start=True, stop=True),
                reads=("mursrow0", "h2t"), writes=("psb3",))
            p.add("scalar", lambda e: e.activation(
                Bp(0, 128, SCR, 512), PS[3][:, 0:512], AF.Copy),
                reads=("psb3",), writes=RB(SCR, 512))
            p.add("vector", lambda e, pb=pb: e.tensor_mul(
                Bp(0, 128, SCR + 512, 512), PS[pb][:, 0:512], Bp(0, 128, SCR, 512)),
                reads=(f"psb{pb}",) + RB(SCR, 512), writes=RB(SCR + 512, 512))
            rot = rq_bf if isq else rk_bf
            rotkey = "rq_bf" if isq else "rk_bf"
            p.add("tensor", lambda e, rot=rot: e.matmul(
                PS[3][:, 0:512], rot.ap(), Bp(0, 128, SCR + 512, 512),
                start=True, stop=True),
                reads=RB(SCR + 512, 512) + (rotkey,), writes=("psb3",))
            tab = AB_TAB + (0 if isq else 2 * T) + t0
            p.add("vector", lambda e, tab=tab: e.tensor_mul(
                Bp(0, 128, SCR + 1024, 512), Bp(0, 128, SCR + 512, 512),
                B_(tab, 512)),
                reads=RB(SCR + 512, 512) + RB(tab, 512), writes=RB(SCR + 1024, 512))
            stab = AB_TAB + (T if isq else 3 * T) + t0
            p.add("vector", lambda e, stab=stab: e.tensor_mul(
                Bp(0, 128, SCR + 1536, 512), PS[3][:, 0:512], B_(stab, 512)),
                reads=("psb3",) + RB(stab, 512), writes=RB(SCR + 1536, 512))
            dst = (BB_Q + o * 512) if isq else (BB_KCUR + (o - 8) * 512)
            p.add("vector", lambda e, dst=dst: e.tensor_add(
                Bp(0, 128, dst, 512), Bp(0, 128, SCR + 1024, 512),
                Bp(0, 128, SCR + 1536, 512)),
                reads=RB(SCR + 1024, 512) + RB(SCR + 1536, 512),
                writes=RB(dst, 512))
        for vb in range(2):
            for dc in range(DC):
                slot = dc % 2
                p.dma("sync", lambda e, vb=vb, dc=dc, l=l, slot=slot: e.dma_start(
                    out=B_(BB_WROT + slot * 1024, 512),
                    in_=wbf["Wqkv"][l][dc * 128:(dc + 1) * 128,
                                       2048 + vb * 512:2048 + vb * 512 + 512]),
                    reads=(f"wbf_Wqkv{l}",), writes=RB(BB_WROT + slot * 1024, 512))
                for ts_ in range(4):
                    p.add("tensor", lambda e, vb=vb, dc=dc, ts_=ts_, slot=slot:
                          e.matmul(PS[ts_ * 2 + vb][:, 0:512],
                                   Bp(0, 128, xbf_slice(BB_XBF, dc) + ts_ * 128, 128),
                                   B_(BB_WROT + slot * 1024, 512),
                                   start=(dc == 0), stop=(dc == DC - 1)),
                          reads=RB(BB_WROT + slot * 1024, 512) +
                                RB(BB_XBF + dc * 520, 520),
                          writes=(f"psb{ts_ * 2 + vb}",))
        for ts_ in range(4):
            for vb in range(2):
                p.add("scalar", lambda e, ts_=ts_, vb=vb: e.activation(
                    Bp(0, 128, BB_VCUR + ts_ * 1024 + vb * 512, 512),
                    PS[ts_ * 2 + vb][:, 0:512], AF.Copy),
                    reads=(f"psb{ts_ * 2 + vb}",),
                    writes=RB(BB_VCUR + ts_ * 1024 + vb * 512, 512))
        for wloc in range(2):
            w = 2 * g + wloc
            bb_window(b, l, g, wloc, w)
        p.add("vector", lambda e: e.tensor_copy(
            ab[:, BB_K:BB_K + 2048].rearrange("p (c t) -> p c t", t=256),
            ab[:, BB_KCUR:BB_KCUR + 4096].rearrange(
                "p (c t) -> p c t", t=512)[:, :, 256:512]),
            reads=RB(BB_KCUR, 4096), writes=RB(BB_K, 2048))
        p.add("vector", lambda e: e.tensor_copy(
            B_(BB_V, 2048), B_(BB_VCUR + 2048, 2048)),
            reads=RB(BB_VCUR + 2048, 2048), writes=RB(BB_V, 2048))

    def bb_window(b, l, g, wloc, w):
        t0 = w * 256
        kts = (2, 3) if w == 0 else (0, 1, 2, 3)

        def ksrc(kt):
            if kt < 2:
                if wloc == 0:
                    return BB_K + kt * 128, 256, RB(BB_K, 2048)
                return BB_KCUR + kt * 128, 512, RB(BB_KCUR, 4096)
            tloc = 2 * wloc + (kt - 2) - (0 if wloc == 0 else 2)
            tsub = 2 * wloc + (kt - 2)
            return BB_KCUR + tsub * 128, 512, RB(BB_KCUR, 4096)

        def vsrc(kt):
            if kt < 2:
                if wloc == 0:
                    return BB_V + kt * 1024, RB(BB_V, 2048)
                return BB_VCUR + kt * 1024, RB(BB_VCUR, 4096)
            tsub = 2 * wloc + (kt - 2)
            return BB_VCUR + tsub * 1024, RB(BB_VCUR, 4096)

        for h in range(HEADS):
            par = h % 2
            ch = h // 2
            sps = PS[0] if par == 0 else PS[2]
            sp_lo = PS[0] if par == 0 else PS[2]
            sp_hi = PS[1] if par == 0 else PS[3]
            for kt in kts:
                koff, kstride, kregs = ksrc(kt)
                sp = sp_lo if kt < 2 else sp_hi
                scol = (kt % 2) * 256
                p.add("tensor", lambda e, koff=koff, kstride=kstride, par=par,
                      ch=ch, wloc=wloc, sp=sp, scol=scol, kt=kt: e.matmul(
                          sp[:, scol:scol + 256],
                          ab[par * 64:par * 64 + 64,
                             koff + ch * (kstride if True else 0) * 1:
                             koff + ch * kstride + 128]
                          if False else
                          ab[par * 64:par * 64 + 64,
                             koff + ch * kstride:koff + ch * kstride + 128],
                          ab[par * 64:par * 64 + 64,
                             BB_Q + ch * 512 + wloc * 256:
                             BB_Q + ch * 512 + wloc * 256 + 256],
                          start=True, stop=True),
                      reads=kregs + RB(BB_Q + ch * 512, 512),
                      writes=(f"psb{0 if sp is PS[0] else (1 if sp is PS[1] else (2 if sp is PS[2] else 3))}",))
            for sp, bank, lohi in ((sp_lo, 0 if par == 0 else 2, 0),
                                   (sp_hi, 1 if par == 0 else 3, 1)):
                if w == 0 and lohi == 0:
                    continue
                p.add("vector", lambda e, sp=sp, lohi=lohi: e.tensor_tensor(
                    sp[:, 0:512], sp[:, 0:512],
                    maskT[:, lohi * 512:lohi * 512 + 512], ALU.add),
                    reads=(f"psb{bank}", "maskT"), writes=(f"psb{bank}",))
            pts = BB_PT + par * 1024
            if w == 0:
                p.add("vector", lambda e, pts=pts: e.memset(
                    B_(pts, 512), 0.0), writes=RB(pts, 512))
                p.add("scalar", lambda e, sp=sp_hi, pts=pts,
                      bank=1 if par == 0 else 3: e.activation(
                          B_(pts + 512, 512), sp[:, 0:512], AF.Exp),
                      reads=(f"psb{1 if par == 0 else 3}",),
                      writes=RB(pts + 512, 512))
            else:
                p.add("scalar", lambda e, sp=sp_lo, pts=pts,
                      bank=0 if par == 0 else 2: e.activation(
                          B_(pts, 512), sp[:, 0:512], AF.Exp),
                      reads=(f"psb{0 if par == 0 else 2}",), writes=RB(pts, 512))
                p.add("scalar", lambda e, sp=sp_hi, pts=pts,
                      bank=1 if par == 0 else 3: e.activation(
                          B_(pts + 512, 512), sp[:, 0:512], AF.Exp),
                      reads=(f"psb{1 if par == 0 else 3}",),
                      writes=RB(pts + 512, 512))
            for i, kt in enumerate(kts):
                p.add("tensor", lambda e, kt=kt, par=par, pts=pts,
                      first=(i == 0), last=(i == len(kts) - 1): e.matmul(
                          PS[7][0:1, par * 256:par * 256 + 256], ones_bf.ap(),
                          B_(pts + kt * 256, 256),
                          start=first, stop=last),
                      reads=RB(pts, 1024) + ("ones_bf",), writes=("psb7",))
            for i, kt in enumerate(kts):
                voff, vregs = vsrc(kt)
                p.add("tensor", lambda e, voff=voff, h=h, par=par, pts=pts, kt=kt,
                      first=(i == 0), last=(i == len(kts) - 1): e.matmul(
                          PS[4 + par][par * 64:par * 64 + 64, 0:256],
                          B_(voff + h * 64, 64), B_(pts + kt * 256, 256),
                          start=first, stop=last),
                      reads=vregs + RB(pts, 1024), writes=(f"psb{4 + par}",))
            if par == 1:
                p.add("scalar", lambda e: e.activation(
                    Ap(0, 1, SC + 1536, 512), PS[7][0:1, 0:512], AF.Ln),
                    reads=("psb7",), writes=RA(SC + 1536, 512))
                p.add("scalar", lambda e: e.activation(
                    rs_row[0:1, 0:512], Ap(0, 1, SC + 1536, 512), AF.Exp,
                    scale=-1.0),
                    reads=RA(SC + 1536, 512), writes=("rs_row",))
                p.add("tensor", lambda e: e.matmul(
                    PS[6][0:64, 0:256], onesrow_bf[0:1, 0:64],
                    rs_row[0:1, 0:256], start=True, stop=True),
                    reads=("rs_row", "onesrow_bf"), writes=("psb6",))
                p.add("tensor", lambda e: e.matmul(
                    PS[6][64:128, 0:256], onesrow_bf[0:1, 0:64],
                    rs_row[0:1, 256:512], start=True, stop=True),
                    reads=("rs_row", "onesrow_bf"), writes=("psb6",))
                p.add("scalar", lambda e: e.activation(
                    Bp(0, 128, BB_MISC, 256), PS[6][:, 0:256], AF.Copy),
                    reads=("psb6",), writes=RB(BB_MISC, 256))
                for pp in range(2):
                    p.add("vector", lambda e, pp=pp, ch=ch: e.tensor_mul(
                        Bp(pp * 64, 64, BB_AFM + ch * 256, 256),
                        PS[4 + pp][pp * 64:pp * 64 + 64, 0:256],
                        Bp(pp * 64, 64, BB_MISC, 256)),
                        reads=(f"psb{4 + pp}",) + RB(BB_MISC, 256),
                        writes=RB(BB_AFM + ch * 256, 256))
        for k in range(DC):
            slot = k % 2
            p.dma("sync", lambda e, k=k, l=l, slot=slot: e.dma_start(
                out=B_(BB_WOROT + slot * 1024, 1024),
                in_=wbf["Wo"][l][k * 128:(k + 1) * 128, :]),
                reads=(f"wbf_Wo{l}",), writes=RB(BB_WOROT + slot * 1024, 1024))
            for o in range(DC):
                p.add("tensor", lambda e, k=k, o=o, slot=slot: e.matmul(
                    PS[o][:, 256:512],
                    B_(BB_WOROT + slot * 1024 + o * 128, 128),
                    Bp(0, 128, BB_AFM + k * 256, 256),
                    start=(k == 0), stop=(k == DC - 1)),
                    reads=RB(BB_WOROT + slot * 1024, 1024) +
                          RB(BB_AFM + k * 256, 256),
                    writes=(f"psb{o}",))
        for o in range(DC):
            p.add("vector", lambda e, o=o, t0=t0: e.tensor_add(
                A(xb_off(o, t0), 256), A(xb_off(o, t0), 256), PS[o][:, 256:512]),
                reads=(f"psb{o}",) + RA(xb_off(o, t0), 256),
                writes=RA(xb_off(o, t0), 256))

    def bb_ffn_tt(b, l, tt):
        t0 = tt * 512
        if tt == 0:
            for dc in range(DC):
                p.add("vector", lambda e, dc=dc: e.memset(
                    ab[:, FB_XBF2 + dc * 520:FB_XBF2 + dc * 520 + 1], 0.0),
                    writes=RB(FB_XBF2 + dc * 520, 1))
        else:
            p.add("vector", lambda e: e.tensor_copy(
                ab[:, FB_XBF2:FB_XBF2 + 8 * 520].rearrange(
                    "p (c t) -> p c t", t=520)[:, :, 0],
                ab[:, FB_XBF2:FB_XBF2 + 8 * 520].rearrange(
                    "p (c t) -> p c t", t=520)[:, :, 512]),
                reads=RB(FB_XBF2, 8 * 520), writes=RB(FB_XBF2, 8 * 520))
        for dc in range(DC):
            p.add("vector", lambda e, dc=dc, t0=t0: e.tensor_copy(
                Bp(0, 128, FB_XBF2 + dc * 520 + 1, 512), A(xb_off(dc, t0), 512)),
                reads=RA(xb_off(dc, t0), 512),
                writes=RB(FB_XBF2 + dc * 520 + 1, 512))
        for dc in range(DC):
            p.add("scalar", lambda e, dc=dc: e.activation(
                Bp(0, 128, FB_M + dc * 512, 512),
                Bp(0, 128, xbf_slice(FB_XBF2, dc), 512), AF.Square),
                reads=RB(FB_XBF2 + dc * 520, 520), writes=RB(FB_M + dc * 512, 512))
        for dc in range(DC):
            p.add("tensor", lambda e, dc=dc: e.matmul(
                PS[5][0:1, 0:512], ones_bf.ap(),
                Bp(0, 128, xbf_slice(FB_XBF2, dc), 512),
                start=(dc == 0), stop=(dc == DC - 1)),
                reads=RB(FB_XBF2 + dc * 520, 520) + ("ones_bf",), writes=("psb5",))
            p.add("tensor", lambda e, dc=dc: e.matmul(
                PS[6][0:1, 0:512], ones_bf.ap(), Bp(0, 128, FB_M + dc * 512, 512),
                start=(dc == 0), stop=(dc == DC - 1)),
                reads=RB(FB_M + dc * 512, 512) + ("ones_bf",), writes=("psb6",))
        ln_rows_512()
        p.add("scalar", lambda e: e.activation(
            murs_row[0:1, 0:512], Ap(0, 1, SC, 512), AF.Copy),
            reads=RA(SC, 512), writes=("mursrow0",))
        p.add("scalar", lambda e: e.activation(
            rs_row[0:1, 0:512], Ap(0, 1, SC + 1024, 512), AF.Copy),
            reads=RA(SC + 1024, 512), writes=("rs_row",))
        p.add("tensor", lambda e: e.matmul(
            PS[5][:, 0:512], onesrow_bf[0:1, :], rs_row[0:1, 0:512],
            start=True, stop=True),
            reads=("rs_row", "onesrow_bf"), writes=("psb5",))
        for dc in range(DC):
            p.add("tensor", lambda e, dc=dc: e.matmul(
                PS[6][:, 0:512], lrows2_bf[0:2, dc * 128:(dc + 1) * 128],
                murs_row[0:2, 0:512], start=True, stop=True),
                reads=("mursrow0", "mursrow1", "lrows2_bf"), writes=("psb6",))
            p.add("vector", lambda e, dc=dc: e.tensor_mul(
                Ap(0, 128, BF_QN, 512), Bp(0, 128, xbf_slice(FB_XBF2, dc), 512),
                PS[5][:, 0:512]),
                reads=("psb5",) + RB(FB_XBF2 + dc * 520, 520),
                writes=RA(BF_QN, 512))
            p.add("vector", lambda e, dc=dc: e.scalar_tensor_tensor(
                Bp(0, 128, FB_XN + dc * 512, 512), Ap(0, 128, BF_QN, 512),
                lvecs[:, 16 + dc:17 + dc], PS[6][:, 0:512],
                ALU.mult, ALU.subtract),
                reads=RA(BF_QN, 512) + ("psb6", "lvecs"),
                writes=RB(FB_XN + dc * 512, 512))
        for ci in range(HC):
            slot = ci % 2
            p.dma("sync", lambda e, ci=ci, l=l, slot=slot: e.dma_start(
                out=B_(FB_W1 + slot * 2048, 2048).rearrange(
                    "p (c m) -> p c m", m=256),
                in_=wbf["W1"][l].rearrange("(c p) m -> p c m", p=128)[
                    :, :, 256 * ci:256 * ci + 256]),
                reads=(f"wbf_W1{l}",), writes=RB(FB_W1 + slot * 2048, 2048))
            for dc in range(DC):
                p.add("tensor", lambda e, ci=ci, dc=dc, slot=slot: e.matmul(
                    PS[0][:, 0:512],
                    B_(FB_W1 + slot * 2048 + dc * 256, 128),
                    Bp(0, 128, FB_XN + dc * 512, 512),
                    start=(dc == 0), stop=(dc == DC - 1)),
                    reads=RB(FB_W1 + slot * 2048, 2048) + RB(FB_XN + dc * 512, 512),
                    writes=("psb0",))
                p.add("tensor", lambda e, ci=ci, dc=dc, slot=slot: e.matmul(
                    PS[1][:, 0:512],
                    B_(FB_W1 + slot * 2048 + dc * 256 + 128, 128),
                    Bp(0, 128, FB_XN + dc * 512, 512),
                    start=(dc == 0), stop=(dc == DC - 1)),
                    reads=RB(FB_W1 + slot * 2048, 2048) + RB(FB_XN + dc * 512, 512),
                    writes=("psb1",))
            gsl = ci % 2
            p.add("scalar", lambda e, ci=ci, gsl=gsl: e.activation(
                Bp(0, 128, FB_GELU + gsl * 512, 512), PS[1][:, 0:512], AF.Gelu,
                bias=b1ab[:, HC + ci:HC + ci + 1]),
                reads=("psb1", "b1ab_g"), writes=RB(FB_GELU + gsl * 512, 512))
            p.add("vector", lambda e, ci=ci, gsl=gsl: e.scalar_tensor_tensor(
                Bp(0, 128, FB_M + ci * 512, 512), PS[0][:, 0:512],
                b1ab[:, ci:ci + 1], Bp(0, 128, FB_GELU + gsl * 512, 512),
                ALU.add, ALU.mult),
                reads=("psb0", "b1ab_a") + RB(FB_GELU + gsl * 512, 512),
                writes=RB(FB_M + ci * 512, 512))
        for ci in range(HC):
            slot = ci % 2
            p.dma("sync", lambda e, ci=ci, l=l, slot=slot: e.dma_start(
                out=B_(FB_W2 + slot * 1024, 1024),
                in_=wbf["W2"][l][ci * 128:(ci + 1) * 128, :]),
                reads=(f"wbf_W2{l}",), writes=RB(FB_W2 + slot * 1024, 1024))
            for o in range(DC):
                p.add("tensor", lambda e, ci=ci, o=o, slot=slot: e.matmul(
                    PS[o][:, 0:512],
                    B_(FB_W2 + slot * 1024 + o * 128, 128),
                    Bp(0, 128, FB_M + ci * 512, 512),
                    start=(ci == 0), stop=(ci == HC - 1)),
                    reads=RB(FB_W2 + slot * 1024, 1024) + RB(FB_M + ci * 512, 512),
                    writes=(f"psb{o}",))
        for o in range(DC):
            p.add("vector", lambda e, o=o, t0=t0: e.scalar_tensor_tensor(
                A(xb_off(o, t0), 512), PS[o][:, 0:512],
                lvecs[:, 32 + o:33 + o], A(xb_off(o, t0), 512),
                ALU.add, ALU.add),
                reads=(f"psb{o}", "lvecs") + RA(xb_off(o, t0), 512),
                writes=RA(xb_off(o, t0), 512))

    def bb_combine(b):
        outs = []
        for tt in range(4):
            t0 = tt * 512
            for dc in range(DC):
                p.dma("sync", lambda e, b=b, dc=dc, t0=t0: e.dma_start(
                    out=Ap(0, 128, BF_QN, 512),
                    in_=x_fm[b, dc * 128:(dc + 1) * 128, t0:t0 + 512]),
                    writes=RA(BF_QN, 512))
                p.dma("scalar", lambda e, b=b, dc=dc, t0=t0: e.dma_start(
                    out=Ap(0, 128, BF_QN + 512, 512),
                    in_=s_scr[dc * 128:(dc + 1) * 128, b * T + t0:b * T + t0 + 512]),
                    reads=tuple(f"sst{b}_{c}_{dc}" for c in range(NMC)),
                    writes=RA(BF_QN + 512, 512))
                p.add("scalar", lambda e, dc=dc, t0=t0: e.activation(
                    Ap(0, 128, BF_QN + 1024, 512), A(xb_off(dc, t0), 512), AF.Silu),
                    reads=RA(xb_off(dc, t0), 512), writes=RA(BF_QN + 1024, 512))
                p.add("vector", lambda e: e.tensor_mul(
                    Ap(0, 128, BF_QN + 1024, 512), Ap(0, 128, BF_QN + 1024, 512),
                    Ap(0, 128, BF_QN + 512, 512)),
                    reads=RA(BF_QN + 512, 512) + RA(BF_QN + 1024, 512),
                    writes=RA(BF_QN + 1024, 512))
                p.add("vector", lambda e: e.tensor_add(
                    Ap(0, 128, BF_QN + 1024, 512), Ap(0, 128, BF_QN + 1024, 512),
                    Ap(0, 128, BF_QN, 512)),
                    reads=RA(BF_QN, 512) + RA(BF_QN + 1024, 512),
                    writes=RA(BF_QN + 1024, 512))
                op = p.dma("gpsimd", lambda e, b=b, dc=dc, t0=t0: e.dma_start(
                    out=out_fm[b, dc * 128:(dc + 1) * 128, t0:t0 + 512],
                    in_=Ap(0, 128, BF_QN + 1024, 512)),
                    reads=RA(BF_QN + 1024, 512), writes=(f"oo{b}_{dc}",))
                outs.append(op)
        return outs

    def bb_out_xb(b):
        outs = []
        for dc in range(DC):
            op = p.dma("gpsimd", lambda e, b=b, dc=dc: e.dma_start(
                out=out_fm[b, dc * 128:(dc + 1) * 128, :],
                in_=A(xb_off(dc, 0), T)),
                reads=RA(xb_off(dc, 0), T), writes=(f"oo{b}_{dc}",))
            outs.append(op)
        return outs

    def phase_branchB(b, milestone_out):
        bb_init(b)
        for l in range(BB_LAYERS):
            bb_layer_setup(l)
            for g in range(4):
                bb_attn_group(b, l, g)
            for tt in range(4):
                bb_ffn_tt(b, l, tt)
        if milestone_out == "xb":
            return bb_out_xb(b)
        return bb_combine(b)

    phase_consts()
    if MILESTONE == "mamba":
        phase_w()
        phase_mamba()
        phase_out_mamba()
    elif MILESTONE == "bb":
        phase_w_bb()
        fin = []
        for b in range(B_PER_CORE):
            fin += phase_branchB(b, "xb")
        p.add("gpsimd", lambda e: e.engine_nop(), deps=fin)
    else:
        phase_w()
        phase_mamba()
        phase_w_bb()
        fin = []
        for b in range(B_PER_CORE):
            fin += phase_branchB(b, "combine")
        p.add("gpsimd", lambda e: e.engine_nop(), deps=fin)

    n_ops = p.finalize_and_emit()
    return nc, n_ops, (dbg_state["map"], dbg_state["mapb"])


def kernel(**inputs):
    consts = _host_consts()
    w = _prep_weights(inputs)
    x = np.asarray(inputs["L0_attn_x"])
    nc, n_ops, dbg_maps = build_program()
    kernel.dbg_map, kernel.dbg_mapb = dbg_maps

    in_maps = []
    for core in range(N_CORES):
        xs = x[core * B_PER_CORE:(core + 1) * B_PER_CORE]
        x_fmh = np.ascontiguousarray(xs.transpose(0, 2, 1))
        m = dict(x_fm=x_fmh,
                 W_in=w["W_in"], W_out=w["W_out"], W_dt=w["W_dt"],
                 W_xdbl=w["W_xdbl"], mvecs=w["mvecs"],
                 Wqkv=w["Wqkv"], Wo=w["Wo"], W1=w["W1"], W2=w["W2"],
                 b1a=w["b1a"], b1g=w["b1g"], lvecs=w["lvecs"],
                 lrows2=w["lrows2"],
                 **consts)
        in_maps.append(m)

    res = run_bass_kernel_spmd(nc, in_maps, core_ids=list(range(N_CORES)))
    outs = []
    for core in range(N_CORES):
        o = np.asarray(res.results[core]["out_fm"])
        outs.append(np.ascontiguousarray(o.transpose(0, 2, 1)))
    kernel.dbg = [np.asarray(res.results[c]["dbg"]) for c in range(N_CORES)]
    kernel.dbgb = [np.asarray(res.results[c]["dbgb"]) for c in range(N_CORES)]
    return np.concatenate(outs, axis=0).astype(np.float32)


# version 04
# speedup vs baseline: 1.1444x; 1.1444x over previous
import zlib
import numpy as np
import ml_dtypes
import concourse.bass as bass
import concourse.mybir as mybir
from concourse.bass_utils import run_bass_kernel_spmd

F32 = mybir.dt.float32
BF16 = mybir.dt.bfloat16
AF = mybir.ActivationFunctionType
ALU = mybir.AluOpType

N_CORES = 8
B_PER_CORE = 2
T = 2048
D = 1024
DC = 8
D_INNER = 2048
HEADS = 16
DH = 64
WIN = 256
NW = 8
HID = 5632
HC = 22
HIDH = 2816
MCH = 512
NMC = 4

MILESTONE = "full"
PARANOID = True
RELAX_SAME = ("tensor",)
DEBUG_DUMPS = []


class _Op:
    __slots__ = ("eng", "fn", "deps", "is_dma", "inc_no", "gidx", "fam")

    def __init__(self, eng, fn, deps, is_dma, gidx, fam=None):
        self.eng, self.fn, self.deps, self.is_dma = eng, fn, deps, is_dma
        self.inc_no = None
        self.gidx = gidx
        self.fam = fam


class Prog:
    ENGINES = ("tensor", "vector", "scalar", "gpsimd", "sync")

    def __init__(self, nc):
        self.nc = nc
        self.ops = {e: [] for e in self.ENGINES}
        self.gcount = 0
        self.regions = {}
        self.last_fam_dma = {}

    def add(self, eng, fn, reads=(), writes=(), deps=(), is_dma=False, fam=None):
        dl = list(deps)
        for r in reads:
            ent = self.regions.get(r)
            if ent is not None and ent[0] is not None:
                dl.append(ent[0])
        for w in writes:
            ent = self.regions.get(w)
            if ent is not None:
                if ent[0] is not None:
                    dl.append(ent[0])
                dl.extend(ent[1])
        if is_dma:
            fam = fam or (writes[0] if writes else "misc")
            nslots = {"sync": 44, "gpsimd": 20, "scalar": 12,
                      "vector": 4, "tensor": 4}[eng]
            fam = f"d{zlib.crc32((eng + ':' + fam).encode()) % nslots}"
            prev = self.last_fam_dma.get((eng, fam))
            if prev is not None:
                dl.append(prev)
        op = _Op(eng, fn, dl, is_dma, self.gcount, fam)
        self.gcount += 1
        self.ops[eng].append(op)
        if is_dma:
            self.last_fam_dma[(eng, fam)] = op
        for r in reads:
            self.regions.setdefault(r, [None, []])[1].append(op)
        for w in writes:
            self.regions[w] = [op, []]
        return op

    def dma(self, eng, fn, reads=(), writes=(), deps=(), fam=None):
        return self.add(eng, fn, reads, writes, deps, is_dma=True, fam=fam)

    def finalize_and_emit(self):
        nc = self.nc
        needed = set()
        for e in self.ENGINES:
            for op in self.ops[e]:
                if op.is_dma:
                    needed.add(id(op))
                for d in op.deps:
                    if d.eng != op.eng or d.is_dma or \
                       (PARANOID and d.eng not in RELAX_SAME):
                        needed.add(id(d))
        sems, counters = {}, {}

        def op_key(op):
            return (op.eng, op.fam) if op.is_dma else (op.eng, None)

        for e in self.ENGINES:
            for op in self.ops[e]:
                if id(op) in needed:
                    key = op_key(op)
                    if key not in sems:
                        sems[key] = nc.alloc_semaphore(
                            f"s_{key[0]}_{key[1] or 'c'}")
                        counters[key] = 0
                    counters[key] += 1
                    op.inc_no = counters[key]
        for e in self.ENGINES:
            for op in self.ops[e]:
                for d in op.deps:
                    if d.gidx >= op.gidx:
                        raise RuntimeError(f"dep ordering violation on {e}")

        def mk(engine_name):
            def _run(eng):
                waited = {}
                for op in self.ops[engine_name]:
                    for d in op.deps:
                        if d.eng == engine_name and not d.is_dma and \
                           ((not PARANOID) or engine_name in RELAX_SAME):
                            continue
                        key = (d.eng, d.fam) if d.is_dma else (d.eng, None)
                        sem = sems[key]
                        val = d.inc_no * (16 if d.is_dma else 1)
                        if waited.get(key, -1) >= val:
                            continue
                        eng.wait_ge(sem, val)
                        waited[key] = val
                    ins = op.fn(eng)
                    if op.inc_no is not None:
                        key = (engine_name, op.fam) if op.is_dma else (engine_name, None)
                        ins.then_inc(sems[key], 16 if op.is_dma else 1)
                if engine_name == "gpsimd":
                    for key, cnt in counters.items():
                        if key[1] is not None:
                            if waited.get(key, -1) < cnt * 16:
                                eng.wait_ge(sems[key], cnt * 16)
            return _run

        with self.nc.Block() as block:
            block.tensor(mk("tensor"))
            block.vector(mk("vector"))
            block.scalar(mk("scalar"))
            block.gpsimd(mk("gpsimd"))
            block.sync(mk("sync"))
        return sum(len(v) for v in self.ops.values())


def _host_consts():
    c = {}
    half = np.arange(0, DH, 2, dtype=np.float64)
    inv_freq = 1.0 / 10000.0 ** (half / DH)
    t = np.arange(T, dtype=np.float64)
    freqs = np.concatenate([t[:, None] * inv_freq[None, :]] * 2, -1)
    base_scale = (half + 0.4 * DH) / (1.4 * DH)
    power = (t - T // 2) / (WIN // 2)
    scale = np.concatenate([base_scale[None, :] ** power[:, None]] * 2, -1)
    cos, sin = np.cos(freqs), np.sin(freqs)
    inv = 1.0 / scale
    for name, tab in (("cs_q", cos * scale), ("ss_q", sin * scale),
                      ("cs_k", cos * inv), ("ss_k", sin * inv)):
        fm = np.ascontiguousarray(tab.T)
        fm = np.concatenate([fm, fm], 0)
        c[name] = fm.astype(ml_dtypes.bfloat16)
    jl = np.arange(512)[:, None]
    il = np.arange(256)[None, :]
    allowed = (jl - il >= 0) & (jl - il <= WIN)
    mask = np.where(allowed, 0.0, -1e30).astype(np.float32)
    c["maskT"] = np.ascontiguousarray(
        mask.reshape(4, 128, 256).transpose(1, 0, 2)).astype(ml_dtypes.bfloat16)
    h2 = np.zeros((128, 2), np.float32)
    h2[0:64, 0] = 1.0
    h2[64:128, 1] = 1.0
    c["h2"] = h2.astype(ml_dtypes.bfloat16)
    R = np.zeros((128, 128), np.float32)
    for h in range(2):
        o = h * 64
        for j in range(32):
            R[o + j + 32, o + j] = -1.0
        for j in range(32, 64):
            R[o + j - 32, o + j] = 1.0
    c["rperm"] = R.astype(ml_dtypes.bfloat16)
    c["ones_bf"] = np.ones((128, 1), np.float32).astype(ml_dtypes.bfloat16)
    c["ones_f"] = np.ones((128, 1), np.float32)
    c["onesrow_f"] = np.ones((1, 128), np.float32)
    c["onesrow_bf"] = np.ones((1, 128), np.float32).astype(ml_dtypes.bfloat16)
    c["ident_bf"] = np.eye(128, dtype=np.float32).astype(ml_dtypes.bfloat16)
    h2t = np.zeros((2, 128), np.float32)
    h2t[0, 0:64] = 1.0
    h2t[1, 64:128] = 1.0
    c["h2t"] = h2t.astype(ml_dtypes.bfloat16)
    return c


def _pp(vec):
    return np.ascontiguousarray(np.asarray(vec).reshape(DC, 128).T)


def _prep_weights(iw):
    w = {}
    w["W_in"] = np.asarray(iw["L0_attn_W_in"])
    w["W_out"] = np.asarray(iw["L0_attn_W_out"])
    w["W_dt"] = np.asarray(iw["L0_attn_W_dt"])
    xd = np.asarray(iw["L0_attn_W_xdbl"])
    xdn = np.zeros((D, 128), np.float32)
    xdn[:, 0:16] = xd[:, 64:80]
    xdn[:, 32:48] = xd[:, 80:96]
    xdn[:, 64:128] = xd[:, 0:64]
    w["W_xdbl"] = xdn
    cxw = np.asarray(iw["L0_attn_conv_x_w"])[:, 0, :]
    czw = np.asarray(iw["L0_attn_conv_z_w"])[:, 0, :]
    cxw2 = np.stack([_pp(cxw[k]) for k in range(4)], axis=2)
    czw2 = np.stack([_pp(czw[k]) for k in range(4)], axis=2)
    mv = np.zeros((128, 112), np.float32)
    mv[:, 0:32] = cxw2.reshape(128, 32)
    mv[:, 32:64] = czw2.reshape(128, 32)
    mv[:, 64:72] = _pp(iw["L0_attn_conv_x_b"])
    mv[:, 72:80] = _pp(iw["L0_attn_conv_z_b"])
    mv[:, 80:88] = _pp(iw["L0_attn_inv_dt"])
    mv[:, 96:104] = _pp(iw["L0_attn_D"])
    mv[:, 104:112] = _pp(iw["L0_attn_b_out"])
    w["mvecs"] = mv
    w["Wqkv"] = np.asarray(iw["L0_ffn_Wqkv"])
    w["Wo"] = np.asarray(iw["L0_ffn_Wo"])
    W1 = np.asarray(iw["L0_ffn_W1"])
    b1 = np.asarray(iw["L0_ffn_b1"])
    W1n = np.zeros((6, D, HID), np.float32)
    b1n = np.zeros((6, HID), np.float32)
    for ci in range(HC):
        lo, hi = 128 * ci, min(128 * ci + 128, 2730)
        if lo < 2730:
            n = hi - lo
            W1n[:, :, 256 * ci:256 * ci + n] = W1[:, :, lo:hi]
            W1n[:, :, 256 * ci + 128:256 * ci + 128 + n] = W1[:, :, 2730 + lo:2730 + hi]
            b1n[:, 256 * ci:256 * ci + n] = b1[:, lo:hi]
            b1n[:, 256 * ci + 128:256 * ci + 128 + n] = b1[:, 2730 + lo:2730 + hi]
    w["W1"] = W1n
    b1r = b1n.reshape(6, HC, 2, 128)
    w["b1a"] = np.ascontiguousarray(b1r[:, :, 0, :].transpose(0, 2, 1))
    w["b1g"] = np.ascontiguousarray(b1r[:, :, 1, :].transpose(0, 2, 1))
    W2 = np.asarray(iw["L0_ffn_W2"])
    W2n = np.zeros((6, HIDH, D), np.float32)
    W2n[:, 0:2730, :] = W2
    w["W2"] = W2n
    qs = np.tile(np.asarray(iw["L0_ffn_q_scale"])[:, None, :], (1, HEADS, 1)).reshape(6, D)
    ks = np.tile(np.asarray(iw["L0_ffn_k_scale"])[:, None, :], (1, HEADS, 1)).reshape(6, D)
    lv = np.zeros((6, 128, 48), np.float32)
    for l in range(6):
        lv[l, :, 0:8] = _pp(qs[l])
        lv[l, :, 8:16] = _pp(ks[l])
        lv[l, :, 16:24] = _pp(np.asarray(iw["L0_ffn_ln_g"])[l])
        lv[l, :, 24:32] = _pp(np.asarray(iw["L0_ffn_ln_b"])[l])
        lv[l, :, 32:40] = _pp(np.asarray(iw["L0_ffn_b2"])[l])
    w["lvecs"] = lv
    lr = np.zeros((6, 2, D), np.float32)
    lr[:, 0, :] = np.asarray(iw["L0_ffn_ln_g"])
    lr[:, 1, :] = np.asarray(iw["L0_ffn_ln_b"])
    w["lrows2"] = lr
    return w


MF_XF = 0
MF_PRE = 4160
MF_ZPRE = 8320
MF_XS = 12480
MF_RES = 16576
MF_BREP = 20672
MF_CREP = 21184
MF_P = 21696
MF_W = 22720
MF_BC = 23232
MF_MISC = 23744
AF_COLS = 26816
BF_XB = 0
BF_MISC = 16512

MB_WROT = 0
MB_WDT = 2048
MB_WXD = 3072
MB_XN = 4096
MB_ZBF = 8256
MB_XSBF = 12416
MB_DTL = 13440
MB_YBF = 13952
BB_WROT = 0
BB_WOROT = 2048
BB_XBF = 4096
BB_Q = 8256
BB_K = 12352
BB_KCUR = BB_K + 2048
BB_V = 18496
BB_VCUR = BB_V + 2048
BB_PT = 24640
BB_AFM = 26688
BB_MISC = 28736
FB_W1 = 0
FB_W2 = 4096
FB_M = 6144
FB_GELU = 17408
FB_XBF2 = 18432
FB_XN = 22592
AB_COLS = 37952
AB_TAB = AB_COLS - 4 * T
BF_QN = 16512



def build_program():
    nc = bass.Bass("TRN2", target_bir_lowering=False)
    p = Prog(nc)

    def dtin(name, shape, dtype=F32):
        return nc.dram_tensor(name, list(shape), dtype, kind="ExternalInput").ap()

    x_fm = dtin("x_fm", [B_PER_CORE, D, T])
    W_in_d = dtin("W_in", [D, D_INNER])
    W_out_d = dtin("W_out", [D_INNER, D])
    W_dt_d = dtin("W_dt", [64, D])
    W_xdbl_d = dtin("W_xdbl", [D, 128])
    mvecs_d = dtin("mvecs", [128, 112])
    Wqkv_d = dtin("Wqkv", [6, D, 3 * D])
    Wo_d = dtin("Wo", [6, D, D])
    W1_d = dtin("W1", [6, D, HID])
    W2_d = dtin("W2", [6, HIDH, D])
    b1a_d = dtin("b1a", [6, 128, HC])
    b1g_d = dtin("b1g", [6, 128, HC])
    lvecs_d = dtin("lvecs", [6, 128, 48])
    tabs_d = {nm: dtin(nm, [128, T], BF16) for nm in ("cs_q", "ss_q", "cs_k", "ss_k")}
    maskT_d = dtin("maskT", [128, 4, WIN], BF16)
    h2_d = dtin("h2", [128, 2], BF16)
    rperm_d = dtin("rperm", [128, 128], BF16)
    ones_bf_d = dtin("ones_bf", [128, 1], BF16)
    ones_f_d = dtin("ones_f", [128, 1])
    onesrow_f_d = dtin("onesrow_f", [1, 128])
    onesrow_bf_d = dtin("onesrow_bf", [1, 128], BF16)
    ident_bf_d = dtin("ident_bf", [128, 128], BF16)
    h2t_d = dtin("h2t", [2, 128], BF16)
    lrows2_d = dtin("lrows2", [6, 2, D])

    out_fm = nc.dram_tensor("out_fm", [B_PER_CORE, D, T], F32,
                            kind="ExternalOutput").ap()
    s_scr = nc.dram_tensor("s_scr", [D, B_PER_CORE * T], F32, kind="Internal").ap()
    bc_scr = nc.dram_tensor("bc_scr", [2, 48, MCH], F32, kind="Internal").ap()
    wbf = {}
    for name, shape in (("W_in", [D, D_INNER]), ("W_out", [D_INNER, D]),
                        ("W_dt", [64, D]), ("W_xdbl", [D, 128]),
                        ("Wqkv", [6, D, 3 * D]), ("Wo", [6, D, D]),
                        ("W1", [6, D, HID]), ("W2", [6, HIDH, D])):
        wbf[name] = nc.dram_tensor("bf_" + name, list(shape), BF16,
                                   kind="Internal").ap()

    dbg_d = nc.dram_tensor("dbg", [128, 65536], F32, kind="ExternalOutput").ap()
    dbg_b_d = nc.dram_tensor("dbgb", [128, 16384], BF16, kind="ExternalOutput").ap()
    dbg_state = {"off": 0, "map": {}, "offb": 0, "mapb": {}}

    af = nc.alloc_sbuf_tensor("af", [128, AF_COLS], F32)
    ab = nc.alloc_sbuf_tensor("ab", [128, AB_COLS], BF16)
    maskT = nc.alloc_sbuf_tensor("sb_maskT", [128, 4 * WIN], BF16)
    h2 = nc.alloc_sbuf_tensor("sb_h2", [128, 2], BF16)
    rperm = nc.alloc_sbuf_tensor("sb_rperm", [128, 128], BF16)
    ones_bf = nc.alloc_sbuf_tensor("sb_ones_bf", [128, 1], BF16)
    ones_f = nc.alloc_sbuf_tensor("sb_ones_f", [128, 1], F32)
    onesrow_f = nc.alloc_sbuf_tensor("sb_onesrow_f", [1, 128], F32)
    onesrow_bf = nc.alloc_sbuf_tensor("sb_onesrow_bf", [1, 128], BF16)
    ident_bf = nc.alloc_sbuf_tensor("sb_ident_bf", [128, 128], BF16)
    mvecs = nc.alloc_sbuf_tensor("sb_mvecs", [128, 112], F32)
    lvecs = nc.alloc_sbuf_tensor("sb_lvecs", [128, 48], F32)
    b1ab = nc.alloc_sbuf_tensor("sb_b1ab", [128, 2 * HC], F32)
    h2t = nc.alloc_sbuf_tensor("sb_h2t", [2, 128], BF16)
    lrows2 = nc.alloc_sbuf_tensor("sb_lrows2", [2, D], F32)
    lrows2_bf = nc.alloc_sbuf_tensor("sb_lrows2_bf", [2, D], BF16)
    rq_bf = nc.alloc_sbuf_tensor("sb_rq_bf", [128, 128], BF16)
    rk_bf = nc.alloc_sbuf_tensor("sb_rk_bf", [128, 128], BF16)
    murs_row = nc.alloc_sbuf_tensor("sb_murs_row", [2, 512], BF16)
    rs_row = nc.alloc_sbuf_tensor("sb_rs_row", [1, 512], BF16)
    qkrs = nc.alloc_sbuf_tensor("sb_qkrs", [2, 1024], BF16)
    small = nc.alloc_sbuf_tensor("sb_small", [128, 8], F32)
    wstage = [nc.alloc_sbuf_tensor(f"wstage{i}", [128, 1024], F32) for i in range(2)]
    castb = [nc.alloc_sbuf_tensor(f"castb{i}", [128, 1024], BF16) for i in range(2)]
    PS = [nc.alloc_psum_tensor(f"ps{i}", [128, 512], F32) for i in range(8)]

    def dbg(tag, ap_fn, cols, reads, rows=128):
        if not DEBUG_DUMPS or tag not in DEBUG_DUMPS:
            return
        off = dbg_state["off"]
        assert off + cols <= 65536, "dbg overflow"
        dbg_state["map"][tag] = (off, cols)
        dbg_state["off"] = off + cols
        p.dma("gpsimd", lambda e, off=off, cols=cols, rows=rows: e.dma_start(
            out=dbg_d[0:rows, off:off + cols], in_=ap_fn()),
            reads=reads, writes=(f"dbg_{tag}",))

    def dbgb(tag, ap_fn, cols, reads, rows=128):
        if not DEBUG_DUMPS or tag not in DEBUG_DUMPS:
            return
        off = dbg_state["offb"]
        assert off + cols <= 16384, "dbgb overflow"
        dbg_state["mapb"][tag] = (off, cols)
        dbg_state["offb"] = off + cols
        p.dma("gpsimd", lambda e, off=off, cols=cols, rows=rows: e.dma_start(
            out=dbg_b_d[0:rows, off:off + cols], in_=ap_fn()),
            reads=reads, writes=(f"dbgb_{tag}",))

    def A(off, length):
        return af[:, off:off + length]

    def Ap(p0, np_, off, length):
        return af[p0:p0 + np_, off:off + length]

    def B_(off, length):
        return ab[:, off:off + length]

    def Bp(p0, np_, off, length):
        return ab[p0:p0 + np_, off:off + length]

    def RA(off, length):
        return tuple(f"af{i}" for i in range(off // 512, (off + length + 511) // 512))

    def RB(off, length):
        return tuple(f"ab{i}" for i in range(off // 512, (off + length + 511) // 512))

    cast_ctr = [0]

    def cast_weight(src_ap, dst_ap, rows, cols, rkey):
        nstr = (rows + 127) // 128
        for si in range(nstr):
            r0 = si * 128
            nr = min(128, rows - r0)
            coff = 0
            while coff < cols:
                cw = min(1024, cols - coff)
                sl = cast_ctr[0] % 2
                cast_ctr[0] += 1
                p.dma("sync",
                      lambda e, s=src_ap, r0=r0, nr=nr, co=coff, cw=cw, sl=sl:
                      e.dma_start(out=wstage[sl][0:nr, 0:cw],
                                  in_=s[r0:r0 + nr, co:co + cw]),
                      writes=(f"wstage{sl}",))
                p.add("scalar",
                      lambda e, nr=nr, cw=cw, sl=sl:
                      e.activation(castb[sl][0:nr, 0:cw], wstage[sl][0:nr, 0:cw],
                                   AF.Copy),
                      reads=(f"wstage{sl}",), writes=(f"castb{sl}",))
                p.dma("gpsimd",
                      lambda e, d=dst_ap, r0=r0, nr=nr, co=coff, cw=cw, sl=sl:
                      e.dma_start(out=d[r0:r0 + nr, co:co + cw],
                                  in_=castb[sl][0:nr, 0:cw]),
                      reads=(f"castb{sl}",), writes=(rkey,))
                coff += cw

    def phase_w():
        cast_weight(W_in_d, wbf["W_in"], D, D_INNER, "wbf_W_in")
        cast_weight(W_out_d, wbf["W_out"], D_INNER, D, "wbf_W_out")
        cast_weight(W_dt_d, wbf["W_dt"], 64, D, "wbf_W_dt")
        cast_weight(W_xdbl_d, wbf["W_xdbl"], D, 128, "wbf_W_xdbl")
        if MILESTONE != "mamba":
            for l in range(6):
                cast_weight(Wqkv_d[l], wbf["Wqkv"][l], D, 3 * D, f"wbf_Wqkv{l}")
                cast_weight(Wo_d[l], wbf["Wo"][l], D, D, f"wbf_Wo{l}")
                cast_weight(W1_d[l], wbf["W1"][l], D, HID, f"wbf_W1{l}")
                cast_weight(W2_d[l], wbf["W2"][l], HIDH, D, f"wbf_W2{l}")

    def phase_consts():
        p.dma("sync", lambda e: e.dma_start(
            out=maskT[:, :], in_=maskT_d.rearrange("p k w -> p (k w)")),
            writes=("maskT",))
        for t_, nm in ((h2, "h2"), (rperm, "rperm"), (ones_bf, "ones_bf"),
                       (h2t, "h2t"),
                       (ones_f, "ones_f"), (ident_bf, "ident_bf"),
                       (mvecs, "mvecs"), (onesrow_f, "onesrow_f"),
                       (onesrow_bf, "onesrow_bf")):
            src = {"h2": h2_d, "rperm": rperm_d, "ones_bf": ones_bf_d,
                   "ones_f": ones_f_d, "ident_bf": ident_bf_d,
                   "mvecs": mvecs_d, "onesrow_f": onesrow_f_d,
                   "onesrow_bf": onesrow_bf_d, "h2t": h2t_d}[nm]
            p.dma("sync", lambda e, t_=t_, src=src: e.dma_start(
                out=t_[:, :], in_=src), writes=(nm,))
        for i, nm in enumerate(("cs_q", "ss_q", "cs_k", "ss_k")):
            p.dma("sync", lambda e, nm=nm, i=i: e.dma_start(
                out=ab[:, AB_TAB + i * T:AB_TAB + (i + 1) * T], in_=tabs_d[nm]),
                writes=(f"tab_{nm}",))
        p.add("vector", lambda e: e.tensor_scalar(
            mvecs[:, 88:96], mvecs[:, 80:88], 2.0, None, ALU.mult),
            reads=("mvecs",), writes=("mvecs",))
        p.add("vector", lambda e: e.memset(small[:, 0:1], 1e-5), writes=("eps",))
        p.add("vector", lambda e: e.memset(small[:, 2:3], 1.0), writes=("one1",))
        p.add("vector", lambda e: e.memset(murs_row[0:2, :], -1.0),
              writes=("mursrow1", "mursrow0"))

    SC = MF_MISC + 128
    STASH = MF_MISC + 2304

    def m_ln_chunk(b, c):
        t0 = c * MCH
        hal = 4 if c < NMC - 1 else 0
        nt = MCH + hal
        for dc in range(DC):
            p.dma("sync", lambda e, b=b, dc=dc, t0=t0, nt=nt:
                  e.dma_start(out=A(MF_XF + dc * 520, nt),
                              in_=x_fm[b, dc * 128:(dc + 1) * 128, t0:t0 + nt]),
                  writes=RA(MF_XF + dc * 520, nt))
        for dc in range(DC):
            p.add("vector", lambda e, dc=dc, nt=nt:
                  e.tensor_copy(Bp(0, 128, MB_XN + dc * 520, nt),
                                A(MF_XF + dc * 520, nt)),
                  reads=RA(MF_XF + dc * 520, nt),
                  writes=RB(MB_XN + dc * 520, nt))
        for dc in range(DC):
            p.add("scalar", lambda e, dc=dc, nt=nt:
                  e.activation(Bp(0, 128, MB_ZBF + dc * 520, nt),
                               A(MF_XF + dc * 520, nt), AF.Square),
                  reads=RA(MF_XF + dc * 520, nt),
                  writes=RB(MB_ZBF + dc * 520, nt))
        for dc in range(DC):
            p.add("tensor", lambda e, dc=dc: e.matmul(
                PS[5][0:1, 0:512], ones_bf.ap(), Bp(0, 128, MB_XN + dc * 520, 512),
                start=(dc == 0), stop=(dc == DC - 1)),
                reads=RB(MB_XN + dc * 520, 512) + ("ones_bf",), writes=("psb5",))
        for dc in range(DC):
            p.add("tensor", lambda e, dc=dc: e.matmul(
                PS[6][0:1, 0:512], ones_bf.ap(), Bp(0, 128, MB_ZBF + dc * 520, 512),
                start=(dc == 0), stop=(dc == DC - 1)),
                reads=RB(MB_ZBF + dc * 520, 512) + ("ones_bf",), writes=("psb6",))
        if hal:
            for dc in range(DC):
                p.add("tensor", lambda e, dc=dc, hal=hal: e.matmul(
                    PS[7][0:1, 0:hal], ones_bf.ap(),
                    Bp(0, 128, MB_XN + dc * 520 + 512, hal),
                    start=(dc == 0), stop=(dc == DC - 1)),
                    reads=RB(MB_XN + dc * 520 + 512, hal) + ("ones_bf",),
                    writes=("psb7",))
            for dc in range(DC):
                p.add("tensor", lambda e, dc=dc, hal=hal: e.matmul(
                    PS[7][0:1, 8:8 + hal], ones_bf.ap(),
                    Bp(0, 128, MB_ZBF + dc * 520 + 512, hal),
                    start=(dc == 0), stop=(dc == DC - 1)),
                    reads=RB(MB_ZBF + dc * 520 + 512, hal) + ("ones_bf",),
                    writes=("psb7",))
        p.add("vector", lambda e: e.tensor_scalar(
            Ap(0, 1, SC, 512), PS[5][0:1, 0:512], 1.0 / D, None, ALU.mult),
            reads=("psb5",), writes=RA(SC, 512))
        p.add("scalar", lambda e: e.activation(
            Ap(0, 1, SC + 512, 512), Ap(0, 1, SC, 512), AF.Square),
            reads=RA(SC, 512), writes=RA(SC + 512, 512))
        p.add("vector", lambda e: e.scalar_tensor_tensor(
            Ap(0, 1, SC + 1024, 512), PS[6][0:1, 0:512], 1.0 / D,
            Ap(0, 1, SC + 512, 512), ALU.mult, ALU.subtract),
            reads=("psb6",) + RA(SC + 512, 512), writes=RA(SC + 1024, 512))
        p.add("scalar", lambda e: e.activation(
            Ap(0, 1, SC + 1536, 512), Ap(0, 1, SC + 1024, 512), AF.Ln,
            bias=small[0:1, 0:1]),
            reads=RA(SC + 1024, 512) + ("eps",), writes=RA(SC + 1536, 512))
        p.add("scalar", lambda e: e.activation(
            Ap(0, 1, SC + 1024, 512), Ap(0, 1, SC + 1536, 512), AF.Exp,
            scale=-0.5),
            reads=RA(SC + 1536, 512), writes=RA(SC + 1024, 512))
        p.add("vector", lambda e: e.tensor_mul(
            Ap(0, 1, SC, 512), Ap(0, 1, SC, 512), Ap(0, 1, SC + 1024, 512)),
            reads=RA(SC, 512) + RA(SC + 1024, 512), writes=RA(SC, 512))
        if hal:
            p.add("vector", lambda e, hal=hal: e.tensor_scalar(
                Ap(0, 1, SC + 2048, hal), PS[7][0:1, 0:hal], 1.0 / D, None,
                ALU.mult), reads=("psb7",), writes=RA(SC + 2048, hal))
            p.add("scalar", lambda e, hal=hal: e.activation(
                Ap(0, 1, SC + 2056, hal), Ap(0, 1, SC + 2048, hal), AF.Square),
                reads=RA(SC + 2048, hal), writes=RA(SC + 2056, hal))
            p.add("vector", lambda e, hal=hal: e.scalar_tensor_tensor(
                Ap(0, 1, SC + 2064, hal), PS[7][0:1, 8:8 + hal], 1.0 / D,
                Ap(0, 1, SC + 2056, hal), ALU.mult, ALU.subtract),
                reads=("psb7",) + RA(SC + 2056, hal), writes=RA(SC + 2064, hal))
            p.add("scalar", lambda e, hal=hal: e.activation(
                Ap(0, 1, SC + 2072, hal), Ap(0, 1, SC + 2064, hal), AF.Ln,
                bias=small[0:1, 0:1]),
                reads=RA(SC + 2064, hal) + ("eps",), writes=RA(SC + 2072, hal))
            p.add("scalar", lambda e, hal=hal: e.activation(
                Ap(0, 1, SC + 2064, hal), Ap(0, 1, SC + 2072, hal), AF.Exp,
                scale=-0.5),
                reads=RA(SC + 2072, hal), writes=RA(SC + 2064, hal))
            p.add("vector", lambda e, hal=hal: e.tensor_mul(
                Ap(0, 1, SC + 2048, hal), Ap(0, 1, SC + 2048, hal),
                Ap(0, 1, SC + 2064, hal)),
                reads=RA(SC + 2048, hal) + RA(SC + 2064, hal),
                writes=RA(SC + 2048, hal))
        p.add("tensor", lambda e: e.matmul(
            PS[5][:, 0:512], onesrow_f[0:1, :], Ap(0, 1, SC + 1024, 512),
            start=True, stop=True),
            reads=RA(SC + 1024, 512) + ("onesrow_f",), writes=("psb5",))
        p.add("tensor", lambda e: e.matmul(
            PS[6][:, 0:512], onesrow_f[0:1, :], Ap(0, 1, SC, 512),
            start=True, stop=True),
            reads=RA(SC, 512) + ("onesrow_f",), writes=("psb6",))
        if hal:
            p.add("tensor", lambda e, hal=hal: e.matmul(
                PS[7][:, 16:16 + hal], onesrow_f[0:1, :], Ap(0, 1, SC + 2064, hal),
                start=True, stop=True),
                reads=RA(SC + 2064, hal) + ("onesrow_f",), writes=("psb7",))
            p.add("tensor", lambda e, hal=hal: e.matmul(
                PS[7][:, 24:24 + hal], onesrow_f[0:1, :], Ap(0, 1, SC + 2048, hal),
                start=True, stop=True),
                reads=RA(SC + 2048, hal) + ("onesrow_f",), writes=("psb7",))
        for dc in range(DC):
            p.add("vector", lambda e, dc=dc: e.tensor_mul(
                A(MF_XF + dc * 520, 512), A(MF_XF + dc * 520, 512),
                PS[5][:, 0:512]),
                reads=("psb5",) + RA(MF_XF + dc * 520, 512),
                writes=RA(MF_XF + dc * 520, 512))
            p.add("vector", lambda e, dc=dc: e.tensor_tensor(
                Bp(0, 128, MB_XN + dc * 520, 512), A(MF_XF + dc * 520, 512),
                PS[6][:, 0:512], ALU.subtract),
                reads=("psb6",) + RA(MF_XF + dc * 520, 512),
                writes=RB(MB_XN + dc * 520, 512))
            if hal:
                p.add("vector", lambda e, dc=dc, hal=hal: e.tensor_mul(
                    A(MF_XF + dc * 520 + 512, hal),
                    A(MF_XF + dc * 520 + 512, hal), PS[7][:, 16:16 + hal]),
                    reads=("psb7",) + RA(MF_XF + dc * 520 + 512, hal),
                    writes=RA(MF_XF + dc * 520 + 512, hal))
                p.add("vector", lambda e, dc=dc, hal=hal: e.tensor_tensor(
                    Bp(0, 128, MB_XN + dc * 520 + 512, hal),
                    A(MF_XF + dc * 520 + 512, hal),
                    PS[7][:, 24:24 + hal], ALU.subtract),
                    reads=("psb7",) + RA(MF_XF + dc * 520 + 512, hal),
                    writes=RB(MB_XN + dc * 520 + 512, hal))

    def m_xz_chunk(b, c):
        hal = 4 if c < NMC - 1 else 0
        if c == 0:
            for dc in range(DC):
                p.add("vector", lambda e, dc=dc: e.memset(
                    A(MF_PRE + dc * 520, 2), 0.0), writes=RA(MF_PRE + dc * 520, 2))
                p.add("vector", lambda e, dc=dc: e.memset(
                    A(MF_ZPRE + dc * 520, 2), 0.0), writes=RA(MF_ZPRE + dc * 520, 2))
        else:
            for dc in range(DC):
                p.add("vector", lambda e, dc=dc: e.tensor_copy(
                    A(MF_PRE + dc * 520, 2), A(STASH + dc * 2, 2)),
                    reads=RA(STASH + dc * 2, 2), writes=RA(MF_PRE + dc * 520, 2))
                p.add("vector", lambda e, dc=dc: e.tensor_copy(
                    A(MF_ZPRE + dc * 520, 2), A(STASH + 16 + dc * 2, 2)),
                    reads=RA(STASH + 16 + dc * 2, 2),
                    writes=RA(MF_ZPRE + dc * 520, 2))
        for o in range(16):
            slot = o % 2
            pb = o % 2
            p.dma("sync", lambda e, o=o, slot=slot: e.dma_start(
                out=B_(MB_WROT + slot * 1024, 1024).rearrange(
                    "p (c m) -> p c m", m=128),
                in_=wbf["W_in"].rearrange("(c p) m -> p c m", p=128)[
                    :, :, o * 128:(o + 1) * 128]),
                reads=("wbf_W_in",), writes=RB(MB_WROT + slot * 1024, 1024))
            for dc in range(DC):
                p.add("tensor", lambda e, dc=dc, slot=slot, pb=pb: e.matmul(
                    PS[pb][:, 0:512],
                    B_(MB_WROT + slot * 1024 + dc * 128, 128),
                    Bp(0, 128, MB_XN + dc * 520, 512),
                    start=(dc == 0), stop=(dc == DC - 1)),
                    reads=RB(MB_WROT + slot * 1024, 1024) +
                          RB(MB_XN + dc * 520, 512),
                    writes=(f"psb{pb}",))
            if hal:
                for dc in range(DC):
                    p.add("tensor", lambda e, o=o, dc=dc, slot=slot, hal=hal:
                          e.matmul(PS[2][:, o * 4:o * 4 + hal],
                                   B_(MB_WROT + slot * 1024 + dc * 128, 128),
                                   Bp(0, 128, MB_XN + dc * 520 + 512, hal),
                                   start=(dc == 0), stop=(dc == DC - 1)),
                          reads=RB(MB_WROT + slot * 1024, 1024) +
                                RB(MB_XN + dc * 520 + 512, hal),
                          writes=("psb2",))
            base = MF_PRE if o < 8 else MF_ZPRE
            dcc = o % 8
            p.add("vector", lambda e, base=base, dcc=dcc, pb=pb: e.tensor_copy(
                A(base + dcc * 520 + 2, 512), PS[pb][:, 0:512]),
                reads=(f"psb{pb}",), writes=RA(base + dcc * 520 + 2, 512))
            if hal:
                p.add("vector", lambda e, base=base, dcc=dcc, o=o, hal=hal:
                      e.tensor_copy(A(base + dcc * 520 + 514, hal),
                                    PS[2][:, o * 4:o * 4 + hal]),
                      reads=("psb2",), writes=RA(base + dcc * 520 + 514, hal))
            else:
                p.add("vector", lambda e, base=base, dcc=dcc: e.memset(
                    A(base + dcc * 520 + 514, 4), 0.0),
                    writes=RA(base + dcc * 520 + 514, 4))
        for dc in range(DC):
            p.add("vector", lambda e, dc=dc: e.tensor_copy(
                A(STASH + dc * 2, 2), A(MF_PRE + dc * 520 + 512, 2)),
                reads=RA(MF_PRE + dc * 520 + 512, 2),
                writes=RA(STASH + dc * 2, 2))
            p.add("vector", lambda e, dc=dc: e.tensor_copy(
                A(STASH + 16 + dc * 2, 2), A(MF_ZPRE + dc * 520 + 512, 2)),
                reads=RA(MF_ZPRE + dc * 520 + 512, 2),
                writes=RA(STASH + 16 + dc * 2, 2))

    def m_conv_chunk(b, c):
        for dc in range(DC):
            src_ = MF_PRE + dc * 520
            dst = MF_XS + dc * 512
            p.add("vector", lambda e, src_=src_, dst=dst, dc=dc: e.tensor_scalar(
                A(dst, 512), A(src_ + 1, 512), mvecs[:, dc * 4:dc * 4 + 1],
                None, ALU.mult),
                reads=RA(src_, 520) + ("mvecs",), writes=RA(dst, 512))
            for k in range(1, 4):
                p.add("vector", lambda e, src_=src_, dst=dst, dc=dc, k=k:
                      e.scalar_tensor_tensor(
                          A(dst, 512), A(src_ + 1 + k, 512),
                          mvecs[:, dc * 4 + k:dc * 4 + k + 1],
                          A(dst, 512), ALU.mult, ALU.add),
                      reads=RA(src_, 520) + RA(dst, 512) + ("mvecs",),
                      writes=RA(dst, 512))
            p.add("scalar", lambda e, dst=dst, dc=dc: e.activation(
                A(dst, 512), A(dst, 512), AF.Silu, bias=mvecs[:, 64 + dc:65 + dc]),
                reads=RA(dst, 512) + ("mvecs",), writes=RA(dst, 512))
            p.add("vector", lambda e, dst=dst, dc=dc: e.tensor_copy(
                Bp(0, 128, MB_XSBF + (dc % 2) * 512, 512), A(dst, 512)),
                reads=RA(dst, 512), writes=RB(MB_XSBF + (dc % 2) * 512, 512))
            p.add("tensor", lambda e, dc=dc: e.matmul(
                PS[3][:, 0:512], Bp(0, 128, MB_WXD + dc * 128, 128),
                Bp(0, 128, MB_XSBF + (dc % 2) * 512, 512),
                start=(dc == 0), stop=(dc == DC - 1)),
                reads=RB(MB_WXD, 1024) + RB(MB_XSBF + (dc % 2) * 512, 512),
                writes=("psb3",))
        for dc in range(DC):
            src_ = MF_ZPRE + dc * 520
            dst = MF_RES + dc * 512
            p.add("vector", lambda e, src_=src_, dst=dst, dc=dc: e.tensor_scalar(
                A(dst, 512), A(src_ + 1, 512),
                mvecs[:, 32 + dc * 4:32 + dc * 4 + 1], None, ALU.mult),
                reads=RA(src_, 520) + ("mvecs",), writes=RA(dst, 512))
            for k in range(1, 4):
                p.add("vector", lambda e, src_=src_, dst=dst, dc=dc, k=k:
                      e.scalar_tensor_tensor(
                          A(dst, 512), A(src_ + 1 + k, 512),
                          mvecs[:, 32 + dc * 4 + k:32 + dc * 4 + k + 1],
                          A(dst, 512), ALU.mult, ALU.add),
                      reads=RA(src_, 520) + RA(dst, 512) + ("mvecs",),
                      writes=RA(dst, 512))
            p.add("scalar", lambda e, dst=dst, dc=dc: e.activation(
                Bp(0, 128, MB_ZBF + dc * 512, 512), A(dst, 512), AF.Silu,
                bias=mvecs[:, 72 + dc:73 + dc]),
                reads=RA(dst, 512) + ("mvecs",), writes=RB(MB_ZBF + dc * 512, 512))

    def m_xdbl_dt_chunk(b, c):
        p.add("vector", lambda e: e.tensor_copy(
            Ap(0, 48, MF_BC, 512), PS[3][0:48, 0:512]),
            reads=("psb3",), writes=RA(MF_BC, 512))
        p.add("scalar", lambda e: e.activation(
            Bp(0, 64, MB_DTL, 512), PS[3][64:128, 0:512], AF.Copy),
            reads=("psb3",), writes=RB(MB_DTL, 512))
        p.dma("gpsimd", lambda e, c=c: e.dma_start(
            out=bc_scr[c % 2], in_=Ap(0, 48, MF_BC, 512)),
            reads=RA(MF_BC, 512), writes=(f"bcscr{c % 2}",))
        for o in range(DC):
            p.add("tensor", lambda e, o=o: e.matmul(
                PS[4][:, 0:512], Bp(0, 64, MB_WDT + o * 128, 128),
                Bp(0, 64, MB_DTL, 512), start=True, stop=True),
                reads=RB(MB_WDT, 1024) + RB(MB_DTL, 512), writes=("psb4",))
            p.add("vector", lambda e, o=o: e.tensor_scalar(
                A(MF_PRE + o * 512, 512), PS[4][:, 0:512],
                mvecs[:, 88 + o:89 + o], None, ALU.add),
                reads=("psb4", "mvecs"), writes=RA(MF_PRE + o * 512, 512))
            p.add("scalar", lambda e, o=o: e.activation(
                A(MF_W, 512), A(MF_PRE + o * 512, 512), AF.Abs),
                reads=RA(MF_PRE + o * 512, 512), writes=RA(MF_W, 512))
            p.add("scalar", lambda e, o=o: e.activation(
                A(MF_W, 512), A(MF_W, 512), AF.Exp, scale=-1.0),
                reads=RA(MF_W, 512), writes=RA(MF_W, 512))
            p.add("scalar", lambda e, o=o: e.activation(
                A(MF_W, 512), A(MF_W, 512), AF.Ln, bias=small[:, 2:3]),
                reads=RA(MF_W, 512) + ("one1",), writes=RA(MF_W, 512))
            p.add("vector", lambda e, o=o: e.tensor_scalar(
                A(MF_P, 512), A(MF_PRE + o * 512, 512), 0.0, None, ALU.max),
                reads=RA(MF_PRE + o * 512, 512), writes=RA(MF_P, 512))
            p.add("vector", lambda e, o=o: e.tensor_add(
                A(MF_PRE + o * 512, 512), A(MF_P, 512), A(MF_W, 512)),
                reads=RA(MF_P, 512) + RA(MF_W, 512),
                writes=RA(MF_PRE + o * 512, 512))
            p.add("vector", lambda e, o=o: e.tensor_mul(
                A(MF_ZPRE + o * 512, 512), A(MF_PRE + o * 512, 512),
                A(MF_XS + o * 512, 512)),
                reads=RA(MF_PRE + o * 512, 512) + RA(MF_XS + o * 512, 512),
                writes=RA(MF_ZPRE + o * 512, 512))

    def m_scan_chunk(b, c):
        for n in range(16):
            p.dma("scalar", lambda e, c=c, n=n: e.dma_start(
                out=A(MF_BREP, 512),
                in_=bc_scr[c % 2, n:n + 1, :].partition_broadcast(128).squeeze(1)),
                reads=(f"bcscr{c % 2}",), writes=RA(MF_BREP, 512))
            p.dma("scalar", lambda e, c=c, n=n: e.dma_start(
                out=A(MF_CREP, 512),
                in_=bc_scr[c % 2, 32 + n:33 + n, :]
                .partition_broadcast(128).squeeze(1)),
                reads=(f"bcscr{c % 2}",), writes=RA(MF_CREP, 512))
            for dc in range(DC):
                pslot = (n * DC + dc) % 2
                p.add("scalar", lambda e, n=n, dc=dc, pslot=pslot: e.activation(
                    A(MF_P + pslot * 512, 512), A(MF_PRE + dc * 512, 512),
                    AF.Exp, scale=-(n + 1.0)),
                    reads=RA(MF_PRE + dc * 512, 512),
                    writes=RA(MF_P + pslot * 512, 512))
                p.add("vector", lambda e, dc=dc: e.tensor_mul(
                    A(MF_W, 512), A(MF_ZPRE + dc * 512, 512), A(MF_BREP, 512)),
                    reads=RA(MF_ZPRE + dc * 512, 512) + RA(MF_BREP, 512),
                    writes=RA(MF_W, 512))
                p.add("vector", lambda e, n=n, dc=dc, pslot=pslot:
                      e.tensor_tensor_scan(
                          A(MF_RES + dc * 512, 512), A(MF_P + pslot * 512, 512),
                          A(MF_W, 512),
                          af[:, MF_MISC + n * 8 + dc:MF_MISC + n * 8 + dc + 1],
                          ALU.mult, ALU.add),
                      reads=RA(MF_P + pslot * 512, 512) + RA(MF_W, 512) +
                            RA(MF_MISC, 128),
                      writes=RA(MF_RES + dc * 512, 512))
                p.add("vector", lambda e, dc=dc: e.tensor_mul(
                    Bp(0, 128, MB_XSBF + (dc % 2) * 512, 512),
                    A(MF_RES + dc * 512, 512), A(MF_CREP, 512)),
                    reads=RA(MF_RES + dc * 512, 512) + RA(MF_CREP, 512),
                    writes=RB(MB_XSBF + (dc % 2) * 512, 512))
                p.add("tensor", lambda e, n=n, dc=dc: e.matmul(
                    PS[dc][:, 0:512], ident_bf.ap(),
                    Bp(0, 128, MB_XSBF + (dc % 2) * 512, 512),
                    start=(n == 0), stop=False),
                    reads=RB(MB_XSBF + (dc % 2) * 512, 512) + ("ident_bf",),
                    writes=(f"psb{dc}",))
            p.add("vector", lambda e, n=n: e.tensor_copy(
                af[:, MF_MISC + n * 8:MF_MISC + n * 8 + 8],
                af[:, MF_RES:MF_RES + 4096].rearrange(
                    "p (ci t) -> p ci t", t=512)[:, :, 511]),
                reads=RA(MF_RES, 4096),
                writes=RA(MF_MISC, 128))
        for dc in range(DC):
            p.add("vector", lambda e, dc=dc: e.tensor_scalar(
                Bp(0, 128, MB_XSBF + (dc % 2) * 512, 512),
                A(MF_XS + dc * 512, 512),
                mvecs[:, 96 + dc:97 + dc], None, ALU.mult),
                reads=RA(MF_XS + dc * 512, 512) + ("mvecs",),
                writes=RB(MB_XSBF + (dc % 2) * 512, 512))
            p.add("tensor", lambda e, dc=dc: e.matmul(
                PS[dc][:, 0:512], ident_bf.ap(),
                Bp(0, 128, MB_XSBF + (dc % 2) * 512, 512),
                start=False, stop=True),
                reads=RB(MB_XSBF + (dc % 2) * 512, 512) + ("ident_bf",),
                writes=(f"psb{dc}",))
            p.add("scalar", lambda e, dc=dc: e.activation(
                Bp(0, 128, MB_YBF + dc * 512, 512), PS[dc][:, 0:512], AF.Copy),
                reads=(f"psb{dc}",), writes=RB(MB_YBF + dc * 512, 512))

    def m_wout_chunk(b, c):
        t0 = c * MCH
        for k in range(16):
            slot = k % 2
            p.dma("sync", lambda e, k=k, slot=slot: e.dma_start(
                out=B_(MB_WROT + slot * 1024, 1024),
                in_=wbf["W_out"][k * 128:(k + 1) * 128, :]),
                reads=("wbf_W_out",), writes=RB(MB_WROT + slot * 1024, 1024))
            rhs_off = (MB_YBF + k * 512) if k < 8 else (MB_ZBF + (k - 8) * 512)
            for o in range(DC):
                p.add("tensor", lambda e, k=k, o=o, slot=slot, rhs_off=rhs_off:
                      e.matmul(PS[o][:, 0:512],
                               B_(MB_WROT + slot * 1024 + o * 128, 128),
                               Bp(0, 128, rhs_off, 512),
                               start=(k == 0), stop=(k == 15)),
                      reads=RB(MB_WROT + slot * 1024, 1024) + RB(rhs_off, 512),
                      writes=(f"psb{o}",))
        for o in range(DC):
            p.add("vector", lambda e, o=o: e.tensor_scalar(
                A(MF_RES + o * 512, 512), PS[o][:, 0:512],
                mvecs[:, 104 + o:105 + o], None, ALU.add),
                reads=(f"psb{o}", "mvecs"), writes=RA(MF_RES + o * 512, 512))
            p.dma("gpsimd", lambda e, o=o, b=b, t0=t0: e.dma_start(
                out=s_scr[o * 128:(o + 1) * 128, b * T + t0:b * T + t0 + 512],
                in_=A(MF_RES + o * 512, 512)),
                reads=RA(MF_RES + o * 512, 512), writes=(f"sst{b}_{c}_{o}",))

    def phase_mamba():
        p.dma("sync", lambda e: e.dma_start(
            out=Bp(0, 64, MB_WDT, 1024), in_=wbf["W_dt"]),
            reads=("wbf_W_dt",), writes=RB(MB_WDT, 1024))
        for k in range(DC):
            p.dma("sync", lambda e, k=k: e.dma_start(
                out=Bp(0, 128, MB_WXD + k * 128, 128),
                in_=wbf["W_xdbl"][k * 128:(k + 1) * 128, :]),
                reads=("wbf_W_xdbl",), writes=RB(MB_WXD + k * 128, 128))
        for b in range(B_PER_CORE):
            p.add("vector", lambda e: e.memset(
                af[:, MF_MISC:MF_MISC + 128], 0.0),
                writes=RA(MF_MISC, 128))
            for c in range(NMC):
                m_ln_chunk(b, c)
                m_xz_chunk(b, c)
                m_conv_chunk(b, c)
                m_xdbl_dt_chunk(b, c)
                m_scan_chunk(b, c)
                m_wout_chunk(b, c)

    def phase_out_mamba():
        outs = []
        for b in range(B_PER_CORE):
            for o in range(DC):
                op = p.dma("gpsimd", lambda e, b=b, o=o: e.dma_start(
                    out=out_fm[b, o * 128:(o + 1) * 128, :],
                    in_=s_scr[o * 128:(o + 1) * 128, b * T:(b + 1) * T]),
                    reads=tuple(f"sst{b}_{c}_{o}" for c in range(NMC)),
                    writes=(f"oo{b}_{o}",))
                outs.append(op)
        p.add("gpsimd", lambda e: e.engine_nop(), deps=outs)



    BB_LAYERS = int(globals().get("BB_NLAYERS", 6))

    def phase_w_bb():
        for l in range(BB_LAYERS):
            cast_weight(Wqkv_d[l], wbf["Wqkv"][l], D, 3 * D, f"wbf_Wqkv{l}")
            cast_weight(Wo_d[l], wbf["Wo"][l], D, D, f"wbf_Wo{l}")
            cast_weight(W1_d[l], wbf["W1"][l], D, HID, f"wbf_W1{l}")
            cast_weight(W2_d[l], wbf["W2"][l], HIDH, D, f"wbf_W2{l}")

    XBO = BF_XB

    def xb_off(dc, t):
        return XBO + dc * 2064 + 1 + t

    def bb_init(b):
        for dc in range(DC):
            p.add("vector", lambda e, dc=dc: e.memset(
                A(XBO + dc * 2064, 1), 0.0), writes=RA(XBO + dc * 2064, 1))
        for tt in range(4):
            t0 = tt * 512
            for dc in range(DC):
                p.dma("sync", lambda e, b=b, dc=dc, t0=t0: e.dma_start(
                    out=A(xb_off(dc, t0), 512),
                    in_=x_fm[b, dc * 128:(dc + 1) * 128, t0:t0 + 512]),
                    writes=RA(xb_off(dc, t0), 512))
            for dc in range(DC):
                p.add("vector", lambda e, dc=dc, t0=t0: e.tensor_copy(
                    Bp(0, 128, FB_M + dc * 512, 512), A(xb_off(dc, t0), 512)),
                    reads=RA(xb_off(dc, t0), 512),
                    writes=RB(FB_M + dc * 512, 512))
                p.add("scalar", lambda e, dc=dc, t0=t0: e.activation(
                    Bp(0, 128, FB_XN + dc * 512, 512), A(xb_off(dc, t0), 512),
                    AF.Square),
                    reads=RA(xb_off(dc, t0), 512),
                    writes=RB(FB_XN + dc * 512, 512))
            for dc in range(DC):
                p.add("tensor", lambda e, dc=dc: e.matmul(
                    PS[5][0:1, 0:512], ones_bf.ap(), Bp(0, 128, FB_M + dc * 512, 512),
                    start=(dc == 0), stop=(dc == DC - 1)),
                    reads=RB(FB_M + dc * 512, 512) + ("ones_bf",), writes=("psb5",))
                p.add("tensor", lambda e, dc=dc: e.matmul(
                    PS[6][0:1, 0:512], ones_bf.ap(), Bp(0, 128, FB_XN + dc * 512, 512),
                    start=(dc == 0), stop=(dc == DC - 1)),
                    reads=RB(FB_XN + dc * 512, 512) + ("ones_bf",), writes=("psb6",))
            ln_rows_512()
            p.add("tensor", lambda e: e.matmul(
                PS[5][:, 0:512], onesrow_f[0:1, :], Ap(0, 1, SC + 1024, 512),
                start=True, stop=True),
                reads=RA(SC + 1024, 512) + ("onesrow_f",), writes=("psb5",))
            p.add("tensor", lambda e: e.matmul(
                PS[6][:, 0:512], onesrow_f[0:1, :], Ap(0, 1, SC, 512),
                start=True, stop=True),
                reads=RA(SC, 512) + ("onesrow_f",), writes=("psb6",))
            for dc in range(DC):
                p.add("vector", lambda e, dc=dc, t0=t0: e.tensor_mul(
                    A(xb_off(dc, t0), 512), A(xb_off(dc, t0), 512), PS[5][:, 0:512]),
                    reads=("psb5",) + RA(xb_off(dc, t0), 512),
                    writes=RA(xb_off(dc, t0), 512))
                p.add("vector", lambda e, dc=dc, t0=t0: e.tensor_tensor(
                    A(xb_off(dc, t0), 512), A(xb_off(dc, t0), 512), PS[6][:, 0:512],
                    ALU.subtract),
                    reads=("psb6",) + RA(xb_off(dc, t0), 512),
                    writes=RA(xb_off(dc, t0), 512))

    def ln_rows_512():
        p.add("vector", lambda e: e.tensor_scalar(
            Ap(0, 1, SC, 512), PS[5][0:1, 0:512], 1.0 / D, None, ALU.mult),
            reads=("psb5",), writes=RA(SC, 512))
        p.add("scalar", lambda e: e.activation(
            Ap(0, 1, SC + 512, 512), Ap(0, 1, SC, 512), AF.Square),
            reads=RA(SC, 512), writes=RA(SC + 512, 512))
        p.add("vector", lambda e: e.scalar_tensor_tensor(
            Ap(0, 1, SC + 1024, 512), PS[6][0:1, 0:512], 1.0 / D,
            Ap(0, 1, SC + 512, 512), ALU.mult, ALU.subtract),
            reads=("psb6",) + RA(SC + 512, 512), writes=RA(SC + 1024, 512))
        p.add("scalar", lambda e: e.activation(
            Ap(0, 1, SC + 1536, 512), Ap(0, 1, SC + 1024, 512), AF.Ln,
            bias=small[0:1, 0:1]),
            reads=RA(SC + 1024, 512) + ("eps",), writes=RA(SC + 1536, 512))
        p.add("scalar", lambda e: e.activation(
            Ap(0, 1, SC + 1024, 512), Ap(0, 1, SC + 1536, 512), AF.Exp,
            scale=-0.5),
            reads=RA(SC + 1536, 512), writes=RA(SC + 1024, 512))
        p.add("vector", lambda e: e.tensor_mul(
            Ap(0, 1, SC, 512), Ap(0, 1, SC, 512), Ap(0, 1, SC + 1024, 512)),
            reads=RA(SC, 512) + RA(SC + 1024, 512), writes=RA(SC, 512))

    def bb_layer_setup(l):
        p.dma("sync", lambda e, l=l: e.dma_start(
            out=lvecs[:, :], in_=lvecs_d[l]), writes=("lvecs",))
        p.dma("sync", lambda e, l=l: e.dma_start(
            out=b1ab[:, 0:HC], in_=b1a_d[l]), writes=("b1ab_a",))
        p.dma("sync", lambda e, l=l: e.dma_start(
            out=b1ab[:, HC:2 * HC], in_=b1g_d[l]), writes=("b1ab_g",))
        p.dma("sync", lambda e, l=l: e.dma_start(
            out=lrows2[:, :], in_=lrows2_d[l]), writes=("lrows2",))
        p.add("vector", lambda e: e.tensor_copy(lrows2_bf[:, :], lrows2[:, :]),
              reads=("lrows2",), writes=("lrows2_bf",))
        for i, nm in enumerate(("cs_q", "ss_q", "cs_k", "ss_k")):
            p.dma("sync", lambda e, nm=nm, i=i: e.dma_start(
                out=ab[:, AB_TAB + i * T:AB_TAB + (i + 1) * T], in_=tabs_d[nm]),
                writes=RB(AB_TAB + i * T, T))
        p.add("vector", lambda e: e.tensor_scalar(
            ab[:, AB_TAB:AB_TAB + T], ab[:, AB_TAB:AB_TAB + T],
            lvecs[:, 0:1], 8.0, ALU.mult, ALU.mult),
            reads=RB(AB_TAB, T) + ("lvecs",), writes=RB(AB_TAB, T))
        p.add("vector", lambda e: e.tensor_scalar(
            ab[:, AB_TAB + 2 * T:AB_TAB + 3 * T], ab[:, AB_TAB + 2 * T:AB_TAB + 3 * T],
            lvecs[:, 8:9], None, ALU.mult),
            reads=RB(AB_TAB + 2 * T, T) + ("lvecs",), writes=RB(AB_TAB + 2 * T, T))
        p.add("vector", lambda e: e.tensor_scalar(
            rq_bf[:, :], rperm[:, :], lvecs[:, 0:1], 8.0, ALU.mult, ALU.mult),
            reads=("rperm", "lvecs"), writes=("rq_bf",))
        p.add("vector", lambda e: e.tensor_scalar(
            rk_bf[:, :], rperm[:, :], lvecs[:, 8:9], None, ALU.mult),
            reads=("rperm", "lvecs"), writes=("rk_bf",))

    def xbf_slice(base, dc, shifted_halo=True):
        off = base + dc * 520 + (0 if dc >= 4 else 1)
        return off

    def bb_attn_group(b, l, g):
        t0 = g * 512
        if g == 0:
            p.add("vector", lambda e: e.memset(
                af[0:1, 0:1] if False else ab[:, BB_XBF:BB_XBF + 1], 0.0),
                writes=RB(BB_XBF, 1))
            for dc in range(DC):
                p.add("vector", lambda e, dc=dc: e.memset(
                    ab[:, BB_XBF + dc * 520:BB_XBF + dc * 520 + 1], 0.0),
                    writes=RB(BB_XBF + dc * 520, 1))
        else:
            p.add("vector", lambda e: e.tensor_copy(
                ab[:, BB_XBF:BB_XBF + 8 * 520].rearrange(
                    "p (c t) -> p c t", t=520)[:, :, 0:1].squeeze(2)
                if False else
                ab[:, BB_XBF:BB_XBF + 8 * 520].rearrange(
                    "p (c t) -> p c t", t=520)[:, :, 0],
                ab[:, BB_XBF:BB_XBF + 8 * 520].rearrange(
                    "p (c t) -> p c t", t=520)[:, :, 512]),
                reads=RB(BB_XBF, 8 * 520), writes=RB(BB_XBF, 8 * 520))
        for dc in range(DC):
            p.add("vector", lambda e, dc=dc, t0=t0: e.tensor_copy(
                Bp(0, 128, BB_XBF + dc * 520 + 1, 512),
                A(xb_off(dc, t0), 512)),
                reads=RA(xb_off(dc, t0), 512),
                writes=RB(BB_XBF + dc * 520 + 1, 512))
        SCR = BB_AFM

        def qk_mms(o):
            slot = o % 2
            pb = o % 2
            p.dma("sync", lambda e, o=o, l=l, slot=slot: e.dma_start(
                out=B_(BB_WROT + slot * 1024, 1024).rearrange(
                    "p (c m) -> p c m", m=128),
                in_=wbf["Wqkv"][l].rearrange("(c p) m -> p c m", p=128)[
                    :, :, o * 128:(o + 1) * 128]),
                reads=(f"wbf_Wqkv{l}",), writes=RB(BB_WROT + slot * 1024, 1024))
            for dc in range(DC):
                p.add("tensor", lambda e, dc=dc, slot=slot, pb=pb: e.matmul(
                    PS[pb][:, 0:512],
                    B_(BB_WROT + slot * 1024 + dc * 128, 128),
                    Bp(0, 128, xbf_slice(BB_XBF, dc), 512),
                    start=(dc == 0), stop=(dc == DC - 1)),
                    reads=RB(BB_WROT + slot * 1024, 1024) +
                          RB(BB_XBF + dc * 520, 520),
                    writes=(f"psb{pb}",))

        def qk_post(o):
            slot = o % 2
            pb = o % 2
            isq = o < 8
            sqo = BB_PT + slot * 1024
            qro = BB_PT + slot * 1024 + 512
            sb_ = 2 if slot == 0 else 4
            rb_ = 3 if slot == 0 else 5
            lnro = SC + 1536 + slot * 512
            p.add("scalar", lambda e, pb=pb, sqo=sqo: e.activation(
                Bp(0, 128, sqo, 512), PS[pb][:, 0:512], AF.Square),
                reads=(f"psb{pb}",), writes=RB(sqo, 512))
            p.add("scalar", lambda e, pb=pb, qro=qro: e.activation(
                Bp(0, 128, qro, 512), PS[pb][:, 0:512], AF.Copy),
                reads=(f"psb{pb}",), writes=RB(qro, 512))
            p.add("tensor", lambda e, sqo=sqo, sb_=sb_: e.matmul(
                PS[sb_][0:2, 0:512], h2.ap(), Bp(0, 128, sqo, 512),
                start=True, stop=True),
                reads=RB(sqo, 512) + ("h2",), writes=(f"psb{sb_}",))
            p.add("scalar", lambda e, sb_=sb_, lnro=lnro: e.activation(
                Ap(0, 2, lnro, 512), PS[sb_][0:2, 0:512], AF.Ln),
                reads=(f"psb{sb_}",), writes=RA(lnro, 512))
            p.add("scalar", lambda e, slot=slot, lnro=lnro: e.activation(
                qkrs[0:2, slot * 512:slot * 512 + 512], Ap(0, 2, lnro, 512),
                AF.Exp, scale=-0.5),
                reads=RA(lnro, 512), writes=(f"qkrs{slot}",))
            p.add("tensor", lambda e, slot=slot, rb_=rb_: e.matmul(
                PS[rb_][:, 0:512], h2t.ap(), qkrs[0:2, slot * 512:slot * 512 + 512],
                start=True, stop=True),
                reads=(f"qkrs{slot}", "h2t"), writes=(f"psb{rb_}",))
            p.add("vector", lambda e, rb_=rb_, qro=qro: e.tensor_mul(
                Bp(0, 128, qro, 512), Bp(0, 128, qro, 512), PS[rb_][:, 0:512]),
                reads=(f"psb{rb_}",) + RB(qro, 512), writes=RB(qro, 512))
            rot = rq_bf if isq else rk_bf
            rotkey = "rq_bf" if isq else "rk_bf"
            p.add("tensor", lambda e, rot=rot, rb_=rb_, qro=qro: e.matmul(
                PS[rb_][:, 0:512], rot.ap(), Bp(0, 128, qro, 512),
                start=True, stop=True),
                reads=RB(qro, 512) + (rotkey,), writes=(f"psb{rb_}",))
            tab = AB_TAB + (0 if isq else 2 * T) + t0
            p.add("vector", lambda e, tab=tab, qro=qro, slot=slot: e.tensor_mul(
                Bp(0, 128, SCR + slot * 1024, 512), Bp(0, 128, qro, 512),
                B_(tab, 512)),
                reads=RB(qro, 512) + RB(tab, 512),
                writes=RB(SCR + slot * 1024, 512))
            stab = AB_TAB + (T if isq else 3 * T) + t0
            p.add("vector", lambda e, stab=stab, rb_=rb_, slot=slot: e.tensor_mul(
                Bp(0, 128, SCR + slot * 1024 + 512, 512), PS[rb_][:, 0:512],
                B_(stab, 512)),
                reads=(f"psb{rb_}",) + RB(stab, 512),
                writes=RB(SCR + slot * 1024 + 512, 512))
            dst = (BB_Q + o * 512) if isq else (BB_KCUR + (o - 8) * 512)
            p.add("vector", lambda e, dst=dst, slot=slot: e.tensor_add(
                Bp(0, 128, dst, 512), Bp(0, 128, SCR + slot * 1024, 512),
                Bp(0, 128, SCR + slot * 1024 + 512, 512)),
                reads=RB(SCR + slot * 1024, 1024),
                writes=RB(dst, 512))

        for o in range(16):
            qk_mms(o)
            if o > 0:
                qk_post(o - 1)
        qk_post(15)
        for vb in range(2):
            for dc in range(DC):
                slot = dc % 2
                p.dma("sync", lambda e, vb=vb, dc=dc, l=l, slot=slot: e.dma_start(
                    out=B_(BB_WROT + slot * 1024, 512),
                    in_=wbf["Wqkv"][l][dc * 128:(dc + 1) * 128,
                                       2048 + vb * 512:2048 + vb * 512 + 512]),
                    reads=(f"wbf_Wqkv{l}",), writes=RB(BB_WROT + slot * 1024, 512))
                for ts_ in range(4):
                    p.add("tensor", lambda e, vb=vb, dc=dc, ts_=ts_, slot=slot:
                          e.matmul(PS[ts_ * 2 + vb][:, 0:512],
                                   Bp(0, 128, xbf_slice(BB_XBF, dc) + ts_ * 128, 128),
                                   B_(BB_WROT + slot * 1024, 512),
                                   start=(dc == 0), stop=(dc == DC - 1)),
                          reads=RB(BB_WROT + slot * 1024, 512) +
                                RB(BB_XBF + dc * 520, 520),
                          writes=(f"psb{ts_ * 2 + vb}",))
        for ts_ in range(4):
            for vb in range(2):
                p.add("scalar", lambda e, ts_=ts_, vb=vb: e.activation(
                    Bp(0, 128, BB_VCUR + ts_ * 1024 + vb * 512, 512),
                    PS[ts_ * 2 + vb][:, 0:512], AF.Copy),
                    reads=(f"psb{ts_ * 2 + vb}",),
                    writes=RB(BB_VCUR + ts_ * 1024 + vb * 512, 512))
        for wloc in range(2):
            w = 2 * g + wloc
            bb_window(b, l, g, wloc, w)
        p.add("vector", lambda e: e.tensor_copy(
            ab[:, BB_K:BB_K + 2048].rearrange("p (c t) -> p c t", t=256),
            ab[:, BB_KCUR:BB_KCUR + 4096].rearrange(
                "p (c t) -> p c t", t=512)[:, :, 256:512]),
            reads=RB(BB_KCUR, 4096), writes=RB(BB_K, 2048))
        p.add("vector", lambda e: e.tensor_copy(
            B_(BB_V, 2048), B_(BB_VCUR + 2048, 2048)),
            reads=RB(BB_VCUR + 2048, 2048), writes=RB(BB_V, 2048))

    def bb_window(b, l, g, wloc, w):
        t0 = w * 256
        kts = (2, 3) if w == 0 else (0, 1, 2, 3)

        def ksrc(kt):
            if kt < 2:
                if wloc == 0:
                    return BB_K + kt * 128, 256, RB(BB_K, 2048)
                return BB_KCUR + kt * 128, 512, RB(BB_KCUR, 4096)
            tloc = 2 * wloc + (kt - 2) - (0 if wloc == 0 else 2)
            tsub = 2 * wloc + (kt - 2)
            return BB_KCUR + tsub * 128, 512, RB(BB_KCUR, 4096)

        def vsrc(kt):
            if kt < 2:
                if wloc == 0:
                    return BB_V + kt * 1024, RB(BB_V, 2048)
                return BB_VCUR + kt * 1024, RB(BB_VCUR, 4096)
            tsub = 2 * wloc + (kt - 2)
            return BB_VCUR + tsub * 1024, RB(BB_VCUR, 4096)

        def w_scores(h):
            par = h % 2
            ch = h // 2
            sp_lo = PS[0] if par == 0 else PS[2]
            sp_hi = PS[1] if par == 0 else PS[3]
            for kt in kts:
                koff, kstride, kregs = ksrc(kt)
                sp = sp_lo if kt < 2 else sp_hi
                scol = (kt % 2) * 256
                p.add("tensor", lambda e, koff=koff, kstride=kstride, par=par,
                      ch=ch, wloc=wloc, sp=sp, scol=scol, kt=kt: e.matmul(
                          sp[:, scol:scol + 256],
                          ab[par * 64:par * 64 + 64,
                             koff + ch * (kstride if True else 0) * 1:
                             koff + ch * kstride + 128]
                          if False else
                          ab[par * 64:par * 64 + 64,
                             koff + ch * kstride:koff + ch * kstride + 128],
                          ab[par * 64:par * 64 + 64,
                             BB_Q + ch * 512 + wloc * 256:
                             BB_Q + ch * 512 + wloc * 256 + 256],
                          start=True, stop=True),
                      reads=kregs + RB(BB_Q + ch * 512, 512),
                      writes=(f"psb{0 if sp is PS[0] else (1 if sp is PS[1] else (2 if sp is PS[2] else 3))}",))
            for sp, bank, lohi in ((sp_lo, 0 if par == 0 else 2, 0),
                                   (sp_hi, 1 if par == 0 else 3, 1)):
                if w == 0 and lohi == 0:
                    continue
                p.add("vector", lambda e, sp=sp, lohi=lohi: e.tensor_tensor(
                    sp[:, 0:512], sp[:, 0:512],
                    maskT[:, lohi * 512:lohi * 512 + 512], ALU.add),
                    reads=(f"psb{bank}", "maskT"), writes=(f"psb{bank}",))
            pts = BB_PT + par * 1024
            if w == 0:
                p.add("vector", lambda e, pts=pts: e.memset(
                    B_(pts, 512), 0.0), writes=RB(pts, 512))
                p.add("scalar", lambda e, sp=sp_hi, pts=pts,
                      bank=1 if par == 0 else 3: e.activation(
                          B_(pts + 512, 512), sp[:, 0:512], AF.Exp),
                      reads=(f"psb{1 if par == 0 else 3}",),
                      writes=RB(pts + 512, 512))
            else:
                p.add("scalar", lambda e, sp=sp_lo, pts=pts,
                      bank=0 if par == 0 else 2: e.activation(
                          B_(pts, 512), sp[:, 0:512], AF.Exp),
                      reads=(f"psb{0 if par == 0 else 2}",), writes=RB(pts, 512))
                p.add("scalar", lambda e, sp=sp_hi, pts=pts,
                      bank=1 if par == 0 else 3: e.activation(
                          B_(pts + 512, 512), sp[:, 0:512], AF.Exp),
                      reads=(f"psb{1 if par == 0 else 3}",),
                      writes=RB(pts + 512, 512))
        def w_sumspv(h):
            par = h % 2
            ch = h // 2
            pts = BB_PT + par * 1024
            for i, kt in enumerate(kts):
                p.add("tensor", lambda e, kt=kt, par=par, pts=pts,
                      first=(i == 0), last=(i == len(kts) - 1): e.matmul(
                          PS[7][0:1, par * 256:par * 256 + 256], ones_bf.ap(),
                          B_(pts + kt * 256, 256),
                          start=first, stop=last),
                      reads=RB(pts, 1024) + ("ones_bf",), writes=("psb7",))
            for i, kt in enumerate(kts):
                voff, vregs = vsrc(kt)
                p.add("tensor", lambda e, voff=voff, h=h, par=par, pts=pts, kt=kt,
                      first=(i == 0), last=(i == len(kts) - 1): e.matmul(
                          PS[4 + par][par * 64:par * 64 + 64, 0:256],
                          B_(voff + h * 64, 64), B_(pts + kt * 256, 256),
                          start=first, stop=last),
                      reads=vregs + RB(pts, 1024), writes=(f"psb{4 + par}",))
        def w_div(h):
            par = h % 2
            ch = h // 2
            if True:
                p.add("scalar", lambda e: e.activation(
                    Ap(0, 1, SC + 1536, 512), PS[7][0:1, 0:512], AF.Ln),
                    reads=("psb7",), writes=RA(SC + 1536, 512))
                p.add("scalar", lambda e: e.activation(
                    rs_row[0:1, 0:512], Ap(0, 1, SC + 1536, 512), AF.Exp,
                    scale=-1.0),
                    reads=RA(SC + 1536, 512), writes=("rs_row",))
                p.add("tensor", lambda e: e.matmul(
                    PS[6][0:64, 0:256], onesrow_bf[0:1, 0:64],
                    rs_row[0:1, 0:256], start=True, stop=True),
                    reads=("rs_row", "onesrow_bf"), writes=("psb6",))
                p.add("tensor", lambda e: e.matmul(
                    PS[6][64:128, 0:256], onesrow_bf[0:1, 0:64],
                    rs_row[0:1, 256:512], start=True, stop=True),
                    reads=("rs_row", "onesrow_bf"), writes=("psb6",))
                p.add("scalar", lambda e: e.activation(
                    Bp(0, 128, BB_MISC, 256), PS[6][:, 0:256], AF.Copy),
                    reads=("psb6",), writes=RB(BB_MISC, 256))
                for pp in range(2):
                    p.add("vector", lambda e, pp=pp, ch=ch: e.tensor_mul(
                        Bp(pp * 64, 64, BB_AFM + ch * 256, 256),
                        PS[4 + pp][pp * 64:pp * 64 + 64, 0:256],
                        Bp(pp * 64, 64, BB_MISC, 256)),
                        reads=(f"psb{4 + pp}",) + RB(BB_MISC, 256),
                        writes=RB(BB_AFM + ch * 256, 256))

        for h in range(HEADS):
            w_scores(h)
            if h > 0:
                w_sumspv(h - 1)
                if h % 2 == 0 and h >= 2:
                    w_div(h - 1)
        w_sumspv(HEADS - 1)
        w_div(HEADS - 1)
        for k in range(DC):
            slot = k % 2
            p.dma("sync", lambda e, k=k, l=l, slot=slot: e.dma_start(
                out=B_(BB_WOROT + slot * 1024, 1024),
                in_=wbf["Wo"][l][k * 128:(k + 1) * 128, :]),
                reads=(f"wbf_Wo{l}",), writes=RB(BB_WOROT + slot * 1024, 1024))
            for o in range(DC):
                p.add("tensor", lambda e, k=k, o=o, slot=slot: e.matmul(
                    PS[o][:, 256:512],
                    B_(BB_WOROT + slot * 1024 + o * 128, 128),
                    Bp(0, 128, BB_AFM + k * 256, 256),
                    start=(k == 0), stop=(k == DC - 1)),
                    reads=RB(BB_WOROT + slot * 1024, 1024) +
                          RB(BB_AFM + k * 256, 256),
                    writes=(f"psb{o}",))
        for o in range(DC):
            p.add("vector", lambda e, o=o, t0=t0: e.tensor_add(
                A(xb_off(o, t0), 256), A(xb_off(o, t0), 256), PS[o][:, 256:512]),
                reads=(f"psb{o}",) + RA(xb_off(o, t0), 256),
                writes=RA(xb_off(o, t0), 256))

    def bb_ffn_tt(b, l, tt):
        t0 = tt * 512
        if tt == 0:
            for dc in range(DC):
                p.add("vector", lambda e, dc=dc: e.memset(
                    ab[:, FB_XBF2 + dc * 520:FB_XBF2 + dc * 520 + 1], 0.0),
                    writes=RB(FB_XBF2 + dc * 520, 1))
        else:
            p.add("vector", lambda e: e.tensor_copy(
                ab[:, FB_XBF2:FB_XBF2 + 8 * 520].rearrange(
                    "p (c t) -> p c t", t=520)[:, :, 0],
                ab[:, FB_XBF2:FB_XBF2 + 8 * 520].rearrange(
                    "p (c t) -> p c t", t=520)[:, :, 512]),
                reads=RB(FB_XBF2, 8 * 520), writes=RB(FB_XBF2, 8 * 520))
        for dc in range(DC):
            p.add("vector", lambda e, dc=dc, t0=t0: e.tensor_copy(
                Bp(0, 128, FB_XBF2 + dc * 520 + 1, 512), A(xb_off(dc, t0), 512)),
                reads=RA(xb_off(dc, t0), 512),
                writes=RB(FB_XBF2 + dc * 520 + 1, 512))
        for dc in range(DC):
            p.add("scalar", lambda e, dc=dc: e.activation(
                Bp(0, 128, FB_M + dc * 512, 512),
                Bp(0, 128, xbf_slice(FB_XBF2, dc), 512), AF.Square),
                reads=RB(FB_XBF2 + dc * 520, 520), writes=RB(FB_M + dc * 512, 512))
        for dc in range(DC):
            p.add("tensor", lambda e, dc=dc: e.matmul(
                PS[5][0:1, 0:512], ones_bf.ap(),
                Bp(0, 128, xbf_slice(FB_XBF2, dc), 512),
                start=(dc == 0), stop=(dc == DC - 1)),
                reads=RB(FB_XBF2 + dc * 520, 520) + ("ones_bf",), writes=("psb5",))
            p.add("tensor", lambda e, dc=dc: e.matmul(
                PS[6][0:1, 0:512], ones_bf.ap(), Bp(0, 128, FB_M + dc * 512, 512),
                start=(dc == 0), stop=(dc == DC - 1)),
                reads=RB(FB_M + dc * 512, 512) + ("ones_bf",), writes=("psb6",))
        ln_rows_512()
        p.add("scalar", lambda e: e.activation(
            murs_row[0:1, 0:512], Ap(0, 1, SC, 512), AF.Copy),
            reads=RA(SC, 512), writes=("mursrow0",))
        p.add("scalar", lambda e: e.activation(
            rs_row[0:1, 0:512], Ap(0, 1, SC + 1024, 512), AF.Copy),
            reads=RA(SC + 1024, 512), writes=("rs_row",))
        p.add("tensor", lambda e: e.matmul(
            PS[5][:, 0:512], onesrow_bf[0:1, :], rs_row[0:1, 0:512],
            start=True, stop=True),
            reads=("rs_row", "onesrow_bf"), writes=("psb5",))
        for dc in range(DC):
            p.add("tensor", lambda e, dc=dc: e.matmul(
                PS[6][:, 0:512], lrows2_bf[0:2, dc * 128:(dc + 1) * 128],
                murs_row[0:2, 0:512], start=True, stop=True),
                reads=("mursrow0", "mursrow1", "lrows2_bf"), writes=("psb6",))
            p.add("vector", lambda e, dc=dc: e.tensor_mul(
                Ap(0, 128, BF_QN, 512), Bp(0, 128, xbf_slice(FB_XBF2, dc), 512),
                PS[5][:, 0:512]),
                reads=("psb5",) + RB(FB_XBF2 + dc * 520, 520),
                writes=RA(BF_QN, 512))
            p.add("vector", lambda e, dc=dc: e.scalar_tensor_tensor(
                Bp(0, 128, FB_XN + dc * 512, 512), Ap(0, 128, BF_QN, 512),
                lvecs[:, 16 + dc:17 + dc], PS[6][:, 0:512],
                ALU.mult, ALU.subtract),
                reads=RA(BF_QN, 512) + ("psb6", "lvecs"),
                writes=RB(FB_XN + dc * 512, 512))
        def w1_pair(ci):
            slot = ci % 2
            ab_ = ci % 2
            p.dma("sync", lambda e, ci=ci, l=l, slot=slot: e.dma_start(
                out=B_(FB_W1 + slot * 2048, 2048).rearrange(
                    "p (c m) -> p c m", m=256),
                in_=wbf["W1"][l].rearrange("(c p) m -> p c m", p=128)[
                    :, :, 256 * ci:256 * ci + 256]),
                reads=(f"wbf_W1{l}",), writes=RB(FB_W1 + slot * 2048, 2048))
            for dc in range(DC):
                p.add("tensor", lambda e, ci=ci, dc=dc, slot=slot, ab_=ab_:
                      e.matmul(
                          PS[ab_ * 2][:, 0:512],
                          B_(FB_W1 + slot * 2048 + dc * 256, 128),
                          Bp(0, 128, FB_XN + dc * 512, 512),
                          start=(dc == 0), stop=(dc == DC - 1)),
                      reads=RB(FB_W1 + slot * 2048, 2048) +
                            RB(FB_XN + dc * 512, 512),
                      writes=(f"psb{ab_ * 2}",))
                p.add("tensor", lambda e, ci=ci, dc=dc, slot=slot, ab_=ab_:
                      e.matmul(
                          PS[ab_ * 2 + 1][:, 0:512],
                          B_(FB_W1 + slot * 2048 + dc * 256 + 128, 128),
                          Bp(0, 128, FB_XN + dc * 512, 512),
                          start=(dc == 0), stop=(dc == DC - 1)),
                      reads=RB(FB_W1 + slot * 2048, 2048) +
                            RB(FB_XN + dc * 512, 512),
                      writes=(f"psb{ab_ * 2 + 1}",))

        def w1_evac(ci):
            ab_ = ci % 2
            gsl = ci % 2
            p.add("scalar", lambda e, ci=ci, gsl=gsl, ab_=ab_: e.activation(
                Bp(0, 128, FB_GELU + gsl * 512, 512), PS[ab_ * 2 + 1][:, 0:512],
                AF.Gelu, bias=b1ab[:, HC + ci:HC + ci + 1]),
                reads=(f"psb{ab_ * 2 + 1}", "b1ab_g"),
                writes=RB(FB_GELU + gsl * 512, 512))
            p.add("vector", lambda e, ci=ci, gsl=gsl, ab_=ab_:
                  e.scalar_tensor_tensor(
                      Bp(0, 128, FB_M + ci * 512, 512), PS[ab_ * 2][:, 0:512],
                      b1ab[:, ci:ci + 1], Bp(0, 128, FB_GELU + gsl * 512, 512),
                      ALU.add, ALU.mult),
                  reads=(f"psb{ab_ * 2}", "b1ab_a") + RB(FB_GELU + gsl * 512, 512),
                  writes=RB(FB_M + ci * 512, 512))

        def w2_half(ci, orange):
            slot = ci % 2
            p.dma("sync", lambda e, ci=ci, l=l, slot=slot, tag=orange[0]:
                  e.dma_start(
                      out=B_(FB_W2 + slot * 1024, 1024),
                      in_=wbf["W2"][l][ci * 128:(ci + 1) * 128, :]),
                  reads=(f"wbf_W2{l}",),
                  writes=RB(FB_W2 + slot * 1024, 1024), fam=f"w2h{orange[0]}{slot}")
            for o in orange:
                bank = 4 + (o % 4)
                p.add("tensor", lambda e, ci=ci, o=o, slot=slot, bank=bank:
                      e.matmul(
                          PS[bank][:, 0:512],
                          B_(FB_W2 + slot * 1024 + o * 128, 128),
                          Bp(0, 128, FB_M + ci * 512, 512),
                          start=(ci == 0), stop=(ci == HC - 1)),
                      reads=RB(FB_W2 + slot * 1024, 1024) +
                            RB(FB_M + ci * 512, 512),
                      writes=(f"psb{bank}",))

        for ci in range(HC):
            w1_pair(ci)
            w1_evac(ci)
            if ci > 0:
                w2_half(ci - 1, range(0, 4))
        w2_half(HC - 1, range(0, 4))
        for o in range(0, 4):
            p.add("vector", lambda e, o=o, t0=t0: e.scalar_tensor_tensor(
                A(xb_off(o, t0), 512), PS[4 + (o % 4)][:, 0:512],
                lvecs[:, 32 + o:33 + o], A(xb_off(o, t0), 512),
                ALU.add, ALU.add),
                reads=(f"psb{4 + (o % 4)}", "lvecs") + RA(xb_off(o, t0), 512),
                writes=RA(xb_off(o, t0), 512))
        for ci in range(HC):
            w2_half(ci, range(4, 8))
        for o in range(4, 8):
            p.add("vector", lambda e, o=o, t0=t0: e.scalar_tensor_tensor(
                A(xb_off(o, t0), 512), PS[4 + (o % 4)][:, 0:512],
                lvecs[:, 32 + o:33 + o], A(xb_off(o, t0), 512),
                ALU.add, ALU.add),
                reads=(f"psb{4 + (o % 4)}", "lvecs") + RA(xb_off(o, t0), 512),
                writes=RA(xb_off(o, t0), 512))

    def bb_combine(b):
        outs = []
        for tt in range(4):
            t0 = tt * 512
            for dc in range(DC):
                p.dma("sync", lambda e, b=b, dc=dc, t0=t0: e.dma_start(
                    out=Ap(0, 128, BF_QN, 512),
                    in_=x_fm[b, dc * 128:(dc + 1) * 128, t0:t0 + 512]),
                    writes=RA(BF_QN, 512))
                p.dma("scalar", lambda e, b=b, dc=dc, t0=t0: e.dma_start(
                    out=Ap(0, 128, BF_QN + 512, 512),
                    in_=s_scr[dc * 128:(dc + 1) * 128, b * T + t0:b * T + t0 + 512]),
                    reads=tuple(f"sst{b}_{c}_{dc}" for c in range(NMC)),
                    writes=RA(BF_QN + 512, 512))
                p.add("scalar", lambda e, dc=dc, t0=t0: e.activation(
                    Ap(0, 128, BF_QN + 1024, 512), A(xb_off(dc, t0), 512), AF.Silu),
                    reads=RA(xb_off(dc, t0), 512), writes=RA(BF_QN + 1024, 512))
                p.add("vector", lambda e: e.tensor_mul(
                    Ap(0, 128, BF_QN + 1024, 512), Ap(0, 128, BF_QN + 1024, 512),
                    Ap(0, 128, BF_QN + 512, 512)),
                    reads=RA(BF_QN + 512, 512) + RA(BF_QN + 1024, 512),
                    writes=RA(BF_QN + 1024, 512))
                p.add("vector", lambda e: e.tensor_add(
                    Ap(0, 128, BF_QN + 1024, 512), Ap(0, 128, BF_QN + 1024, 512),
                    Ap(0, 128, BF_QN, 512)),
                    reads=RA(BF_QN, 512) + RA(BF_QN + 1024, 512),
                    writes=RA(BF_QN + 1024, 512))
                op = p.dma("gpsimd", lambda e, b=b, dc=dc, t0=t0: e.dma_start(
                    out=out_fm[b, dc * 128:(dc + 1) * 128, t0:t0 + 512],
                    in_=Ap(0, 128, BF_QN + 1024, 512)),
                    reads=RA(BF_QN + 1024, 512), writes=(f"oo{b}_{dc}",))
                outs.append(op)
        return outs

    def bb_out_xb(b):
        outs = []
        for dc in range(DC):
            op = p.dma("gpsimd", lambda e, b=b, dc=dc: e.dma_start(
                out=out_fm[b, dc * 128:(dc + 1) * 128, :],
                in_=A(xb_off(dc, 0), T)),
                reads=RA(xb_off(dc, 0), T), writes=(f"oo{b}_{dc}",))
            outs.append(op)
        return outs

    def phase_branchB(b, milestone_out):
        bb_init(b)
        for l in range(BB_LAYERS):
            bb_layer_setup(l)
            for g in range(4):
                bb_attn_group(b, l, g)
            for tt in range(4):
                bb_ffn_tt(b, l, tt)
        if milestone_out == "xb":
            return bb_out_xb(b)
        return bb_combine(b)

    phase_consts()
    if MILESTONE == "mamba":
        phase_w()
        phase_mamba()
        phase_out_mamba()
    elif MILESTONE == "bb":
        phase_w_bb()
        fin = []
        for b in range(B_PER_CORE):
            fin += phase_branchB(b, "xb")
        p.add("gpsimd", lambda e: e.engine_nop(), deps=fin)
    else:
        phase_w()
        phase_mamba()
        phase_w_bb()
        fin = []
        for b in range(B_PER_CORE):
            fin += phase_branchB(b, "combine")
        p.add("gpsimd", lambda e: e.engine_nop(), deps=fin)

    n_ops = p.finalize_and_emit()
    return nc, n_ops, (dbg_state["map"], dbg_state["mapb"])


def kernel(**inputs):
    consts = _host_consts()
    w = _prep_weights(inputs)
    x = np.asarray(inputs["L0_attn_x"])
    nc, n_ops, dbg_maps = build_program()
    kernel.dbg_map, kernel.dbg_mapb = dbg_maps

    in_maps = []
    for core in range(N_CORES):
        xs = x[core * B_PER_CORE:(core + 1) * B_PER_CORE]
        x_fmh = np.ascontiguousarray(xs.transpose(0, 2, 1))
        m = dict(x_fm=x_fmh,
                 W_in=w["W_in"], W_out=w["W_out"], W_dt=w["W_dt"],
                 W_xdbl=w["W_xdbl"], mvecs=w["mvecs"],
                 Wqkv=w["Wqkv"], Wo=w["Wo"], W1=w["W1"], W2=w["W2"],
                 b1a=w["b1a"], b1g=w["b1g"], lvecs=w["lvecs"],
                 lrows2=w["lrows2"],
                 **consts)
        in_maps.append(m)

    res = run_bass_kernel_spmd(nc, in_maps, core_ids=list(range(N_CORES)))
    outs = []
    for core in range(N_CORES):
        o = np.asarray(res.results[core]["out_fm"])
        outs.append(np.ascontiguousarray(o.transpose(0, 2, 1)))
    kernel.dbg = [np.asarray(res.results[c]["dbg"]) for c in range(N_CORES)]
    kernel.dbgb = [np.asarray(res.results[c]["dbgb"]) for c in range(N_CORES)]
    return np.concatenate(outs, axis=0).astype(np.float32)


# version 05
# speedup vs baseline: 1.1553x; 1.0095x over previous
import zlib
import numpy as np
import ml_dtypes
import concourse.bass as bass
import concourse.mybir as mybir
from concourse.bass_utils import run_bass_kernel_spmd

F32 = mybir.dt.float32
BF16 = mybir.dt.bfloat16
AF = mybir.ActivationFunctionType
ALU = mybir.AluOpType

N_CORES = 8
B_PER_CORE = 2
T = 2048
D = 1024
DC = 8
D_INNER = 2048
HEADS = 16
DH = 64
WIN = 256
NW = 8
HID = 5632
HC = 22
HIDH = 2816
MCH = 512
NMC = 4

MILESTONE = "full"
PARANOID = True
RELAX_SAME = ("tensor",)
DEBUG_DUMPS = []


class _Op:
    __slots__ = ("eng", "fn", "deps", "is_dma", "inc_no", "gidx", "fam")

    def __init__(self, eng, fn, deps, is_dma, gidx, fam=None):
        self.eng, self.fn, self.deps, self.is_dma = eng, fn, deps, is_dma
        self.inc_no = None
        self.gidx = gidx
        self.fam = fam


class Prog:
    ENGINES = ("tensor", "vector", "scalar", "gpsimd", "sync")

    def __init__(self, nc):
        self.nc = nc
        self.ops = {e: [] for e in self.ENGINES}
        self.gcount = 0
        self.regions = {}
        self.last_fam_dma = {}

    def add(self, eng, fn, reads=(), writes=(), deps=(), is_dma=False, fam=None):
        dl = list(deps)
        for r in reads:
            ent = self.regions.get(r)
            if ent is not None and ent[0] is not None:
                dl.append(ent[0])
        for w in writes:
            ent = self.regions.get(w)
            if ent is not None:
                if ent[0] is not None:
                    dl.append(ent[0])
                dl.extend(ent[1])
        if is_dma:
            fam = fam or (writes[0] if writes else "misc")
            nslots = {"sync": 44, "gpsimd": 20, "scalar": 12,
                      "vector": 4, "tensor": 4}[eng]
            fam = f"d{zlib.crc32((eng + ':' + fam).encode()) % nslots}"
            prev = self.last_fam_dma.get((eng, fam))
            if prev is not None:
                dl.append(prev)
        op = _Op(eng, fn, dl, is_dma, self.gcount, fam)
        self.gcount += 1
        self.ops[eng].append(op)
        if is_dma:
            self.last_fam_dma[(eng, fam)] = op
        for r in reads:
            self.regions.setdefault(r, [None, []])[1].append(op)
        for w in writes:
            self.regions[w] = [op, []]
        return op

    def dma(self, eng, fn, reads=(), writes=(), deps=(), fam=None):
        return self.add(eng, fn, reads, writes, deps, is_dma=True, fam=fam)

    def finalize_and_emit(self):
        nc = self.nc
        needed = set()
        for e in self.ENGINES:
            for op in self.ops[e]:
                if op.is_dma:
                    needed.add(id(op))
                for d in op.deps:
                    if d.eng != op.eng or d.is_dma or \
                       (PARANOID and d.eng not in RELAX_SAME):
                        needed.add(id(d))
        sems, counters = {}, {}

        def op_key(op):
            return (op.eng, op.fam) if op.is_dma else (op.eng, None)

        for e in self.ENGINES:
            for op in self.ops[e]:
                if id(op) in needed:
                    key = op_key(op)
                    if key not in sems:
                        sems[key] = nc.alloc_semaphore(
                            f"s_{key[0]}_{key[1] or 'c'}")
                        counters[key] = 0
                    counters[key] += 1
                    op.inc_no = counters[key]
        for e in self.ENGINES:
            for op in self.ops[e]:
                for d in op.deps:
                    if d.gidx >= op.gidx:
                        raise RuntimeError(f"dep ordering violation on {e}")

        def mk(engine_name):
            def _run(eng):
                waited = {}
                for op in self.ops[engine_name]:
                    for d in op.deps:
                        if d.eng == engine_name and not d.is_dma and \
                           ((not PARANOID) or engine_name in RELAX_SAME):
                            continue
                        key = (d.eng, d.fam) if d.is_dma else (d.eng, None)
                        sem = sems[key]
                        val = d.inc_no * (16 if d.is_dma else 1)
                        if waited.get(key, -1) >= val:
                            continue
                        eng.wait_ge(sem, val)
                        waited[key] = val
                    ins = op.fn(eng)
                    if op.inc_no is not None:
                        key = (engine_name, op.fam) if op.is_dma else (engine_name, None)
                        ins.then_inc(sems[key], 16 if op.is_dma else 1)
                if engine_name == "gpsimd":
                    for key, cnt in counters.items():
                        if key[1] is not None:
                            if waited.get(key, -1) < cnt * 16:
                                eng.wait_ge(sems[key], cnt * 16)
            return _run

        with self.nc.Block() as block:
            block.tensor(mk("tensor"))
            block.vector(mk("vector"))
            block.scalar(mk("scalar"))
            block.gpsimd(mk("gpsimd"))
            block.sync(mk("sync"))
        return sum(len(v) for v in self.ops.values())


def _host_consts():
    c = {}
    half = np.arange(0, DH, 2, dtype=np.float64)
    inv_freq = 1.0 / 10000.0 ** (half / DH)
    t = np.arange(T, dtype=np.float64)
    freqs = np.concatenate([t[:, None] * inv_freq[None, :]] * 2, -1)
    base_scale = (half + 0.4 * DH) / (1.4 * DH)
    power = (t - T // 2) / (WIN // 2)
    scale = np.concatenate([base_scale[None, :] ** power[:, None]] * 2, -1)
    cos, sin = np.cos(freqs), np.sin(freqs)
    inv = 1.0 / scale
    for name, tab in (("cs_q", cos * scale), ("ss_q", sin * scale),
                      ("cs_k", cos * inv), ("ss_k", sin * inv)):
        fm = np.ascontiguousarray(tab.T)
        fm = np.concatenate([fm, fm], 0)
        c[name] = fm.astype(ml_dtypes.bfloat16)
    jl = np.arange(512)[:, None]
    il = np.arange(256)[None, :]
    allowed = (jl - il >= 0) & (jl - il <= WIN)
    mask = np.where(allowed, 0.0, -1e30).astype(np.float32)
    c["maskT"] = np.ascontiguousarray(
        mask.reshape(4, 128, 256).transpose(1, 0, 2)).astype(ml_dtypes.bfloat16)
    h2 = np.zeros((128, 2), np.float32)
    h2[0:64, 0] = 1.0
    h2[64:128, 1] = 1.0
    c["h2"] = h2.astype(ml_dtypes.bfloat16)
    R = np.zeros((128, 128), np.float32)
    for h in range(2):
        o = h * 64
        for j in range(32):
            R[o + j + 32, o + j] = -1.0
        for j in range(32, 64):
            R[o + j - 32, o + j] = 1.0
    c["rperm"] = R.astype(ml_dtypes.bfloat16)
    c["ones_bf"] = np.ones((128, 1), np.float32).astype(ml_dtypes.bfloat16)
    c["ones_f"] = np.ones((128, 1), np.float32)
    c["onesrow_f"] = np.ones((1, 128), np.float32)
    c["onesrow_bf"] = np.ones((1, 128), np.float32).astype(ml_dtypes.bfloat16)
    c["ident_bf"] = np.eye(128, dtype=np.float32).astype(ml_dtypes.bfloat16)
    h2t = np.zeros((2, 128), np.float32)
    h2t[0, 0:64] = 1.0
    h2t[1, 64:128] = 1.0
    c["h2t"] = h2t.astype(ml_dtypes.bfloat16)
    return c


def _pp(vec):
    return np.ascontiguousarray(np.asarray(vec).reshape(DC, 128).T)


def _prep_weights(iw):
    w = {}
    w["W_in"] = np.asarray(iw["L0_attn_W_in"])
    w["W_out"] = np.asarray(iw["L0_attn_W_out"])
    w["W_dt"] = np.asarray(iw["L0_attn_W_dt"])
    xd = np.asarray(iw["L0_attn_W_xdbl"])
    xdn = np.zeros((D, 128), np.float32)
    xdn[:, 0:16] = xd[:, 64:80]
    xdn[:, 32:48] = xd[:, 80:96]
    xdn[:, 64:128] = xd[:, 0:64]
    w["W_xdbl"] = xdn
    cxw = np.asarray(iw["L0_attn_conv_x_w"])[:, 0, :]
    czw = np.asarray(iw["L0_attn_conv_z_w"])[:, 0, :]
    cxw2 = np.stack([_pp(cxw[k]) for k in range(4)], axis=2)
    czw2 = np.stack([_pp(czw[k]) for k in range(4)], axis=2)
    mv = np.zeros((128, 112), np.float32)
    mv[:, 0:32] = cxw2.reshape(128, 32)
    mv[:, 32:64] = czw2.reshape(128, 32)
    mv[:, 64:72] = _pp(iw["L0_attn_conv_x_b"])
    mv[:, 72:80] = _pp(iw["L0_attn_conv_z_b"])
    mv[:, 80:88] = _pp(iw["L0_attn_inv_dt"])
    mv[:, 96:104] = _pp(iw["L0_attn_D"])
    mv[:, 104:112] = _pp(iw["L0_attn_b_out"])
    w["mvecs"] = mv
    w["Wqkv"] = np.asarray(iw["L0_ffn_Wqkv"])
    w["Wo"] = np.asarray(iw["L0_ffn_Wo"])
    W1 = np.asarray(iw["L0_ffn_W1"])
    b1 = np.asarray(iw["L0_ffn_b1"])
    W1n = np.zeros((6, D, HID), np.float32)
    b1n = np.zeros((6, HID), np.float32)
    for ci in range(HC):
        lo, hi = 128 * ci, min(128 * ci + 128, 2730)
        if lo < 2730:
            n = hi - lo
            W1n[:, :, 256 * ci:256 * ci + n] = W1[:, :, lo:hi]
            W1n[:, :, 256 * ci + 128:256 * ci + 128 + n] = W1[:, :, 2730 + lo:2730 + hi]
            b1n[:, 256 * ci:256 * ci + n] = b1[:, lo:hi]
            b1n[:, 256 * ci + 128:256 * ci + 128 + n] = b1[:, 2730 + lo:2730 + hi]
    w["W1"] = W1n
    b1r = b1n.reshape(6, HC, 2, 128)
    w["b1a"] = np.ascontiguousarray(b1r[:, :, 0, :].transpose(0, 2, 1))
    w["b1g"] = np.ascontiguousarray(b1r[:, :, 1, :].transpose(0, 2, 1))
    W2 = np.asarray(iw["L0_ffn_W2"])
    W2n = np.zeros((6, HIDH, D), np.float32)
    W2n[:, 0:2730, :] = W2
    w["W2"] = W2n
    qs = np.tile(np.asarray(iw["L0_ffn_q_scale"])[:, None, :], (1, HEADS, 1)).reshape(6, D)
    ks = np.tile(np.asarray(iw["L0_ffn_k_scale"])[:, None, :], (1, HEADS, 1)).reshape(6, D)
    lv = np.zeros((6, 128, 48), np.float32)
    for l in range(6):
        lv[l, :, 0:8] = _pp(qs[l])
        lv[l, :, 8:16] = _pp(ks[l])
        lv[l, :, 16:24] = _pp(np.asarray(iw["L0_ffn_ln_g"])[l])
        lv[l, :, 24:32] = _pp(np.asarray(iw["L0_ffn_ln_b"])[l])
        lv[l, :, 32:40] = _pp(np.asarray(iw["L0_ffn_b2"])[l])
    w["lvecs"] = lv
    lr = np.zeros((6, 2, D), np.float32)
    lr[:, 0, :] = np.asarray(iw["L0_ffn_ln_g"])
    lr[:, 1, :] = np.asarray(iw["L0_ffn_ln_b"])
    w["lrows2"] = lr
    return w


MF_XF = 0
MF_PALL = 0
MF_PRE = 4160
MF_ZPRE = 8320
MF_XS = 12480
MF_RES = 16576
MF_WALL = 20672
MF_W = 22880
MF_P = 23392
MF_BREP = 24768
MF_CREP = 25280
MF_STATE = 25792
MF_BC = 25920
MF_STASH = 26432
AF_COLS = 26816
BF_XB = 0
BF_MISC = 16512

MB_WROT = 0
MB_WDT = 2048
MB_WXD = 3072
MB_XN = 4096
MB_ZBF = 8256
MB_XSBF = 12416
MB_DTL = 13440
MB_YBF = 13952
BB_WROT = 0
BB_WOROT = 2048
BB_XBF = 4096
BB_Q = 8256
BB_K = 12352
BB_KCUR = BB_K + 2048
BB_V = 18496
BB_VCUR = BB_V + 2048
BB_PT = 24640
BB_AFM = 26688
BB_MISC = 28736
FB_W1 = 0
FB_W2 = 4096
FB_M = 6144
FB_GELU = 17408
FB_XBF2 = 18432
FB_XN = 22592
AB_COLS = 37952
AB_TAB = AB_COLS - 4 * T
BF_QN = 16512



def build_program():
    nc = bass.Bass("TRN2", target_bir_lowering=False)
    p = Prog(nc)

    def dtin(name, shape, dtype=F32):
        return nc.dram_tensor(name, list(shape), dtype, kind="ExternalInput").ap()

    x_fm = dtin("x_fm", [B_PER_CORE, D, T])
    W_in_d = dtin("W_in", [D, D_INNER])
    W_out_d = dtin("W_out", [D_INNER, D])
    W_dt_d = dtin("W_dt", [64, D])
    W_xdbl_d = dtin("W_xdbl", [D, 128])
    mvecs_d = dtin("mvecs", [128, 112])
    Wqkv_d = dtin("Wqkv", [6, D, 3 * D])
    Wo_d = dtin("Wo", [6, D, D])
    W1_d = dtin("W1", [6, D, HID])
    W2_d = dtin("W2", [6, HIDH, D])
    b1a_d = dtin("b1a", [6, 128, HC])
    b1g_d = dtin("b1g", [6, 128, HC])
    lvecs_d = dtin("lvecs", [6, 128, 48])
    tabs_d = {nm: dtin(nm, [128, T], BF16) for nm in ("cs_q", "ss_q", "cs_k", "ss_k")}
    maskT_d = dtin("maskT", [128, 4, WIN], BF16)
    h2_d = dtin("h2", [128, 2], BF16)
    rperm_d = dtin("rperm", [128, 128], BF16)
    ones_bf_d = dtin("ones_bf", [128, 1], BF16)
    ones_f_d = dtin("ones_f", [128, 1])
    onesrow_f_d = dtin("onesrow_f", [1, 128])
    onesrow_bf_d = dtin("onesrow_bf", [1, 128], BF16)
    ident_bf_d = dtin("ident_bf", [128, 128], BF16)
    h2t_d = dtin("h2t", [2, 128], BF16)
    lrows2_d = dtin("lrows2", [6, 2, D])

    out_fm = nc.dram_tensor("out_fm", [B_PER_CORE, D, T], F32,
                            kind="ExternalOutput").ap()
    s_scr = nc.dram_tensor("s_scr", [D, B_PER_CORE * T], F32, kind="Internal").ap()
    bc_scr = nc.dram_tensor("bc_scr", [2, 48, MCH], F32, kind="Internal").ap()
    wbf = {}
    for name, shape in (("W_in", [D, D_INNER]), ("W_out", [D_INNER, D]),
                        ("W_dt", [64, D]), ("W_xdbl", [D, 128]),
                        ("Wqkv", [6, D, 3 * D]), ("Wo", [6, D, D]),
                        ("W1", [6, D, HID]), ("W2", [6, HIDH, D])):
        wbf[name] = nc.dram_tensor("bf_" + name, list(shape), BF16,
                                   kind="Internal").ap()

    dbg_d = nc.dram_tensor("dbg", [128, 65536], F32, kind="ExternalOutput").ap()
    dbg_b_d = nc.dram_tensor("dbgb", [128, 16384], BF16, kind="ExternalOutput").ap()
    dbg_state = {"off": 0, "map": {}, "offb": 0, "mapb": {}}

    af = nc.alloc_sbuf_tensor("af", [128, AF_COLS], F32)
    ab = nc.alloc_sbuf_tensor("ab", [128, AB_COLS], BF16)
    maskT = nc.alloc_sbuf_tensor("sb_maskT", [128, 4 * WIN], BF16)
    h2 = nc.alloc_sbuf_tensor("sb_h2", [128, 2], BF16)
    rperm = nc.alloc_sbuf_tensor("sb_rperm", [128, 128], BF16)
    ones_bf = nc.alloc_sbuf_tensor("sb_ones_bf", [128, 1], BF16)
    ones_f = nc.alloc_sbuf_tensor("sb_ones_f", [128, 1], F32)
    onesrow_f = nc.alloc_sbuf_tensor("sb_onesrow_f", [1, 128], F32)
    onesrow_bf = nc.alloc_sbuf_tensor("sb_onesrow_bf", [1, 128], BF16)
    ident_bf = nc.alloc_sbuf_tensor("sb_ident_bf", [128, 128], BF16)
    mvecs = nc.alloc_sbuf_tensor("sb_mvecs", [128, 112], F32)
    lvecs = nc.alloc_sbuf_tensor("sb_lvecs", [128, 48], F32)
    b1ab = nc.alloc_sbuf_tensor("sb_b1ab", [128, 2 * HC], F32)
    h2t = nc.alloc_sbuf_tensor("sb_h2t", [2, 128], BF16)
    lrows2 = nc.alloc_sbuf_tensor("sb_lrows2", [2, D], F32)
    lrows2_bf = nc.alloc_sbuf_tensor("sb_lrows2_bf", [2, D], BF16)
    rq_bf = nc.alloc_sbuf_tensor("sb_rq_bf", [128, 128], BF16)
    rk_bf = nc.alloc_sbuf_tensor("sb_rk_bf", [128, 128], BF16)
    murs_row = nc.alloc_sbuf_tensor("sb_murs_row", [2, 512], BF16)
    rs_row = nc.alloc_sbuf_tensor("sb_rs_row", [1, 512], BF16)
    qkrs = nc.alloc_sbuf_tensor("sb_qkrs", [2, 1024], BF16)
    small = nc.alloc_sbuf_tensor("sb_small", [128, 8], F32)
    wstage = [nc.alloc_sbuf_tensor(f"wstage{i}", [128, 1024], F32) for i in range(2)]
    castb = [nc.alloc_sbuf_tensor(f"castb{i}", [128, 1024], BF16) for i in range(2)]
    PS = [nc.alloc_psum_tensor(f"ps{i}", [128, 512], F32) for i in range(8)]

    def dbg(tag, ap_fn, cols, reads, rows=128):
        if not DEBUG_DUMPS or tag not in DEBUG_DUMPS:
            return
        off = dbg_state["off"]
        assert off + cols <= 65536, "dbg overflow"
        dbg_state["map"][tag] = (off, cols)
        dbg_state["off"] = off + cols
        p.dma("gpsimd", lambda e, off=off, cols=cols, rows=rows: e.dma_start(
            out=dbg_d[0:rows, off:off + cols], in_=ap_fn()),
            reads=reads, writes=(f"dbg_{tag}",))

    def dbgb(tag, ap_fn, cols, reads, rows=128):
        if not DEBUG_DUMPS or tag not in DEBUG_DUMPS:
            return
        off = dbg_state["offb"]
        assert off + cols <= 16384, "dbgb overflow"
        dbg_state["mapb"][tag] = (off, cols)
        dbg_state["offb"] = off + cols
        p.dma("gpsimd", lambda e, off=off, cols=cols, rows=rows: e.dma_start(
            out=dbg_b_d[0:rows, off:off + cols], in_=ap_fn()),
            reads=reads, writes=(f"dbgb_{tag}",))

    def A(off, length):
        return af[:, off:off + length]

    def Ap(p0, np_, off, length):
        return af[p0:p0 + np_, off:off + length]

    def B_(off, length):
        return ab[:, off:off + length]

    def Bp(p0, np_, off, length):
        return ab[p0:p0 + np_, off:off + length]

    def RA(off, length):
        return tuple(f"af{i}" for i in range(off // 512, (off + length + 511) // 512))

    def RB(off, length):
        return tuple(f"ab{i}" for i in range(off // 512, (off + length + 511) // 512))

    cast_ctr = [0]

    def cast_weight(src_ap, dst_ap, rows, cols, rkey):
        nstr = (rows + 127) // 128
        for si in range(nstr):
            r0 = si * 128
            nr = min(128, rows - r0)
            coff = 0
            while coff < cols:
                cw = min(1024, cols - coff)
                sl = cast_ctr[0] % 2
                cast_ctr[0] += 1
                p.dma("sync",
                      lambda e, s=src_ap, r0=r0, nr=nr, co=coff, cw=cw, sl=sl:
                      e.dma_start(out=wstage[sl][0:nr, 0:cw],
                                  in_=s[r0:r0 + nr, co:co + cw]),
                      writes=(f"wstage{sl}",))
                p.add("scalar",
                      lambda e, nr=nr, cw=cw, sl=sl:
                      e.activation(castb[sl][0:nr, 0:cw], wstage[sl][0:nr, 0:cw],
                                   AF.Copy),
                      reads=(f"wstage{sl}",), writes=(f"castb{sl}",))
                p.dma("gpsimd",
                      lambda e, d=dst_ap, r0=r0, nr=nr, co=coff, cw=cw, sl=sl:
                      e.dma_start(out=d[r0:r0 + nr, co:co + cw],
                                  in_=castb[sl][0:nr, 0:cw]),
                      reads=(f"castb{sl}",), writes=(rkey,))
                coff += cw

    def phase_w():
        cast_weight(W_in_d, wbf["W_in"], D, D_INNER, "wbf_W_in")
        cast_weight(W_out_d, wbf["W_out"], D_INNER, D, "wbf_W_out")
        cast_weight(W_dt_d, wbf["W_dt"], 64, D, "wbf_W_dt")
        cast_weight(W_xdbl_d, wbf["W_xdbl"], D, 128, "wbf_W_xdbl")
        if MILESTONE != "mamba":
            for l in range(6):
                cast_weight(Wqkv_d[l], wbf["Wqkv"][l], D, 3 * D, f"wbf_Wqkv{l}")
                cast_weight(Wo_d[l], wbf["Wo"][l], D, D, f"wbf_Wo{l}")
                cast_weight(W1_d[l], wbf["W1"][l], D, HID, f"wbf_W1{l}")
                cast_weight(W2_d[l], wbf["W2"][l], HIDH, D, f"wbf_W2{l}")

    def phase_consts():
        p.dma("sync", lambda e: e.dma_start(
            out=maskT[:, :], in_=maskT_d.rearrange("p k w -> p (k w)")),
            writes=("maskT",))
        for t_, nm in ((h2, "h2"), (rperm, "rperm"), (ones_bf, "ones_bf"),
                       (h2t, "h2t"),
                       (ones_f, "ones_f"), (ident_bf, "ident_bf"),
                       (mvecs, "mvecs"), (onesrow_f, "onesrow_f"),
                       (onesrow_bf, "onesrow_bf")):
            src = {"h2": h2_d, "rperm": rperm_d, "ones_bf": ones_bf_d,
                   "ones_f": ones_f_d, "ident_bf": ident_bf_d,
                   "mvecs": mvecs_d, "onesrow_f": onesrow_f_d,
                   "onesrow_bf": onesrow_bf_d, "h2t": h2t_d}[nm]
            p.dma("sync", lambda e, t_=t_, src=src: e.dma_start(
                out=t_[:, :], in_=src), writes=(nm,))
        p.add("vector", lambda e: e.tensor_scalar(
            mvecs[:, 88:96], mvecs[:, 80:88], 2.0, None, ALU.mult),
            reads=("mvecs",), writes=("mvecs",))
        p.add("vector", lambda e: e.memset(small[:, 0:1], 1e-5), writes=("eps",))
        p.add("vector", lambda e: e.memset(small[:, 2:3], 1.0), writes=("one1",))
        p.add("vector", lambda e: e.memset(murs_row[0:2, :], -1.0),
              writes=("mursrow1", "mursrow0"))

    SC = MF_WALL + 128
    STASH = MF_STASH

    def m_ln_chunk(b, c):
        t0 = c * MCH
        hal = 4 if c < NMC - 1 else 0
        nt = MCH + hal
        for dc in range(DC):
            p.dma("sync", lambda e, b=b, dc=dc, t0=t0, nt=nt:
                  e.dma_start(out=A(MF_XF + dc * 520, nt),
                              in_=x_fm[b, dc * 128:(dc + 1) * 128, t0:t0 + nt]),
                  writes=RA(MF_XF + dc * 520, nt))
        for dc in range(DC):
            p.add("vector", lambda e, dc=dc, nt=nt:
                  e.tensor_copy(Bp(0, 128, MB_XN + dc * 520, nt),
                                A(MF_XF + dc * 520, nt)),
                  reads=RA(MF_XF + dc * 520, nt),
                  writes=RB(MB_XN + dc * 520, nt))
        for dc in range(DC):
            p.add("scalar", lambda e, dc=dc, nt=nt:
                  e.activation(Bp(0, 128, MB_ZBF + dc * 520, nt),
                               A(MF_XF + dc * 520, nt), AF.Square),
                  reads=RA(MF_XF + dc * 520, nt),
                  writes=RB(MB_ZBF + dc * 520, nt))
        for dc in range(DC):
            p.add("tensor", lambda e, dc=dc: e.matmul(
                PS[5][0:1, 0:512], ones_bf.ap(), Bp(0, 128, MB_XN + dc * 520, 512),
                start=(dc == 0), stop=(dc == DC - 1)),
                reads=RB(MB_XN + dc * 520, 512) + ("ones_bf",), writes=("psb5",))
        for dc in range(DC):
            p.add("tensor", lambda e, dc=dc: e.matmul(
                PS[6][0:1, 0:512], ones_bf.ap(), Bp(0, 128, MB_ZBF + dc * 520, 512),
                start=(dc == 0), stop=(dc == DC - 1)),
                reads=RB(MB_ZBF + dc * 520, 512) + ("ones_bf",), writes=("psb6",))
        if hal:
            for dc in range(DC):
                p.add("tensor", lambda e, dc=dc, hal=hal: e.matmul(
                    PS[7][0:1, 0:hal], ones_bf.ap(),
                    Bp(0, 128, MB_XN + dc * 520 + 512, hal),
                    start=(dc == 0), stop=(dc == DC - 1)),
                    reads=RB(MB_XN + dc * 520 + 512, hal) + ("ones_bf",),
                    writes=("psb7",))
            for dc in range(DC):
                p.add("tensor", lambda e, dc=dc, hal=hal: e.matmul(
                    PS[7][0:1, 8:8 + hal], ones_bf.ap(),
                    Bp(0, 128, MB_ZBF + dc * 520 + 512, hal),
                    start=(dc == 0), stop=(dc == DC - 1)),
                    reads=RB(MB_ZBF + dc * 520 + 512, hal) + ("ones_bf",),
                    writes=("psb7",))
        p.add("vector", lambda e: e.tensor_scalar(
            Ap(0, 1, SC, 512), PS[5][0:1, 0:512], 1.0 / D, None, ALU.mult),
            reads=("psb5",), writes=RA(SC, 512))
        p.add("scalar", lambda e: e.activation(
            Ap(0, 1, SC + 512, 512), Ap(0, 1, SC, 512), AF.Square),
            reads=RA(SC, 512), writes=RA(SC + 512, 512))
        p.add("vector", lambda e: e.scalar_tensor_tensor(
            Ap(0, 1, SC + 1024, 512), PS[6][0:1, 0:512], 1.0 / D,
            Ap(0, 1, SC + 512, 512), ALU.mult, ALU.subtract),
            reads=("psb6",) + RA(SC + 512, 512), writes=RA(SC + 1024, 512))
        p.add("scalar", lambda e: e.activation(
            Ap(0, 1, SC + 1536, 512), Ap(0, 1, SC + 1024, 512), AF.Ln,
            bias=small[0:1, 0:1]),
            reads=RA(SC + 1024, 512) + ("eps",), writes=RA(SC + 1536, 512))
        p.add("scalar", lambda e: e.activation(
            Ap(0, 1, SC + 1024, 512), Ap(0, 1, SC + 1536, 512), AF.Exp,
            scale=-0.5),
            reads=RA(SC + 1536, 512), writes=RA(SC + 1024, 512))
        p.add("vector", lambda e: e.tensor_mul(
            Ap(0, 1, SC, 512), Ap(0, 1, SC, 512), Ap(0, 1, SC + 1024, 512)),
            reads=RA(SC, 512) + RA(SC + 1024, 512), writes=RA(SC, 512))
        if hal:
            p.add("vector", lambda e, hal=hal: e.tensor_scalar(
                Ap(0, 1, SC + 2048, hal), PS[7][0:1, 0:hal], 1.0 / D, None,
                ALU.mult), reads=("psb7",), writes=RA(SC + 2048, hal))
            p.add("scalar", lambda e, hal=hal: e.activation(
                Ap(0, 1, SC + 2056, hal), Ap(0, 1, SC + 2048, hal), AF.Square),
                reads=RA(SC + 2048, hal), writes=RA(SC + 2056, hal))
            p.add("vector", lambda e, hal=hal: e.scalar_tensor_tensor(
                Ap(0, 1, SC + 2064, hal), PS[7][0:1, 8:8 + hal], 1.0 / D,
                Ap(0, 1, SC + 2056, hal), ALU.mult, ALU.subtract),
                reads=("psb7",) + RA(SC + 2056, hal), writes=RA(SC + 2064, hal))
            p.add("scalar", lambda e, hal=hal: e.activation(
                Ap(0, 1, SC + 2072, hal), Ap(0, 1, SC + 2064, hal), AF.Ln,
                bias=small[0:1, 0:1]),
                reads=RA(SC + 2064, hal) + ("eps",), writes=RA(SC + 2072, hal))
            p.add("scalar", lambda e, hal=hal: e.activation(
                Ap(0, 1, SC + 2064, hal), Ap(0, 1, SC + 2072, hal), AF.Exp,
                scale=-0.5),
                reads=RA(SC + 2072, hal), writes=RA(SC + 2064, hal))
            p.add("vector", lambda e, hal=hal: e.tensor_mul(
                Ap(0, 1, SC + 2048, hal), Ap(0, 1, SC + 2048, hal),
                Ap(0, 1, SC + 2064, hal)),
                reads=RA(SC + 2048, hal) + RA(SC + 2064, hal),
                writes=RA(SC + 2048, hal))
        p.add("tensor", lambda e: e.matmul(
            PS[5][:, 0:512], onesrow_f[0:1, :], Ap(0, 1, SC + 1024, 512),
            start=True, stop=True),
            reads=RA(SC + 1024, 512) + ("onesrow_f",), writes=("psb5",))
        p.add("tensor", lambda e: e.matmul(
            PS[6][:, 0:512], onesrow_f[0:1, :], Ap(0, 1, SC, 512),
            start=True, stop=True),
            reads=RA(SC, 512) + ("onesrow_f",), writes=("psb6",))
        if hal:
            p.add("tensor", lambda e, hal=hal: e.matmul(
                PS[7][:, 16:16 + hal], onesrow_f[0:1, :], Ap(0, 1, SC + 2064, hal),
                start=True, stop=True),
                reads=RA(SC + 2064, hal) + ("onesrow_f",), writes=("psb7",))
            p.add("tensor", lambda e, hal=hal: e.matmul(
                PS[7][:, 24:24 + hal], onesrow_f[0:1, :], Ap(0, 1, SC + 2048, hal),
                start=True, stop=True),
                reads=RA(SC + 2048, hal) + ("onesrow_f",), writes=("psb7",))
        for dc in range(DC):
            p.add("vector", lambda e, dc=dc: e.tensor_mul(
                A(MF_XF + dc * 520, 512), A(MF_XF + dc * 520, 512),
                PS[5][:, 0:512]),
                reads=("psb5",) + RA(MF_XF + dc * 520, 512),
                writes=RA(MF_XF + dc * 520, 512))
            p.add("vector", lambda e, dc=dc: e.tensor_tensor(
                Bp(0, 128, MB_XN + dc * 520, 512), A(MF_XF + dc * 520, 512),
                PS[6][:, 0:512], ALU.subtract),
                reads=("psb6",) + RA(MF_XF + dc * 520, 512),
                writes=RB(MB_XN + dc * 520, 512))
            if hal:
                p.add("vector", lambda e, dc=dc, hal=hal: e.tensor_mul(
                    A(MF_XF + dc * 520 + 512, hal),
                    A(MF_XF + dc * 520 + 512, hal), PS[7][:, 16:16 + hal]),
                    reads=("psb7",) + RA(MF_XF + dc * 520 + 512, hal),
                    writes=RA(MF_XF + dc * 520 + 512, hal))
                p.add("vector", lambda e, dc=dc, hal=hal: e.tensor_tensor(
                    Bp(0, 128, MB_XN + dc * 520 + 512, hal),
                    A(MF_XF + dc * 520 + 512, hal),
                    PS[7][:, 24:24 + hal], ALU.subtract),
                    reads=("psb7",) + RA(MF_XF + dc * 520 + 512, hal),
                    writes=RB(MB_XN + dc * 520 + 512, hal))

    def m_xz_chunk(b, c):
        hal = 4 if c < NMC - 1 else 0
        if c == 0:
            for dc in range(DC):
                p.add("vector", lambda e, dc=dc: e.memset(
                    A(MF_PRE + dc * 520, 2), 0.0), writes=RA(MF_PRE + dc * 520, 2))
                p.add("vector", lambda e, dc=dc: e.memset(
                    A(MF_ZPRE + dc * 520, 2), 0.0), writes=RA(MF_ZPRE + dc * 520, 2))
        else:
            for dc in range(DC):
                p.add("vector", lambda e, dc=dc: e.tensor_copy(
                    A(MF_PRE + dc * 520, 2), A(STASH + dc * 2, 2)),
                    reads=RA(STASH + dc * 2, 2), writes=RA(MF_PRE + dc * 520, 2))
                p.add("vector", lambda e, dc=dc: e.tensor_copy(
                    A(MF_ZPRE + dc * 520, 2), A(STASH + 16 + dc * 2, 2)),
                    reads=RA(STASH + 16 + dc * 2, 2),
                    writes=RA(MF_ZPRE + dc * 520, 2))
        for o in range(16):
            slot = o % 2
            pb = o % 2
            p.dma("sync", lambda e, o=o, slot=slot: e.dma_start(
                out=B_(MB_WROT + slot * 1024, 1024).rearrange(
                    "p (c m) -> p c m", m=128),
                in_=wbf["W_in"].rearrange("(c p) m -> p c m", p=128)[
                    :, :, o * 128:(o + 1) * 128]),
                reads=("wbf_W_in",), writes=RB(MB_WROT + slot * 1024, 1024))
            for dc in range(DC):
                p.add("tensor", lambda e, dc=dc, slot=slot, pb=pb: e.matmul(
                    PS[pb][:, 0:512],
                    B_(MB_WROT + slot * 1024 + dc * 128, 128),
                    Bp(0, 128, MB_XN + dc * 520, 512),
                    start=(dc == 0), stop=(dc == DC - 1)),
                    reads=RB(MB_WROT + slot * 1024, 1024) +
                          RB(MB_XN + dc * 520, 512),
                    writes=(f"psb{pb}",))
            if hal:
                for dc in range(DC):
                    p.add("tensor", lambda e, o=o, dc=dc, slot=slot, hal=hal:
                          e.matmul(PS[2][:, o * 4:o * 4 + hal],
                                   B_(MB_WROT + slot * 1024 + dc * 128, 128),
                                   Bp(0, 128, MB_XN + dc * 520 + 512, hal),
                                   start=(dc == 0), stop=(dc == DC - 1)),
                          reads=RB(MB_WROT + slot * 1024, 1024) +
                                RB(MB_XN + dc * 520 + 512, hal),
                          writes=("psb2",))
            base = MF_PRE if o < 8 else MF_ZPRE
            dcc = o % 8
            p.add("vector", lambda e, base=base, dcc=dcc, pb=pb: e.tensor_copy(
                A(base + dcc * 520 + 2, 512), PS[pb][:, 0:512]),
                reads=(f"psb{pb}",), writes=RA(base + dcc * 520 + 2, 512))
            if hal:
                p.add("vector", lambda e, base=base, dcc=dcc, o=o, hal=hal:
                      e.tensor_copy(A(base + dcc * 520 + 514, hal),
                                    PS[2][:, o * 4:o * 4 + hal]),
                      reads=("psb2",), writes=RA(base + dcc * 520 + 514, hal))
            else:
                p.add("vector", lambda e, base=base, dcc=dcc: e.memset(
                    A(base + dcc * 520 + 514, 4), 0.0),
                    writes=RA(base + dcc * 520 + 514, 4))
        for dc in range(DC):
            p.add("vector", lambda e, dc=dc: e.tensor_copy(
                A(STASH + dc * 2, 2), A(MF_PRE + dc * 520 + 512, 2)),
                reads=RA(MF_PRE + dc * 520 + 512, 2),
                writes=RA(STASH + dc * 2, 2))
            p.add("vector", lambda e, dc=dc: e.tensor_copy(
                A(STASH + 16 + dc * 2, 2), A(MF_ZPRE + dc * 520 + 512, 2)),
                reads=RA(MF_ZPRE + dc * 520 + 512, 2),
                writes=RA(STASH + 16 + dc * 2, 2))

    def m_conv_chunk(b, c):
        for dc in range(DC):
            src_ = MF_PRE + dc * 520
            dst = MF_XS + dc * 512
            p.add("vector", lambda e, src_=src_, dst=dst, dc=dc: e.tensor_scalar(
                A(dst, 512), A(src_ + 1, 512), mvecs[:, dc * 4:dc * 4 + 1],
                None, ALU.mult),
                reads=RA(src_, 520) + ("mvecs",), writes=RA(dst, 512))
            for k in range(1, 4):
                p.add("vector", lambda e, src_=src_, dst=dst, dc=dc, k=k:
                      e.scalar_tensor_tensor(
                          A(dst, 512), A(src_ + 1 + k, 512),
                          mvecs[:, dc * 4 + k:dc * 4 + k + 1],
                          A(dst, 512), ALU.mult, ALU.add),
                      reads=RA(src_, 520) + RA(dst, 512) + ("mvecs",),
                      writes=RA(dst, 512))
            p.add("scalar", lambda e, dst=dst, dc=dc: e.activation(
                A(dst, 512), A(dst, 512), AF.Silu, bias=mvecs[:, 64 + dc:65 + dc]),
                reads=RA(dst, 512) + ("mvecs",), writes=RA(dst, 512))
            p.add("vector", lambda e, dst=dst, dc=dc: e.tensor_copy(
                Bp(0, 128, MB_XSBF + (dc % 2) * 512, 512), A(dst, 512)),
                reads=RA(dst, 512), writes=RB(MB_XSBF + (dc % 2) * 512, 512))
            p.add("tensor", lambda e, dc=dc: e.matmul(
                PS[3][:, 0:512], Bp(0, 128, MB_WXD + dc * 128, 128),
                Bp(0, 128, MB_XSBF + (dc % 2) * 512, 512),
                start=(dc == 0), stop=(dc == DC - 1)),
                reads=RB(MB_WXD, 1024) + RB(MB_XSBF + (dc % 2) * 512, 512),
                writes=("psb3",))
        for dc in range(DC):
            src_ = MF_ZPRE + dc * 520
            dst = MF_RES + dc * 512
            p.add("vector", lambda e, src_=src_, dst=dst, dc=dc: e.tensor_scalar(
                A(dst, 512), A(src_ + 1, 512),
                mvecs[:, 32 + dc * 4:32 + dc * 4 + 1], None, ALU.mult),
                reads=RA(src_, 520) + ("mvecs",), writes=RA(dst, 512))
            for k in range(1, 4):
                p.add("vector", lambda e, src_=src_, dst=dst, dc=dc, k=k:
                      e.scalar_tensor_tensor(
                          A(dst, 512), A(src_ + 1 + k, 512),
                          mvecs[:, 32 + dc * 4 + k:32 + dc * 4 + k + 1],
                          A(dst, 512), ALU.mult, ALU.add),
                      reads=RA(src_, 520) + RA(dst, 512) + ("mvecs",),
                      writes=RA(dst, 512))
            p.add("scalar", lambda e, dst=dst, dc=dc: e.activation(
                Bp(0, 128, MB_ZBF + dc * 512, 512), A(dst, 512), AF.Silu,
                bias=mvecs[:, 72 + dc:73 + dc]),
                reads=RA(dst, 512) + ("mvecs",), writes=RB(MB_ZBF + dc * 512, 512))

    def m_xdbl_dt_chunk(b, c):
        p.add("vector", lambda e: e.tensor_copy(
            Ap(0, 48, MF_BC, 512), PS[3][0:48, 0:512]),
            reads=("psb3",), writes=RA(MF_BC, 512))
        p.add("scalar", lambda e: e.activation(
            Bp(0, 64, MB_DTL, 512), PS[3][64:128, 0:512], AF.Copy),
            reads=("psb3",), writes=RB(MB_DTL, 512))
        p.dma("gpsimd", lambda e, c=c: e.dma_start(
            out=bc_scr[c % 2], in_=Ap(0, 48, MF_BC, 512)),
            reads=RA(MF_BC, 512), writes=(f"bcscr{c % 2}",))
        for o in range(DC):
            p.add("tensor", lambda e, o=o: e.matmul(
                PS[4][:, 0:512], Bp(0, 64, MB_WDT + o * 128, 128),
                Bp(0, 64, MB_DTL, 512), start=True, stop=True),
                reads=RB(MB_WDT, 1024) + RB(MB_DTL, 512), writes=("psb4",))
            p.add("vector", lambda e, o=o: e.tensor_scalar(
                A(MF_PRE + o * 512, 512), PS[4][:, 0:512],
                mvecs[:, 88 + o:89 + o], None, ALU.add),
                reads=("psb4", "mvecs"), writes=RA(MF_PRE + o * 512, 512))
            p.add("scalar", lambda e, o=o: e.activation(
                A(MF_W, 512), A(MF_PRE + o * 512, 512), AF.Abs),
                reads=RA(MF_PRE + o * 512, 512), writes=RA(MF_W, 512))
            p.add("scalar", lambda e, o=o: e.activation(
                A(MF_W, 512), A(MF_W, 512), AF.Exp, scale=-1.0),
                reads=RA(MF_W, 512), writes=RA(MF_W, 512))
            p.add("scalar", lambda e, o=o: e.activation(
                A(MF_W, 512), A(MF_W, 512), AF.Ln, bias=small[:, 2:3]),
                reads=RA(MF_W, 512) + ("one1",), writes=RA(MF_W, 512))
            p.add("vector", lambda e, o=o: e.tensor_scalar(
                A(MF_P, 512), A(MF_PRE + o * 512, 512), 0.0, None, ALU.max),
                reads=RA(MF_PRE + o * 512, 512), writes=RA(MF_P, 512))
            p.add("vector", lambda e, o=o: e.tensor_add(
                A(MF_PRE + o * 512, 512), A(MF_P, 512), A(MF_W, 512)),
                reads=RA(MF_P, 512) + RA(MF_W, 512),
                writes=RA(MF_PRE + o * 512, 512))
            p.add("vector", lambda e, o=o: e.tensor_mul(
                A(MF_ZPRE + o * 512, 512), A(MF_PRE + o * 512, 512),
                A(MF_XS + o * 512, 512)),
                reads=RA(MF_PRE + o * 512, 512) + RA(MF_XS + o * 512, 512),
                writes=RA(MF_ZPRE + o * 512, 512))

    def m_scan_chunk(b, c):
        for n in range(16):
            p.dma("scalar", lambda e, c=c, n=n: e.dma_start(
                out=A(MF_BREP, 512),
                in_=bc_scr[c % 2, n:n + 1, :].partition_broadcast(128).squeeze(1)),
                reads=(f"bcscr{c % 2}",), writes=RA(MF_BREP, 512))
            p.dma("scalar", lambda e, c=c, n=n: e.dma_start(
                out=A(MF_CREP, 512),
                in_=bc_scr[c % 2, 32 + n:33 + n, :]
                .partition_broadcast(128).squeeze(1)),
                reads=(f"bcscr{c % 2}",), writes=RA(MF_CREP, 512))

            def brep4(base):
                return A(base, 512).rearrange("p (x t) -> p x t", x=1) \
                    .broadcast_to([128, 4, 512])

            for hf in range(2):
                po = MF_PALL + hf * 2048
                wo = MF_WALL + hf * 2048
                do = MF_PRE + hf * 2048
                uo = MF_ZPRE + hf * 2048
                ro = MF_RES + hf * 2048
                p.add("scalar", lambda e, n=n, po=po, do=do: e.activation(
                    A(po, 2048), A(do, 2048), AF.Exp, scale=-(n + 1.0)),
                    reads=RA(do, 2048), writes=RA(po, 2048))
                p.add("vector", lambda e, wo=wo, uo=uo: e.tensor_tensor(
                    A(wo, 2048).rearrange("p (x t) -> p x t", x=4),
                    A(uo, 2048).rearrange("p (x t) -> p x t", x=4),
                    brep4(MF_BREP), ALU.mult),
                    reads=RA(uo, 2048) + RA(MF_BREP, 512), writes=RA(wo, 2048))
                for i in range(4):
                    dc = hf * 4 + i
                    p.add("vector", lambda e, n=n, dc=dc, po=po, wo=wo, i=i:
                          e.tensor_tensor_scan(
                              A(MF_RES + dc * 512, 512), A(po + i * 512, 512),
                              A(wo + i * 512, 512),
                              af[:, MF_STATE + n * 8 + dc:
                                 MF_STATE + n * 8 + dc + 1],
                              ALU.mult, ALU.add),
                          reads=RA(po + i * 512, 512) + RA(wo + i * 512, 512) +
                                RA(MF_STATE, 128),
                          writes=RA(MF_RES + dc * 512, 512))
                p.add("vector", lambda e, ro=ro, hf=hf: e.tensor_tensor(
                    Bp(0, 128, MB_XN + hf * 2048, 2048).rearrange(
                        "p (x t) -> p x t", x=4),
                    A(ro, 2048).rearrange("p (x t) -> p x t", x=4),
                    brep4(MF_CREP), ALU.mult),
                    reads=RA(ro, 2048) + RA(MF_CREP, 512),
                    writes=RB(MB_XN + hf * 2048, 2048))
                for i in range(4):
                    dc = hf * 4 + i
                    p.add("tensor", lambda e, n=n, dc=dc, hf=hf, i=i: e.matmul(
                        PS[dc][:, 0:512], ident_bf.ap(),
                        Bp(0, 128, MB_XN + hf * 2048 + i * 512, 512),
                        start=(n == 0), stop=False),
                        reads=RB(MB_XN + hf * 2048 + i * 512, 512) + ("ident_bf",),
                        writes=(f"psb{dc}",))
            p.add("vector", lambda e, n=n: e.tensor_copy(
                af[:, MF_STATE + n * 8:MF_STATE + n * 8 + 8],
                af[:, MF_RES:MF_RES + 4096].rearrange(
                    "p (ci t) -> p ci t", t=512)[:, :, 511]),
                reads=RA(MF_RES, 4096),
                writes=RA(MF_STATE, 128))
        for dc in range(DC):
            p.add("vector", lambda e, dc=dc: e.tensor_scalar(
                Bp(0, 128, MB_XSBF + (dc % 2) * 512, 512),
                A(MF_XS + dc * 512, 512),
                mvecs[:, 96 + dc:97 + dc], None, ALU.mult),
                reads=RA(MF_XS + dc * 512, 512) + ("mvecs",),
                writes=RB(MB_XSBF + (dc % 2) * 512, 512))
            p.add("tensor", lambda e, dc=dc: e.matmul(
                PS[dc][:, 0:512], ident_bf.ap(),
                Bp(0, 128, MB_XSBF + (dc % 2) * 512, 512),
                start=False, stop=True),
                reads=RB(MB_XSBF + (dc % 2) * 512, 512) + ("ident_bf",),
                writes=(f"psb{dc}",))
            p.add("scalar", lambda e, dc=dc: e.activation(
                Bp(0, 128, MB_YBF + dc * 512, 512), PS[dc][:, 0:512], AF.Copy),
                reads=(f"psb{dc}",), writes=RB(MB_YBF + dc * 512, 512))

    def m_wout_chunk(b, c):
        t0 = c * MCH
        for k in range(16):
            slot = k % 2
            p.dma("sync", lambda e, k=k, slot=slot: e.dma_start(
                out=B_(MB_WROT + slot * 1024, 1024),
                in_=wbf["W_out"][k * 128:(k + 1) * 128, :]),
                reads=("wbf_W_out",), writes=RB(MB_WROT + slot * 1024, 1024))
            rhs_off = (MB_YBF + k * 512) if k < 8 else (MB_ZBF + (k - 8) * 512)
            for o in range(DC):
                p.add("tensor", lambda e, k=k, o=o, slot=slot, rhs_off=rhs_off:
                      e.matmul(PS[o][:, 0:512],
                               B_(MB_WROT + slot * 1024 + o * 128, 128),
                               Bp(0, 128, rhs_off, 512),
                               start=(k == 0), stop=(k == 15)),
                      reads=RB(MB_WROT + slot * 1024, 1024) + RB(rhs_off, 512),
                      writes=(f"psb{o}",))
        for o in range(DC):
            p.add("vector", lambda e, o=o: e.tensor_scalar(
                A(MF_RES + o * 512, 512), PS[o][:, 0:512],
                mvecs[:, 104 + o:105 + o], None, ALU.add),
                reads=(f"psb{o}", "mvecs"), writes=RA(MF_RES + o * 512, 512))
            p.dma("gpsimd", lambda e, o=o, b=b, t0=t0: e.dma_start(
                out=s_scr[o * 128:(o + 1) * 128, b * T + t0:b * T + t0 + 512],
                in_=A(MF_RES + o * 512, 512)),
                reads=RA(MF_RES + o * 512, 512), writes=(f"sst{b}_{c}_{o}",))

    def phase_mamba():
        p.dma("sync", lambda e: e.dma_start(
            out=Bp(0, 64, MB_WDT, 1024), in_=wbf["W_dt"]),
            reads=("wbf_W_dt",), writes=RB(MB_WDT, 1024))
        for k in range(DC):
            p.dma("sync", lambda e, k=k: e.dma_start(
                out=Bp(0, 128, MB_WXD + k * 128, 128),
                in_=wbf["W_xdbl"][k * 128:(k + 1) * 128, :]),
                reads=("wbf_W_xdbl",), writes=RB(MB_WXD + k * 128, 128),
                fam=f"wxd{k % 2}")
        for b in range(B_PER_CORE):
            p.add("vector", lambda e: e.memset(
                af[:, MF_STATE:MF_STATE + 128], 0.0),
                writes=RA(MF_STATE, 128))
            for c in range(NMC):
                m_ln_chunk(b, c)
                m_xz_chunk(b, c)
                m_conv_chunk(b, c)
                m_xdbl_dt_chunk(b, c)
                m_scan_chunk(b, c)
                m_wout_chunk(b, c)

    def phase_out_mamba():
        outs = []
        for b in range(B_PER_CORE):
            for o in range(DC):
                op = p.dma("gpsimd", lambda e, b=b, o=o: e.dma_start(
                    out=out_fm[b, o * 128:(o + 1) * 128, :],
                    in_=s_scr[o * 128:(o + 1) * 128, b * T:(b + 1) * T]),
                    reads=tuple(f"sst{b}_{c}_{o}" for c in range(NMC)),
                    writes=(f"oo{b}_{o}",))
                outs.append(op)
        p.add("gpsimd", lambda e: e.engine_nop(), deps=outs)



    BB_LAYERS = int(globals().get("BB_NLAYERS", 6))

    def phase_w_bb():
        for l in range(BB_LAYERS):
            cast_weight(Wqkv_d[l], wbf["Wqkv"][l], D, 3 * D, f"wbf_Wqkv{l}")
            cast_weight(Wo_d[l], wbf["Wo"][l], D, D, f"wbf_Wo{l}")
            cast_weight(W1_d[l], wbf["W1"][l], D, HID, f"wbf_W1{l}")
            cast_weight(W2_d[l], wbf["W2"][l], HIDH, D, f"wbf_W2{l}")

    XBO = BF_XB

    def xb_off(dc, t):
        return XBO + dc * 2064 + 1 + t

    def bb_init(b):
        for dc in range(DC):
            p.add("vector", lambda e, dc=dc: e.memset(
                A(XBO + dc * 2064, 1), 0.0), writes=RA(XBO + dc * 2064, 1))
        for tt in range(4):
            t0 = tt * 512
            for dc in range(DC):
                p.dma("sync", lambda e, b=b, dc=dc, t0=t0: e.dma_start(
                    out=A(xb_off(dc, t0), 512),
                    in_=x_fm[b, dc * 128:(dc + 1) * 128, t0:t0 + 512]),
                    writes=RA(xb_off(dc, t0), 512))
            for dc in range(DC):
                p.add("vector", lambda e, dc=dc, t0=t0: e.tensor_copy(
                    Bp(0, 128, FB_M + dc * 512, 512), A(xb_off(dc, t0), 512)),
                    reads=RA(xb_off(dc, t0), 512),
                    writes=RB(FB_M + dc * 512, 512))
                p.add("scalar", lambda e, dc=dc, t0=t0: e.activation(
                    Bp(0, 128, FB_XN + dc * 512, 512), A(xb_off(dc, t0), 512),
                    AF.Square),
                    reads=RA(xb_off(dc, t0), 512),
                    writes=RB(FB_XN + dc * 512, 512))
            for dc in range(DC):
                p.add("tensor", lambda e, dc=dc: e.matmul(
                    PS[5][0:1, 0:512], ones_bf.ap(), Bp(0, 128, FB_M + dc * 512, 512),
                    start=(dc == 0), stop=(dc == DC - 1)),
                    reads=RB(FB_M + dc * 512, 512) + ("ones_bf",), writes=("psb5",))
                p.add("tensor", lambda e, dc=dc: e.matmul(
                    PS[6][0:1, 0:512], ones_bf.ap(), Bp(0, 128, FB_XN + dc * 512, 512),
                    start=(dc == 0), stop=(dc == DC - 1)),
                    reads=RB(FB_XN + dc * 512, 512) + ("ones_bf",), writes=("psb6",))
            ln_rows_512()
            p.add("tensor", lambda e: e.matmul(
                PS[5][:, 0:512], onesrow_f[0:1, :], Ap(0, 1, SC + 1024, 512),
                start=True, stop=True),
                reads=RA(SC + 1024, 512) + ("onesrow_f",), writes=("psb5",))
            p.add("tensor", lambda e: e.matmul(
                PS[6][:, 0:512], onesrow_f[0:1, :], Ap(0, 1, SC, 512),
                start=True, stop=True),
                reads=RA(SC, 512) + ("onesrow_f",), writes=("psb6",))
            for dc in range(DC):
                p.add("vector", lambda e, dc=dc, t0=t0: e.tensor_mul(
                    A(xb_off(dc, t0), 512), A(xb_off(dc, t0), 512), PS[5][:, 0:512]),
                    reads=("psb5",) + RA(xb_off(dc, t0), 512),
                    writes=RA(xb_off(dc, t0), 512))
                p.add("vector", lambda e, dc=dc, t0=t0: e.tensor_tensor(
                    A(xb_off(dc, t0), 512), A(xb_off(dc, t0), 512), PS[6][:, 0:512],
                    ALU.subtract),
                    reads=("psb6",) + RA(xb_off(dc, t0), 512),
                    writes=RA(xb_off(dc, t0), 512))

    def ln_rows_512():
        p.add("vector", lambda e: e.tensor_scalar(
            Ap(0, 1, SC, 512), PS[5][0:1, 0:512], 1.0 / D, None, ALU.mult),
            reads=("psb5",), writes=RA(SC, 512))
        p.add("scalar", lambda e: e.activation(
            Ap(0, 1, SC + 512, 512), Ap(0, 1, SC, 512), AF.Square),
            reads=RA(SC, 512), writes=RA(SC + 512, 512))
        p.add("vector", lambda e: e.scalar_tensor_tensor(
            Ap(0, 1, SC + 1024, 512), PS[6][0:1, 0:512], 1.0 / D,
            Ap(0, 1, SC + 512, 512), ALU.mult, ALU.subtract),
            reads=("psb6",) + RA(SC + 512, 512), writes=RA(SC + 1024, 512))
        p.add("scalar", lambda e: e.activation(
            Ap(0, 1, SC + 1536, 512), Ap(0, 1, SC + 1024, 512), AF.Ln,
            bias=small[0:1, 0:1]),
            reads=RA(SC + 1024, 512) + ("eps",), writes=RA(SC + 1536, 512))
        p.add("scalar", lambda e: e.activation(
            Ap(0, 1, SC + 1024, 512), Ap(0, 1, SC + 1536, 512), AF.Exp,
            scale=-0.5),
            reads=RA(SC + 1536, 512), writes=RA(SC + 1024, 512))
        p.add("vector", lambda e: e.tensor_mul(
            Ap(0, 1, SC, 512), Ap(0, 1, SC, 512), Ap(0, 1, SC + 1024, 512)),
            reads=RA(SC, 512) + RA(SC + 1024, 512), writes=RA(SC, 512))

    def bb_layer_setup(l):
        p.dma("sync", lambda e, l=l: e.dma_start(
            out=lvecs[:, :], in_=lvecs_d[l]), writes=("lvecs",))
        p.dma("sync", lambda e, l=l: e.dma_start(
            out=b1ab[:, 0:HC], in_=b1a_d[l]), writes=("b1ab_a",))
        p.dma("sync", lambda e, l=l: e.dma_start(
            out=b1ab[:, HC:2 * HC], in_=b1g_d[l]), writes=("b1ab_g",))
        p.dma("sync", lambda e, l=l: e.dma_start(
            out=lrows2[:, :], in_=lrows2_d[l]), writes=("lrows2",))
        p.add("vector", lambda e: e.tensor_copy(lrows2_bf[:, :], lrows2[:, :]),
              reads=("lrows2",), writes=("lrows2_bf",))
        for i, nm in enumerate(("cs_q", "ss_q", "cs_k", "ss_k")):
            p.dma("sync", lambda e, nm=nm, i=i: e.dma_start(
                out=ab[:, AB_TAB + i * T:AB_TAB + (i + 1) * T], in_=tabs_d[nm]),
                writes=RB(AB_TAB + i * T, T))
        p.add("vector", lambda e: e.tensor_scalar(
            ab[:, AB_TAB:AB_TAB + T], ab[:, AB_TAB:AB_TAB + T],
            lvecs[:, 0:1], 8.0, ALU.mult, ALU.mult),
            reads=RB(AB_TAB, T) + ("lvecs",), writes=RB(AB_TAB, T))
        p.add("vector", lambda e: e.tensor_scalar(
            ab[:, AB_TAB + 2 * T:AB_TAB + 3 * T], ab[:, AB_TAB + 2 * T:AB_TAB + 3 * T],
            lvecs[:, 8:9], None, ALU.mult),
            reads=RB(AB_TAB + 2 * T, T) + ("lvecs",), writes=RB(AB_TAB + 2 * T, T))
        p.add("vector", lambda e: e.tensor_scalar(
            rq_bf[:, :], rperm[:, :], lvecs[:, 0:1], 8.0, ALU.mult, ALU.mult),
            reads=("rperm", "lvecs"), writes=("rq_bf",))
        p.add("vector", lambda e: e.tensor_scalar(
            rk_bf[:, :], rperm[:, :], lvecs[:, 8:9], None, ALU.mult),
            reads=("rperm", "lvecs"), writes=("rk_bf",))

    def xbf_slice(base, dc, shifted_halo=True):
        off = base + dc * 520 + (0 if dc >= 4 else 1)
        return off

    def bb_attn_group(b, l, g):
        t0 = g * 512
        if g == 0:
            p.add("vector", lambda e: e.memset(
                af[0:1, 0:1] if False else ab[:, BB_XBF:BB_XBF + 1], 0.0),
                writes=RB(BB_XBF, 1))
            for dc in range(DC):
                p.add("vector", lambda e, dc=dc: e.memset(
                    ab[:, BB_XBF + dc * 520:BB_XBF + dc * 520 + 1], 0.0),
                    writes=RB(BB_XBF + dc * 520, 1))
        else:
            p.add("vector", lambda e: e.tensor_copy(
                ab[:, BB_XBF:BB_XBF + 8 * 520].rearrange(
                    "p (c t) -> p c t", t=520)[:, :, 0:1].squeeze(2)
                if False else
                ab[:, BB_XBF:BB_XBF + 8 * 520].rearrange(
                    "p (c t) -> p c t", t=520)[:, :, 0],
                ab[:, BB_XBF:BB_XBF + 8 * 520].rearrange(
                    "p (c t) -> p c t", t=520)[:, :, 512]),
                reads=RB(BB_XBF, 8 * 520), writes=RB(BB_XBF, 8 * 520))
        for dc in range(DC):
            p.add("vector", lambda e, dc=dc, t0=t0: e.tensor_copy(
                Bp(0, 128, BB_XBF + dc * 520 + 1, 512),
                A(xb_off(dc, t0), 512)),
                reads=RA(xb_off(dc, t0), 512),
                writes=RB(BB_XBF + dc * 520 + 1, 512))
        SCR = BB_AFM

        def qk_mms(o):
            slot = o % 2
            pb = o % 2
            p.dma("sync", lambda e, o=o, l=l, slot=slot: e.dma_start(
                out=B_(BB_WROT + slot * 1024, 1024).rearrange(
                    "p (c m) -> p c m", m=128),
                in_=wbf["Wqkv"][l].rearrange("(c p) m -> p c m", p=128)[
                    :, :, o * 128:(o + 1) * 128]),
                reads=(f"wbf_Wqkv{l}",), writes=RB(BB_WROT + slot * 1024, 1024))
            for dc in range(DC):
                p.add("tensor", lambda e, dc=dc, slot=slot, pb=pb: e.matmul(
                    PS[pb][:, 0:512],
                    B_(BB_WROT + slot * 1024 + dc * 128, 128),
                    Bp(0, 128, xbf_slice(BB_XBF, dc), 512),
                    start=(dc == 0), stop=(dc == DC - 1)),
                    reads=RB(BB_WROT + slot * 1024, 1024) +
                          RB(BB_XBF + dc * 520, 520),
                    writes=(f"psb{pb}",))

        def qk_post(o):
            slot = o % 2
            pb = o % 2
            isq = o < 8
            sqo = BB_PT + slot * 1024
            qro = BB_PT + slot * 1024 + 512
            sb_ = 2 if slot == 0 else 4
            rb_ = 3 if slot == 0 else 5
            lnro = SC + 1536 + slot * 512
            p.add("scalar", lambda e, pb=pb, sqo=sqo: e.activation(
                Bp(0, 128, sqo, 512), PS[pb][:, 0:512], AF.Square),
                reads=(f"psb{pb}",), writes=RB(sqo, 512))
            p.add("scalar", lambda e, pb=pb, qro=qro: e.activation(
                Bp(0, 128, qro, 512), PS[pb][:, 0:512], AF.Copy),
                reads=(f"psb{pb}",), writes=RB(qro, 512))
            p.add("tensor", lambda e, sqo=sqo, sb_=sb_: e.matmul(
                PS[sb_][0:2, 0:512], h2.ap(), Bp(0, 128, sqo, 512),
                start=True, stop=True),
                reads=RB(sqo, 512) + ("h2",), writes=(f"psb{sb_}",))
            p.add("scalar", lambda e, sb_=sb_, lnro=lnro: e.activation(
                Ap(0, 2, lnro, 512), PS[sb_][0:2, 0:512], AF.Ln),
                reads=(f"psb{sb_}",), writes=RA(lnro, 512))
            p.add("scalar", lambda e, slot=slot, lnro=lnro: e.activation(
                qkrs[0:2, slot * 512:slot * 512 + 512], Ap(0, 2, lnro, 512),
                AF.Exp, scale=-0.5),
                reads=RA(lnro, 512), writes=(f"qkrs{slot}",))
            p.add("tensor", lambda e, slot=slot, rb_=rb_: e.matmul(
                PS[rb_][:, 0:512], h2t.ap(), qkrs[0:2, slot * 512:slot * 512 + 512],
                start=True, stop=True),
                reads=(f"qkrs{slot}", "h2t"), writes=(f"psb{rb_}",))
            p.add("vector", lambda e, rb_=rb_, qro=qro: e.tensor_mul(
                Bp(0, 128, qro, 512), Bp(0, 128, qro, 512), PS[rb_][:, 0:512]),
                reads=(f"psb{rb_}",) + RB(qro, 512), writes=RB(qro, 512))
            rot = rq_bf if isq else rk_bf
            rotkey = "rq_bf" if isq else "rk_bf"
            p.add("tensor", lambda e, rot=rot, rb_=rb_, qro=qro: e.matmul(
                PS[rb_][:, 0:512], rot.ap(), Bp(0, 128, qro, 512),
                start=True, stop=True),
                reads=RB(qro, 512) + (rotkey,), writes=(f"psb{rb_}",))
            tab = AB_TAB + (0 if isq else 2 * T) + t0
            p.add("vector", lambda e, tab=tab, qro=qro, slot=slot: e.tensor_mul(
                Bp(0, 128, SCR + slot * 1024, 512), Bp(0, 128, qro, 512),
                B_(tab, 512)),
                reads=RB(qro, 512) + RB(tab, 512),
                writes=RB(SCR + slot * 1024, 512))
            stab = AB_TAB + (T if isq else 3 * T) + t0
            p.add("vector", lambda e, stab=stab, rb_=rb_, slot=slot: e.tensor_mul(
                Bp(0, 128, SCR + slot * 1024 + 512, 512), PS[rb_][:, 0:512],
                B_(stab, 512)),
                reads=(f"psb{rb_}",) + RB(stab, 512),
                writes=RB(SCR + slot * 1024 + 512, 512))
            dst = (BB_Q + o * 512) if isq else (BB_KCUR + (o - 8) * 512)
            p.add("vector", lambda e, dst=dst, slot=slot: e.tensor_add(
                Bp(0, 128, dst, 512), Bp(0, 128, SCR + slot * 1024, 512),
                Bp(0, 128, SCR + slot * 1024 + 512, 512)),
                reads=RB(SCR + slot * 1024, 1024),
                writes=RB(dst, 512))

        for o in range(16):
            qk_mms(o)
            if o > 0:
                qk_post(o - 1)
        qk_post(15)
        for vb in range(2):
            for dc in range(DC):
                slot = dc % 2
                p.dma("sync", lambda e, vb=vb, dc=dc, l=l, slot=slot: e.dma_start(
                    out=B_(BB_WROT + slot * 1024, 512),
                    in_=wbf["Wqkv"][l][dc * 128:(dc + 1) * 128,
                                       2048 + vb * 512:2048 + vb * 512 + 512]),
                    reads=(f"wbf_Wqkv{l}",), writes=RB(BB_WROT + slot * 1024, 512))
                for ts_ in range(4):
                    p.add("tensor", lambda e, vb=vb, dc=dc, ts_=ts_, slot=slot:
                          e.matmul(PS[ts_ * 2 + vb][:, 0:512],
                                   Bp(0, 128, xbf_slice(BB_XBF, dc) + ts_ * 128, 128),
                                   B_(BB_WROT + slot * 1024, 512),
                                   start=(dc == 0), stop=(dc == DC - 1)),
                          reads=RB(BB_WROT + slot * 1024, 512) +
                                RB(BB_XBF + dc * 520, 520),
                          writes=(f"psb{ts_ * 2 + vb}",))
        for ts_ in range(4):
            for vb in range(2):
                p.add("scalar", lambda e, ts_=ts_, vb=vb: e.activation(
                    Bp(0, 128, BB_VCUR + ts_ * 1024 + vb * 512, 512),
                    PS[ts_ * 2 + vb][:, 0:512], AF.Copy),
                    reads=(f"psb{ts_ * 2 + vb}",),
                    writes=RB(BB_VCUR + ts_ * 1024 + vb * 512, 512))
        for wloc in range(2):
            w = 2 * g + wloc
            bb_window(b, l, g, wloc, w)
        p.add("vector", lambda e: e.tensor_copy(
            ab[:, BB_K:BB_K + 2048].rearrange("p (c t) -> p c t", t=256),
            ab[:, BB_KCUR:BB_KCUR + 4096].rearrange(
                "p (c t) -> p c t", t=512)[:, :, 256:512]),
            reads=RB(BB_KCUR, 4096), writes=RB(BB_K, 2048))
        p.add("vector", lambda e: e.tensor_copy(
            B_(BB_V, 2048), B_(BB_VCUR + 2048, 2048)),
            reads=RB(BB_VCUR + 2048, 2048), writes=RB(BB_V, 2048))

    def bb_window(b, l, g, wloc, w):
        t0 = w * 256
        kts = (2, 3) if w == 0 else (0, 1, 2, 3)

        def ksrc(kt):
            if kt < 2:
                if wloc == 0:
                    return BB_K + kt * 128, 256, RB(BB_K, 2048)
                return BB_KCUR + kt * 128, 512, RB(BB_KCUR, 4096)
            tloc = 2 * wloc + (kt - 2) - (0 if wloc == 0 else 2)
            tsub = 2 * wloc + (kt - 2)
            return BB_KCUR + tsub * 128, 512, RB(BB_KCUR, 4096)

        def vsrc(kt):
            if kt < 2:
                if wloc == 0:
                    return BB_V + kt * 1024, RB(BB_V, 2048)
                return BB_VCUR + kt * 1024, RB(BB_VCUR, 4096)
            tsub = 2 * wloc + (kt - 2)
            return BB_VCUR + tsub * 1024, RB(BB_VCUR, 4096)

        def w_scores(h):
            par = h % 2
            ch = h // 2
            sp_lo = PS[0] if par == 0 else PS[2]
            sp_hi = PS[1] if par == 0 else PS[3]
            for kt in kts:
                koff, kstride, kregs = ksrc(kt)
                sp = sp_lo if kt < 2 else sp_hi
                scol = (kt % 2) * 256
                p.add("tensor", lambda e, koff=koff, kstride=kstride, par=par,
                      ch=ch, wloc=wloc, sp=sp, scol=scol, kt=kt: e.matmul(
                          sp[:, scol:scol + 256],
                          ab[par * 64:par * 64 + 64,
                             koff + ch * (kstride if True else 0) * 1:
                             koff + ch * kstride + 128]
                          if False else
                          ab[par * 64:par * 64 + 64,
                             koff + ch * kstride:koff + ch * kstride + 128],
                          ab[par * 64:par * 64 + 64,
                             BB_Q + ch * 512 + wloc * 256:
                             BB_Q + ch * 512 + wloc * 256 + 256],
                          start=True, stop=True),
                      reads=kregs + RB(BB_Q + ch * 512, 512),
                      writes=(f"psb{0 if sp is PS[0] else (1 if sp is PS[1] else (2 if sp is PS[2] else 3))}",))
            for sp, bank, lohi in ((sp_lo, 0 if par == 0 else 2, 0),
                                   (sp_hi, 1 if par == 0 else 3, 1)):
                if w == 0 and lohi == 0:
                    continue
                p.add("vector", lambda e, sp=sp, lohi=lohi: e.tensor_tensor(
                    sp[:, 0:512], sp[:, 0:512],
                    maskT[:, lohi * 512:lohi * 512 + 512], ALU.add),
                    reads=(f"psb{bank}", "maskT"), writes=(f"psb{bank}",))
            pts = BB_PT + par * 1024
            if w == 0:
                p.add("vector", lambda e, pts=pts: e.memset(
                    B_(pts, 512), 0.0), writes=RB(pts, 512))
                p.add("scalar", lambda e, sp=sp_hi, pts=pts,
                      bank=1 if par == 0 else 3: e.activation(
                          B_(pts + 512, 512), sp[:, 0:512], AF.Exp),
                      reads=(f"psb{1 if par == 0 else 3}",),
                      writes=RB(pts + 512, 512))
            else:
                p.add("scalar", lambda e, sp=sp_lo, pts=pts,
                      bank=0 if par == 0 else 2: e.activation(
                          B_(pts, 512), sp[:, 0:512], AF.Exp),
                      reads=(f"psb{0 if par == 0 else 2}",), writes=RB(pts, 512))
                p.add("scalar", lambda e, sp=sp_hi, pts=pts,
                      bank=1 if par == 0 else 3: e.activation(
                          B_(pts + 512, 512), sp[:, 0:512], AF.Exp),
                      reads=(f"psb{1 if par == 0 else 3}",),
                      writes=RB(pts + 512, 512))
        def w_sumspv(h):
            par = h % 2
            ch = h // 2
            pts = BB_PT + par * 1024
            for i, kt in enumerate(kts):
                p.add("tensor", lambda e, kt=kt, par=par, pts=pts,
                      first=(i == 0), last=(i == len(kts) - 1): e.matmul(
                          PS[7][0:1, par * 256:par * 256 + 256], ones_bf.ap(),
                          B_(pts + kt * 256, 256),
                          start=first, stop=last),
                      reads=RB(pts, 1024) + ("ones_bf",), writes=("psb7",))
            for i, kt in enumerate(kts):
                voff, vregs = vsrc(kt)
                p.add("tensor", lambda e, voff=voff, h=h, par=par, pts=pts, kt=kt,
                      first=(i == 0), last=(i == len(kts) - 1): e.matmul(
                          PS[4 + par][par * 64:par * 64 + 64, 0:256],
                          B_(voff + h * 64, 64), B_(pts + kt * 256, 256),
                          start=first, stop=last),
                      reads=vregs + RB(pts, 1024), writes=(f"psb{4 + par}",))
        def w_div(h):
            par = h % 2
            ch = h // 2
            if True:
                p.add("scalar", lambda e: e.activation(
                    Ap(0, 1, SC + 1536, 512), PS[7][0:1, 0:512], AF.Ln),
                    reads=("psb7",), writes=RA(SC + 1536, 512))
                p.add("scalar", lambda e: e.activation(
                    rs_row[0:1, 0:512], Ap(0, 1, SC + 1536, 512), AF.Exp,
                    scale=-1.0),
                    reads=RA(SC + 1536, 512), writes=("rs_row",))
                p.add("tensor", lambda e: e.matmul(
                    PS[6][0:64, 0:256], onesrow_bf[0:1, 0:64],
                    rs_row[0:1, 0:256], start=True, stop=True),
                    reads=("rs_row", "onesrow_bf"), writes=("psb6",))
                p.add("tensor", lambda e: e.matmul(
                    PS[6][64:128, 0:256], onesrow_bf[0:1, 0:64],
                    rs_row[0:1, 256:512], start=True, stop=True),
                    reads=("rs_row", "onesrow_bf"), writes=("psb6",))
                p.add("scalar", lambda e: e.activation(
                    Bp(0, 128, BB_MISC, 256), PS[6][:, 0:256], AF.Copy),
                    reads=("psb6",), writes=RB(BB_MISC, 256))
                for pp in range(2):
                    p.add("vector", lambda e, pp=pp, ch=ch: e.tensor_mul(
                        Bp(pp * 64, 64, BB_AFM + ch * 256, 256),
                        PS[4 + pp][pp * 64:pp * 64 + 64, 0:256],
                        Bp(pp * 64, 64, BB_MISC, 256)),
                        reads=(f"psb{4 + pp}",) + RB(BB_MISC, 256),
                        writes=RB(BB_AFM + ch * 256, 256))

        for h in range(HEADS):
            w_scores(h)
            if h > 0:
                w_sumspv(h - 1)
                if h % 2 == 0 and h >= 2:
                    w_div(h - 1)
        w_sumspv(HEADS - 1)
        w_div(HEADS - 1)
        for k in range(DC):
            slot = k % 2
            p.dma("sync", lambda e, k=k, l=l, slot=slot: e.dma_start(
                out=B_(BB_WOROT + slot * 1024, 1024),
                in_=wbf["Wo"][l][k * 128:(k + 1) * 128, :]),
                reads=(f"wbf_Wo{l}",), writes=RB(BB_WOROT + slot * 1024, 1024))
            for o in range(DC):
                p.add("tensor", lambda e, k=k, o=o, slot=slot: e.matmul(
                    PS[o][:, 256:512],
                    B_(BB_WOROT + slot * 1024 + o * 128, 128),
                    Bp(0, 128, BB_AFM + k * 256, 256),
                    start=(k == 0), stop=(k == DC - 1)),
                    reads=RB(BB_WOROT + slot * 1024, 1024) +
                          RB(BB_AFM + k * 256, 256),
                    writes=(f"psb{o}",))
        for o in range(DC):
            p.add("vector", lambda e, o=o, t0=t0: e.tensor_add(
                A(xb_off(o, t0), 256), A(xb_off(o, t0), 256), PS[o][:, 256:512]),
                reads=(f"psb{o}",) + RA(xb_off(o, t0), 256),
                writes=RA(xb_off(o, t0), 256))

    def bb_ffn_tt(b, l, tt):
        t0 = tt * 512
        if tt == 0:
            for dc in range(DC):
                p.add("vector", lambda e, dc=dc: e.memset(
                    ab[:, FB_XBF2 + dc * 520:FB_XBF2 + dc * 520 + 1], 0.0),
                    writes=RB(FB_XBF2 + dc * 520, 1))
        else:
            p.add("vector", lambda e: e.tensor_copy(
                ab[:, FB_XBF2:FB_XBF2 + 8 * 520].rearrange(
                    "p (c t) -> p c t", t=520)[:, :, 0],
                ab[:, FB_XBF2:FB_XBF2 + 8 * 520].rearrange(
                    "p (c t) -> p c t", t=520)[:, :, 512]),
                reads=RB(FB_XBF2, 8 * 520), writes=RB(FB_XBF2, 8 * 520))
        for dc in range(DC):
            p.add("vector", lambda e, dc=dc, t0=t0: e.tensor_copy(
                Bp(0, 128, FB_XBF2 + dc * 520 + 1, 512), A(xb_off(dc, t0), 512)),
                reads=RA(xb_off(dc, t0), 512),
                writes=RB(FB_XBF2 + dc * 520 + 1, 512))
        for dc in range(DC):
            p.add("scalar", lambda e, dc=dc: e.activation(
                Bp(0, 128, FB_M + dc * 512, 512),
                Bp(0, 128, xbf_slice(FB_XBF2, dc), 512), AF.Square),
                reads=RB(FB_XBF2 + dc * 520, 520), writes=RB(FB_M + dc * 512, 512))
        for dc in range(DC):
            p.add("tensor", lambda e, dc=dc: e.matmul(
                PS[5][0:1, 0:512], ones_bf.ap(),
                Bp(0, 128, xbf_slice(FB_XBF2, dc), 512),
                start=(dc == 0), stop=(dc == DC - 1)),
                reads=RB(FB_XBF2 + dc * 520, 520) + ("ones_bf",), writes=("psb5",))
            p.add("tensor", lambda e, dc=dc: e.matmul(
                PS[6][0:1, 0:512], ones_bf.ap(), Bp(0, 128, FB_M + dc * 512, 512),
                start=(dc == 0), stop=(dc == DC - 1)),
                reads=RB(FB_M + dc * 512, 512) + ("ones_bf",), writes=("psb6",))
        ln_rows_512()
        p.add("scalar", lambda e: e.activation(
            murs_row[0:1, 0:512], Ap(0, 1, SC, 512), AF.Copy),
            reads=RA(SC, 512), writes=("mursrow0",))
        p.add("scalar", lambda e: e.activation(
            rs_row[0:1, 0:512], Ap(0, 1, SC + 1024, 512), AF.Copy),
            reads=RA(SC + 1024, 512), writes=("rs_row",))
        p.add("tensor", lambda e: e.matmul(
            PS[5][:, 0:512], onesrow_bf[0:1, :], rs_row[0:1, 0:512],
            start=True, stop=True),
            reads=("rs_row", "onesrow_bf"), writes=("psb5",))
        for dc in range(DC):
            p.add("tensor", lambda e, dc=dc: e.matmul(
                PS[6][:, 0:512], lrows2_bf[0:2, dc * 128:(dc + 1) * 128],
                murs_row[0:2, 0:512], start=True, stop=True),
                reads=("mursrow0", "mursrow1", "lrows2_bf"), writes=("psb6",))
            p.add("vector", lambda e, dc=dc: e.tensor_mul(
                Ap(0, 128, BF_QN, 512), Bp(0, 128, xbf_slice(FB_XBF2, dc), 512),
                PS[5][:, 0:512]),
                reads=("psb5",) + RB(FB_XBF2 + dc * 520, 520),
                writes=RA(BF_QN, 512))
            p.add("vector", lambda e, dc=dc: e.scalar_tensor_tensor(
                Bp(0, 128, FB_XN + dc * 512, 512), Ap(0, 128, BF_QN, 512),
                lvecs[:, 16 + dc:17 + dc], PS[6][:, 0:512],
                ALU.mult, ALU.subtract),
                reads=RA(BF_QN, 512) + ("psb6", "lvecs"),
                writes=RB(FB_XN + dc * 512, 512))
        def w1_pair(ci):
            slot = ci % 2
            ab_ = ci % 2
            p.dma("sync", lambda e, ci=ci, l=l, slot=slot: e.dma_start(
                out=B_(FB_W1 + slot * 2048, 2048).rearrange(
                    "p (c m) -> p c m", m=256),
                in_=wbf["W1"][l].rearrange("(c p) m -> p c m", p=128)[
                    :, :, 256 * ci:256 * ci + 256]),
                reads=(f"wbf_W1{l}",), writes=RB(FB_W1 + slot * 2048, 2048))
            for dc in range(DC):
                p.add("tensor", lambda e, ci=ci, dc=dc, slot=slot, ab_=ab_:
                      e.matmul(
                          PS[ab_ * 2][:, 0:512],
                          B_(FB_W1 + slot * 2048 + dc * 256, 128),
                          Bp(0, 128, FB_XN + dc * 512, 512),
                          start=(dc == 0), stop=(dc == DC - 1)),
                      reads=RB(FB_W1 + slot * 2048, 2048) +
                            RB(FB_XN + dc * 512, 512),
                      writes=(f"psb{ab_ * 2}",))
                p.add("tensor", lambda e, ci=ci, dc=dc, slot=slot, ab_=ab_:
                      e.matmul(
                          PS[ab_ * 2 + 1][:, 0:512],
                          B_(FB_W1 + slot * 2048 + dc * 256 + 128, 128),
                          Bp(0, 128, FB_XN + dc * 512, 512),
                          start=(dc == 0), stop=(dc == DC - 1)),
                      reads=RB(FB_W1 + slot * 2048, 2048) +
                            RB(FB_XN + dc * 512, 512),
                      writes=(f"psb{ab_ * 2 + 1}",))

        def w1_evac(ci):
            ab_ = ci % 2
            gsl = ci % 2
            p.add("scalar", lambda e, ci=ci, gsl=gsl, ab_=ab_: e.activation(
                Bp(0, 128, FB_GELU + gsl * 512, 512), PS[ab_ * 2 + 1][:, 0:512],
                AF.Gelu, bias=b1ab[:, HC + ci:HC + ci + 1]),
                reads=(f"psb{ab_ * 2 + 1}", "b1ab_g"),
                writes=RB(FB_GELU + gsl * 512, 512))
            p.add("vector", lambda e, ci=ci, gsl=gsl, ab_=ab_:
                  e.scalar_tensor_tensor(
                      Bp(0, 128, FB_M + ci * 512, 512), PS[ab_ * 2][:, 0:512],
                      b1ab[:, ci:ci + 1], Bp(0, 128, FB_GELU + gsl * 512, 512),
                      ALU.add, ALU.mult),
                  reads=(f"psb{ab_ * 2}", "b1ab_a") + RB(FB_GELU + gsl * 512, 512),
                  writes=RB(FB_M + ci * 512, 512))

        def w2_half(ci, orange):
            slot = ci % 2
            p.dma("sync", lambda e, ci=ci, l=l, slot=slot, tag=orange[0]:
                  e.dma_start(
                      out=B_(FB_W2 + slot * 1024, 1024),
                      in_=wbf["W2"][l][ci * 128:(ci + 1) * 128, :]),
                  reads=(f"wbf_W2{l}",),
                  writes=RB(FB_W2 + slot * 1024, 1024), fam=f"w2h{orange[0]}{slot}")
            for o in orange:
                bank = 4 + (o % 4)
                p.add("tensor", lambda e, ci=ci, o=o, slot=slot, bank=bank:
                      e.matmul(
                          PS[bank][:, 0:512],
                          B_(FB_W2 + slot * 1024 + o * 128, 128),
                          Bp(0, 128, FB_M + ci * 512, 512),
                          start=(ci == 0), stop=(ci == HC - 1)),
                      reads=RB(FB_W2 + slot * 1024, 1024) +
                            RB(FB_M + ci * 512, 512),
                      writes=(f"psb{bank}",))

        for ci in range(HC):
            w1_pair(ci)
            w1_evac(ci)
            if ci > 0:
                w2_half(ci - 1, range(0, 4))
        w2_half(HC - 1, range(0, 4))
        for o in range(0, 4):
            p.add("vector", lambda e, o=o, t0=t0: e.scalar_tensor_tensor(
                A(xb_off(o, t0), 512), PS[4 + (o % 4)][:, 0:512],
                lvecs[:, 32 + o:33 + o], A(xb_off(o, t0), 512),
                ALU.add, ALU.add),
                reads=(f"psb{4 + (o % 4)}", "lvecs") + RA(xb_off(o, t0), 512),
                writes=RA(xb_off(o, t0), 512))
        for ci in range(HC):
            w2_half(ci, range(4, 8))
        for o in range(4, 8):
            p.add("vector", lambda e, o=o, t0=t0: e.scalar_tensor_tensor(
                A(xb_off(o, t0), 512), PS[4 + (o % 4)][:, 0:512],
                lvecs[:, 32 + o:33 + o], A(xb_off(o, t0), 512),
                ALU.add, ALU.add),
                reads=(f"psb{4 + (o % 4)}", "lvecs") + RA(xb_off(o, t0), 512),
                writes=RA(xb_off(o, t0), 512))

    def bb_combine(b):
        outs = []
        for tt in range(4):
            t0 = tt * 512
            for dc in range(DC):
                p.dma("sync", lambda e, b=b, dc=dc, t0=t0: e.dma_start(
                    out=Ap(0, 128, BF_QN, 512),
                    in_=x_fm[b, dc * 128:(dc + 1) * 128, t0:t0 + 512]),
                    writes=RA(BF_QN, 512))
                p.dma("scalar", lambda e, b=b, dc=dc, t0=t0: e.dma_start(
                    out=Ap(0, 128, BF_QN + 512, 512),
                    in_=s_scr[dc * 128:(dc + 1) * 128, b * T + t0:b * T + t0 + 512]),
                    reads=tuple(f"sst{b}_{c}_{dc}" for c in range(NMC)),
                    writes=RA(BF_QN + 512, 512))
                p.add("scalar", lambda e, dc=dc, t0=t0: e.activation(
                    Ap(0, 128, BF_QN + 1024, 512), A(xb_off(dc, t0), 512), AF.Silu),
                    reads=RA(xb_off(dc, t0), 512), writes=RA(BF_QN + 1024, 512))
                p.add("vector", lambda e: e.tensor_mul(
                    Ap(0, 128, BF_QN + 1024, 512), Ap(0, 128, BF_QN + 1024, 512),
                    Ap(0, 128, BF_QN + 512, 512)),
                    reads=RA(BF_QN + 512, 512) + RA(BF_QN + 1024, 512),
                    writes=RA(BF_QN + 1024, 512))
                p.add("vector", lambda e: e.tensor_add(
                    Ap(0, 128, BF_QN + 1024, 512), Ap(0, 128, BF_QN + 1024, 512),
                    Ap(0, 128, BF_QN, 512)),
                    reads=RA(BF_QN, 512) + RA(BF_QN + 1024, 512),
                    writes=RA(BF_QN + 1024, 512))
                op = p.dma("gpsimd", lambda e, b=b, dc=dc, t0=t0: e.dma_start(
                    out=out_fm[b, dc * 128:(dc + 1) * 128, t0:t0 + 512],
                    in_=Ap(0, 128, BF_QN + 1024, 512)),
                    reads=RA(BF_QN + 1024, 512), writes=(f"oo{b}_{dc}",))
                outs.append(op)
        return outs

    def bb_out_xb(b):
        outs = []
        for dc in range(DC):
            op = p.dma("gpsimd", lambda e, b=b, dc=dc: e.dma_start(
                out=out_fm[b, dc * 128:(dc + 1) * 128, :],
                in_=A(xb_off(dc, 0), T)),
                reads=RA(xb_off(dc, 0), T), writes=(f"oo{b}_{dc}",))
            outs.append(op)
        return outs

    def phase_branchB(b, milestone_out):
        bb_init(b)
        for l in range(BB_LAYERS):
            bb_layer_setup(l)
            for g in range(4):
                bb_attn_group(b, l, g)
            for tt in range(4):
                bb_ffn_tt(b, l, tt)
        if milestone_out == "xb":
            return bb_out_xb(b)
        return bb_combine(b)

    phase_consts()
    if MILESTONE == "mamba":
        phase_w()
        phase_mamba()
        phase_out_mamba()
    elif MILESTONE == "bb":
        phase_w_bb()
        fin = []
        for b in range(B_PER_CORE):
            fin += phase_branchB(b, "xb")
        p.add("gpsimd", lambda e: e.engine_nop(), deps=fin)
    else:
        phase_w()
        phase_mamba()
        phase_w_bb()
        fin = []
        for b in range(B_PER_CORE):
            fin += phase_branchB(b, "combine")
        p.add("gpsimd", lambda e: e.engine_nop(), deps=fin)

    n_ops = p.finalize_and_emit()
    return nc, n_ops, (dbg_state["map"], dbg_state["mapb"])


def kernel(**inputs):
    consts = _host_consts()
    w = _prep_weights(inputs)
    x = np.asarray(inputs["L0_attn_x"])
    nc, n_ops, dbg_maps = build_program()
    kernel.dbg_map, kernel.dbg_mapb = dbg_maps

    in_maps = []
    for core in range(N_CORES):
        xs = x[core * B_PER_CORE:(core + 1) * B_PER_CORE]
        x_fmh = np.ascontiguousarray(xs.transpose(0, 2, 1))
        m = dict(x_fm=x_fmh,
                 W_in=w["W_in"], W_out=w["W_out"], W_dt=w["W_dt"],
                 W_xdbl=w["W_xdbl"], mvecs=w["mvecs"],
                 Wqkv=w["Wqkv"], Wo=w["Wo"], W1=w["W1"], W2=w["W2"],
                 b1a=w["b1a"], b1g=w["b1g"], lvecs=w["lvecs"],
                 lrows2=w["lrows2"],
                 **consts)
        in_maps.append(m)

    res = run_bass_kernel_spmd(nc, in_maps, core_ids=list(range(N_CORES)))
    outs = []
    for core in range(N_CORES):
        o = np.asarray(res.results[core]["out_fm"])
        outs.append(np.ascontiguousarray(o.transpose(0, 2, 1)))
    kernel.dbg = [np.asarray(res.results[c]["dbg"]) for c in range(N_CORES)]
    kernel.dbgb = [np.asarray(res.results[c]["dbgb"]) for c in range(N_CORES)]
    return np.concatenate(outs, axis=0).astype(np.float32)


# version 07
# speedup vs baseline: 1.1745x; 1.0166x over previous
import zlib
import numpy as np
import ml_dtypes
import concourse.bass as bass
import concourse.mybir as mybir
from concourse.bass_utils import run_bass_kernel_spmd

F32 = mybir.dt.float32
BF16 = mybir.dt.bfloat16
AF = mybir.ActivationFunctionType
ALU = mybir.AluOpType

N_CORES = 8
B_PER_CORE = 2
T = 2048
D = 1024
DC = 8
D_INNER = 2048
HEADS = 16
DH = 64
WIN = 256
NW = 8
HID = 5632
HC = 22
HIDH = 2816
MCH = 512
NMC = 4

MILESTONE = "full"
PARANOID = True
RELAX_SAME = ("tensor",)
DEBUG_DUMPS = []


class _Op:
    __slots__ = ("eng", "fn", "deps", "is_dma", "inc_no", "gidx", "fam")

    def __init__(self, eng, fn, deps, is_dma, gidx, fam=None):
        self.eng, self.fn, self.deps, self.is_dma = eng, fn, deps, is_dma
        self.inc_no = None
        self.gidx = gidx
        self.fam = fam


class Prog:
    ENGINES = ("tensor", "vector", "scalar", "gpsimd", "sync")

    def __init__(self, nc):
        self.nc = nc
        self.ops = {e: [] for e in self.ENGINES}
        self.gcount = 0
        self.regions = {}
        self.last_fam_dma = {}

    def add(self, eng, fn, reads=(), writes=(), deps=(), is_dma=False, fam=None):
        dl = list(deps)
        for r in reads:
            ent = self.regions.get(r)
            if ent is not None and ent[0] is not None:
                dl.append(ent[0])
        for w in writes:
            ent = self.regions.get(w)
            if ent is not None:
                if ent[0] is not None:
                    dl.append(ent[0])
                dl.extend(ent[1])
        if is_dma:
            fam = fam or (writes[0] if writes else "misc")
            nslots = {"sync": 44, "gpsimd": 20, "scalar": 12,
                      "vector": 4, "tensor": 4}[eng]
            fam = f"d{zlib.crc32((eng + ':' + fam).encode()) % nslots}"
            prev = self.last_fam_dma.get((eng, fam))
            if prev is not None:
                dl.append(prev)
        op = _Op(eng, fn, dl, is_dma, self.gcount, fam)
        self.gcount += 1
        self.ops[eng].append(op)
        if is_dma:
            self.last_fam_dma[(eng, fam)] = op
        for r in reads:
            self.regions.setdefault(r, [None, []])[1].append(op)
        for w in writes:
            self.regions[w] = [op, []]
        return op

    def dma(self, eng, fn, reads=(), writes=(), deps=(), fam=None):
        return self.add(eng, fn, reads, writes, deps, is_dma=True, fam=fam)

    def finalize_and_emit(self):
        nc = self.nc
        needed = set()
        for e in self.ENGINES:
            for op in self.ops[e]:
                if op.is_dma:
                    needed.add(id(op))
                for d in op.deps:
                    if d.eng != op.eng or d.is_dma or \
                       (PARANOID and d.eng not in RELAX_SAME):
                        needed.add(id(d))
        sems, counters = {}, {}

        def op_key(op):
            return (op.eng, op.fam) if op.is_dma else (op.eng, None)

        for e in self.ENGINES:
            for op in self.ops[e]:
                if id(op) in needed:
                    key = op_key(op)
                    if key not in sems:
                        sems[key] = nc.alloc_semaphore(
                            f"s_{key[0]}_{key[1] or 'c'}")
                        counters[key] = 0
                    counters[key] += 1
                    op.inc_no = counters[key]
        for e in self.ENGINES:
            for op in self.ops[e]:
                for d in op.deps:
                    if d.gidx >= op.gidx:
                        raise RuntimeError(f"dep ordering violation on {e}")

        def mk(engine_name):
            def _run(eng):
                waited = {}
                for op in self.ops[engine_name]:
                    for d in op.deps:
                        if d.eng == engine_name and not d.is_dma and \
                           ((not PARANOID) or engine_name in RELAX_SAME):
                            continue
                        key = (d.eng, d.fam) if d.is_dma else (d.eng, None)
                        sem = sems[key]
                        val = d.inc_no * (16 if d.is_dma else 1)
                        if waited.get(key, -1) >= val:
                            continue
                        eng.wait_ge(sem, val)
                        waited[key] = val
                    ins = op.fn(eng)
                    if op.inc_no is not None:
                        key = (engine_name, op.fam) if op.is_dma else (engine_name, None)
                        ins.then_inc(sems[key], 16 if op.is_dma else 1)
                if engine_name == "gpsimd":
                    for key, cnt in counters.items():
                        if key[1] is not None:
                            if waited.get(key, -1) < cnt * 16:
                                eng.wait_ge(sems[key], cnt * 16)
            return _run

        with self.nc.Block() as block:
            block.tensor(mk("tensor"))
            block.vector(mk("vector"))
            block.scalar(mk("scalar"))
            block.gpsimd(mk("gpsimd"))
            block.sync(mk("sync"))
        return sum(len(v) for v in self.ops.values())


def _host_consts():
    c = {}
    half = np.arange(0, DH, 2, dtype=np.float64)
    inv_freq = 1.0 / 10000.0 ** (half / DH)
    t = np.arange(T, dtype=np.float64)
    freqs = np.concatenate([t[:, None] * inv_freq[None, :]] * 2, -1)
    base_scale = (half + 0.4 * DH) / (1.4 * DH)
    power = (t - T // 2) / (WIN // 2)
    scale = np.concatenate([base_scale[None, :] ** power[:, None]] * 2, -1)
    cos, sin = np.cos(freqs), np.sin(freqs)
    inv = 1.0 / scale
    for name, tab in (("cs_q", cos * scale), ("ss_q", sin * scale),
                      ("cs_k", cos * inv), ("ss_k", sin * inv)):
        fm = np.ascontiguousarray(tab.T)
        fm = np.concatenate([fm, fm], 0)
        c[name] = fm.astype(ml_dtypes.bfloat16)
    jl = np.arange(512)[:, None]
    il = np.arange(256)[None, :]
    allowed = (jl - il >= 0) & (jl - il <= WIN)
    mask = np.where(allowed, 0.0, -1e30).astype(np.float32)
    c["maskT"] = np.ascontiguousarray(
        mask.reshape(4, 128, 256).transpose(1, 0, 2)).astype(ml_dtypes.bfloat16)
    h2 = np.zeros((128, 2), np.float32)
    h2[0:64, 0] = 1.0
    h2[64:128, 1] = 1.0
    c["h2"] = h2.astype(ml_dtypes.bfloat16)
    R = np.zeros((128, 128), np.float32)
    for h in range(2):
        o = h * 64
        for j in range(32):
            R[o + j + 32, o + j] = -1.0
        for j in range(32, 64):
            R[o + j - 32, o + j] = 1.0
    c["rperm"] = R.astype(ml_dtypes.bfloat16)
    c["ones_bf"] = np.ones((128, 1), np.float32).astype(ml_dtypes.bfloat16)
    c["ones_f"] = np.ones((128, 1), np.float32)
    c["onesrow_f"] = np.ones((1, 128), np.float32)
    c["onesrow_bf"] = np.ones((1, 128), np.float32).astype(ml_dtypes.bfloat16)
    c["ident_bf"] = np.eye(128, dtype=np.float32).astype(ml_dtypes.bfloat16)
    h2t = np.zeros((2, 128), np.float32)
    h2t[0, 0:64] = 1.0
    h2t[1, 64:128] = 1.0
    c["h2t"] = h2t.astype(ml_dtypes.bfloat16)
    return c


def _pp(vec):
    return np.ascontiguousarray(np.asarray(vec).reshape(DC, 128).T)


def _prep_weights(iw):
    w = {}
    w["W_in"] = np.asarray(iw["L0_attn_W_in"])
    w["W_out"] = np.asarray(iw["L0_attn_W_out"])
    w["W_dt"] = np.asarray(iw["L0_attn_W_dt"])
    xd = np.asarray(iw["L0_attn_W_xdbl"])
    xdn = np.zeros((D, 128), np.float32)
    xdn[:, 0:16] = xd[:, 64:80]
    xdn[:, 32:48] = xd[:, 80:96]
    xdn[:, 64:128] = xd[:, 0:64]
    w["W_xdbl"] = xdn
    cxw = np.asarray(iw["L0_attn_conv_x_w"])[:, 0, :]
    czw = np.asarray(iw["L0_attn_conv_z_w"])[:, 0, :]
    cxw2 = np.stack([_pp(cxw[k]) for k in range(4)], axis=2)
    czw2 = np.stack([_pp(czw[k]) for k in range(4)], axis=2)
    mv = np.zeros((128, 112), np.float32)
    mv[:, 0:32] = cxw2.reshape(128, 32)
    mv[:, 32:64] = czw2.reshape(128, 32)
    mv[:, 64:72] = _pp(iw["L0_attn_conv_x_b"])
    mv[:, 72:80] = _pp(iw["L0_attn_conv_z_b"])
    mv[:, 80:88] = _pp(iw["L0_attn_inv_dt"])
    mv[:, 96:104] = _pp(iw["L0_attn_D"])
    mv[:, 104:112] = _pp(iw["L0_attn_b_out"])
    w["mvecs"] = mv
    w["Wqkv"] = np.asarray(iw["L0_ffn_Wqkv"])
    w["Wo"] = np.asarray(iw["L0_ffn_Wo"])
    W1 = np.asarray(iw["L0_ffn_W1"])
    b1 = np.asarray(iw["L0_ffn_b1"])
    W1n = np.zeros((6, D, HID), np.float32)
    b1n = np.zeros((6, HID), np.float32)
    for ci in range(HC):
        lo, hi = 128 * ci, min(128 * ci + 128, 2730)
        if lo < 2730:
            n = hi - lo
            W1n[:, :, 256 * ci:256 * ci + n] = W1[:, :, lo:hi]
            W1n[:, :, 256 * ci + 128:256 * ci + 128 + n] = W1[:, :, 2730 + lo:2730 + hi]
            b1n[:, 256 * ci:256 * ci + n] = b1[:, lo:hi]
            b1n[:, 256 * ci + 128:256 * ci + 128 + n] = b1[:, 2730 + lo:2730 + hi]
    w["W1"] = W1n
    b1r = b1n.reshape(6, HC, 2, 128)
    w["b1a"] = np.ascontiguousarray(b1r[:, :, 0, :].transpose(0, 2, 1))
    w["b1g"] = np.ascontiguousarray(b1r[:, :, 1, :].transpose(0, 2, 1))
    W2 = np.asarray(iw["L0_ffn_W2"])
    W2n = np.zeros((6, HIDH, D), np.float32)
    W2n[:, 0:2730, :] = W2
    w["W2"] = W2n
    qs = np.tile(np.asarray(iw["L0_ffn_q_scale"])[:, None, :], (1, HEADS, 1)).reshape(6, D)
    ks = np.tile(np.asarray(iw["L0_ffn_k_scale"])[:, None, :], (1, HEADS, 1)).reshape(6, D)
    lv = np.zeros((6, 128, 48), np.float32)
    for l in range(6):
        lv[l, :, 0:8] = _pp(qs[l])
        lv[l, :, 8:16] = _pp(ks[l])
        lv[l, :, 16:24] = _pp(np.asarray(iw["L0_ffn_ln_g"])[l])
        lv[l, :, 24:32] = _pp(np.asarray(iw["L0_ffn_ln_b"])[l])
        lv[l, :, 32:40] = _pp(np.asarray(iw["L0_ffn_b2"])[l])
    w["lvecs"] = lv
    Wq = w["Wqkv"]
    w["Wqkv_t"] = np.ascontiguousarray(
        Wq[:, :, 0:2048].reshape(6, 8, 128, 16, 128)
        .transpose(0, 3, 2, 1, 4).reshape(6, 16, 128, 1024))
    w["Wqkv_v"] = np.ascontiguousarray(
        Wq[:, :, 2048:3072].reshape(6, 8, 128, 2, 512)
        .transpose(0, 3, 1, 2, 4).reshape(6, 2, 8, 128, 512))
    W1t = w["W1"].reshape(6, 8, 128, HC, 256)
    w["W1_t"] = np.ascontiguousarray(
        W1t.transpose(0, 3, 2, 1, 4).reshape(6, HC, 128, 2048))
    w["W_in_t"] = np.ascontiguousarray(
        w["W_in"].reshape(8, 128, 16, 128).transpose(2, 1, 0, 3)
        .reshape(16, 128, 1024))
    lr = np.zeros((6, 2, D), np.float32)
    lr[:, 0, :] = np.asarray(iw["L0_ffn_ln_g"])
    lr[:, 1, :] = np.asarray(iw["L0_ffn_ln_b"])
    w["lrows2"] = lr
    return w


MF_XF = 0
MF_PALL = 0
MF_PRE = 4160
MF_ZPRE = 8320
MF_XS = 12480
MF_RES = 16576
MF_WALL = 20672
MF_W = 22880
MF_P = 23392
MF_BREP0 = 24768
MF_BREP1 = 25792
MF_STATE = 26816
MF_BC = 26944
MF_STASH = 27456
AF_COLS = 27488
BF_XB = 0
BF_MISC = 16512

MB_WROT = 0
MB_WDT = 2048
MB_WXD = 3072
MB_XN = 4096
MB_ZBF = 8256
MB_XSBF = 12416
MB_DTL = 13440
MB_YBF = 13952
BB_WROT = 0
BB_WOROT = 2048
BB_XBF = 4096
BB_Q = 8256
BB_K = 12352
BB_KCUR = BB_K + 2048
BB_V = 18496
BB_VCUR = BB_V + 2048
BB_PT = 24640
BB_AFM = 26688
BB_MISC = 28736
FB_W1 = 0
FB_W2 = 4096
FB_M = 6144
FB_GELU = 17408
FB_XBF2 = 18432
FB_XN = 22592
AB_COLS = 37952
AB_TAB = AB_COLS - 4 * T
BF_QN = 16512



def build_program():
    nc = bass.Bass("TRN2", target_bir_lowering=False)
    p = Prog(nc)

    def dtin(name, shape, dtype=F32):
        return nc.dram_tensor(name, list(shape), dtype, kind="ExternalInput").ap()

    x_fm = dtin("x_fm", [B_PER_CORE, D, T])
    W_in_d = dtin("W_in", [D, D_INNER])
    W_out_d = dtin("W_out", [D_INNER, D])
    W_dt_d = dtin("W_dt", [64, D])
    W_xdbl_d = dtin("W_xdbl", [D, 128])
    mvecs_d = dtin("mvecs", [128, 112])
    Wqkv_d = dtin("Wqkv", [6, D, 3 * D])
    Wo_d = dtin("Wo", [6, D, D])
    W1_d = dtin("W1", [6, D, HID])
    Wqkv_t_d = dtin("Wqkv_t", [6, 16, 128, 1024])
    Wqkv_v_d = dtin("Wqkv_v", [6, 2, 8, 128, 512])
    W1_t_d = dtin("W1_t", [6, HC, 128, 2048])
    W_in_t_d = dtin("W_in_t", [16, 128, 1024])
    W2_d = dtin("W2", [6, HIDH, D])
    b1a_d = dtin("b1a", [6, 128, HC])
    b1g_d = dtin("b1g", [6, 128, HC])
    lvecs_d = dtin("lvecs", [6, 128, 48])
    tabs_d = {nm: dtin(nm, [128, T], BF16) for nm in ("cs_q", "ss_q", "cs_k", "ss_k")}
    maskT_d = dtin("maskT", [128, 4, WIN], BF16)
    h2_d = dtin("h2", [128, 2], BF16)
    rperm_d = dtin("rperm", [128, 128], BF16)
    ones_bf_d = dtin("ones_bf", [128, 1], BF16)
    ones_f_d = dtin("ones_f", [128, 1])
    onesrow_f_d = dtin("onesrow_f", [1, 128])
    onesrow_bf_d = dtin("onesrow_bf", [1, 128], BF16)
    ident_bf_d = dtin("ident_bf", [128, 128], BF16)
    h2t_d = dtin("h2t", [2, 128], BF16)
    lrows2_d = dtin("lrows2", [6, 2, D])

    out_fm = nc.dram_tensor("out_fm", [B_PER_CORE, D, T], F32,
                            kind="ExternalOutput").ap()
    s_scr = nc.dram_tensor("s_scr", [D, B_PER_CORE * T], F32, kind="Internal").ap()
    bc_scr = nc.dram_tensor("bc_scr", [2, 48, MCH], F32, kind="Internal").ap()
    wbf = {}
    for name, shape in (("W_in", [D, D_INNER]), ("W_out", [D_INNER, D]),
                        ("W_dt", [64, D]), ("W_xdbl", [D, 128]),
                        ("Wqkv", [6, D, 3 * D]), ("Wo", [6, D, D]),
                        ("W1", [6, D, HID]), ("W2", [6, HIDH, D]),
                        ("Wqkv_t", [6, 16, 128, 1024]),
                        ("Wqkv_v", [6, 2, 8, 128, 512]),
                        ("W1_t", [6, HC, 128, 2048]),
                        ("W_in_t", [16, 128, 1024])):
        wbf[name] = nc.dram_tensor("bf_" + name, list(shape), BF16,
                                   kind="Internal").ap()

    dbg_d = nc.dram_tensor("dbg", [128, 65536], F32, kind="ExternalOutput").ap()
    dbg_b_d = nc.dram_tensor("dbgb", [128, 16384], BF16, kind="ExternalOutput").ap()
    dbg_state = {"off": 0, "map": {}, "offb": 0, "mapb": {}}

    af = nc.alloc_sbuf_tensor("af", [128, AF_COLS], F32)
    ab = nc.alloc_sbuf_tensor("ab", [128, AB_COLS], BF16)
    maskT = nc.alloc_sbuf_tensor("sb_maskT", [128, 4 * WIN], BF16)
    h2 = nc.alloc_sbuf_tensor("sb_h2", [128, 2], BF16)
    rperm = nc.alloc_sbuf_tensor("sb_rperm", [128, 128], BF16)
    ones_bf = nc.alloc_sbuf_tensor("sb_ones_bf", [128, 1], BF16)
    ones_f = nc.alloc_sbuf_tensor("sb_ones_f", [128, 1], F32)
    onesrow_f = nc.alloc_sbuf_tensor("sb_onesrow_f", [1, 128], F32)
    onesrow_bf = nc.alloc_sbuf_tensor("sb_onesrow_bf", [1, 128], BF16)
    ident_bf = nc.alloc_sbuf_tensor("sb_ident_bf", [128, 128], BF16)
    mvecs = nc.alloc_sbuf_tensor("sb_mvecs", [128, 112], F32)
    lvecs = nc.alloc_sbuf_tensor("sb_lvecs", [128, 48], F32)
    b1ab = nc.alloc_sbuf_tensor("sb_b1ab", [128, 2 * HC], F32)
    h2t = nc.alloc_sbuf_tensor("sb_h2t", [2, 128], BF16)
    lrows2 = nc.alloc_sbuf_tensor("sb_lrows2", [2, D], F32)
    lrows2_bf = nc.alloc_sbuf_tensor("sb_lrows2_bf", [2, D], BF16)
    rq_bf = nc.alloc_sbuf_tensor("sb_rq_bf", [128, 128], BF16)
    rk_bf = nc.alloc_sbuf_tensor("sb_rk_bf", [128, 128], BF16)
    murs_row = nc.alloc_sbuf_tensor("sb_murs_row", [2, 512], BF16)
    rs_row = nc.alloc_sbuf_tensor("sb_rs_row", [1, 512], BF16)
    qkrs = nc.alloc_sbuf_tensor("sb_qkrs", [2, 1024], BF16)
    small = nc.alloc_sbuf_tensor("sb_small", [128, 8], F32)
    wstage = [nc.alloc_sbuf_tensor(f"wstage{i}", [128, 512], F32) for i in range(2)]
    castb = [nc.alloc_sbuf_tensor(f"castb{i}", [128, 512], BF16) for i in range(2)]
    PS = [nc.alloc_psum_tensor(f"ps{i}", [128, 512], F32) for i in range(8)]

    def dbg(tag, ap_fn, cols, reads, rows=128):
        if not DEBUG_DUMPS or tag not in DEBUG_DUMPS:
            return
        off = dbg_state["off"]
        assert off + cols <= 65536, "dbg overflow"
        dbg_state["map"][tag] = (off, cols)
        dbg_state["off"] = off + cols
        p.dma("gpsimd", lambda e, off=off, cols=cols, rows=rows: e.dma_start(
            out=dbg_d[0:rows, off:off + cols], in_=ap_fn()),
            reads=reads, writes=(f"dbg_{tag}",))

    def dbgb(tag, ap_fn, cols, reads, rows=128):
        if not DEBUG_DUMPS or tag not in DEBUG_DUMPS:
            return
        off = dbg_state["offb"]
        assert off + cols <= 16384, "dbgb overflow"
        dbg_state["mapb"][tag] = (off, cols)
        dbg_state["offb"] = off + cols
        p.dma("gpsimd", lambda e, off=off, cols=cols, rows=rows: e.dma_start(
            out=dbg_b_d[0:rows, off:off + cols], in_=ap_fn()),
            reads=reads, writes=(f"dbgb_{tag}",))

    def A(off, length):
        return af[:, off:off + length]

    def Ap(p0, np_, off, length):
        return af[p0:p0 + np_, off:off + length]

    def B_(off, length):
        return ab[:, off:off + length]

    def Bp(p0, np_, off, length):
        return ab[p0:p0 + np_, off:off + length]

    def RA(off, length):
        return tuple(f"af{i}" for i in range(off // 512, (off + length + 511) // 512))

    def RB(off, length):
        return tuple(f"ab{i}" for i in range(off // 512, (off + length + 511) // 512))

    cast_ctr = [0]

    def cast_weight(src_ap, dst_ap, rows, cols, rkey):
        nstr = (rows + 127) // 128
        for si in range(nstr):
            r0 = si * 128
            nr = min(128, rows - r0)
            coff = 0
            while coff < cols:
                cw = min(512, cols - coff)
                sl = cast_ctr[0] % 2
                cast_ctr[0] += 1
                p.dma("sync",
                      lambda e, s=src_ap, r0=r0, nr=nr, co=coff, cw=cw, sl=sl:
                      e.dma_start(out=wstage[sl][0:nr, 0:cw],
                                  in_=s[r0:r0 + nr, co:co + cw]),
                      writes=(f"wstage{sl}",))
                p.add("scalar",
                      lambda e, nr=nr, cw=cw, sl=sl:
                      e.activation(castb[sl][0:nr, 0:cw], wstage[sl][0:nr, 0:cw],
                                   AF.Copy),
                      reads=(f"wstage{sl}",), writes=(f"castb{sl}",))
                p.dma("gpsimd",
                      lambda e, d=dst_ap, r0=r0, nr=nr, co=coff, cw=cw, sl=sl:
                      e.dma_start(out=d[r0:r0 + nr, co:co + cw],
                                  in_=castb[sl][0:nr, 0:cw]),
                      reads=(f"castb{sl}",), writes=(rkey,))
                coff += cw

    def phase_w():
        cast_weight(W_in_t_d.rearrange("o p m -> (o p) m"),
                    wbf["W_in_t"].rearrange("o p m -> (o p) m"),
                    16 * 128, 1024, "wbf_W_in_t")
        cast_weight(W_out_d, wbf["W_out"], D_INNER, D, "wbf_W_out")
        cast_weight(W_dt_d, wbf["W_dt"], 64, D, "wbf_W_dt")
        cast_weight(W_xdbl_d, wbf["W_xdbl"], D, 128, "wbf_W_xdbl")


    def phase_consts():
        p.dma("sync", lambda e: e.dma_start(
            out=maskT[:, :], in_=maskT_d.rearrange("p k w -> p (k w)")),
            writes=("maskT",))
        for t_, nm in ((h2, "h2"), (rperm, "rperm"), (ones_bf, "ones_bf"),
                       (h2t, "h2t"),
                       (ones_f, "ones_f"), (ident_bf, "ident_bf"),
                       (mvecs, "mvecs"), (onesrow_f, "onesrow_f"),
                       (onesrow_bf, "onesrow_bf")):
            src = {"h2": h2_d, "rperm": rperm_d, "ones_bf": ones_bf_d,
                   "ones_f": ones_f_d, "ident_bf": ident_bf_d,
                   "mvecs": mvecs_d, "onesrow_f": onesrow_f_d,
                   "onesrow_bf": onesrow_bf_d, "h2t": h2t_d}[nm]
            p.dma("sync", lambda e, t_=t_, src=src: e.dma_start(
                out=t_[:, :], in_=src), writes=(nm,))
        p.add("vector", lambda e: e.tensor_scalar(
            mvecs[:, 88:96], mvecs[:, 80:88], 2.0, None, ALU.mult),
            reads=("mvecs",), writes=("mvecs",))
        p.add("vector", lambda e: e.memset(small[:, 0:1], 1e-5), writes=("eps",))
        p.add("vector", lambda e: e.memset(small[:, 2:3], 1.0), writes=("one1",))
        p.add("vector", lambda e: e.memset(murs_row[0:2, :], -1.0),
              writes=("mursrow1", "mursrow0"))

    SC = MF_WALL + 128
    STASH = MF_STASH

    def m_ln_chunk(b, c):
        t0 = c * MCH
        hal = 4 if c < NMC - 1 else 0
        nt = MCH + hal
        for dc in range(DC):
            p.dma("sync", lambda e, b=b, dc=dc, t0=t0, nt=nt:
                  e.dma_start(out=A(MF_XF + dc * 520, nt),
                              in_=x_fm[b, dc * 128:(dc + 1) * 128, t0:t0 + nt]),
                  writes=RA(MF_XF + dc * 520, nt))
        for dc in range(DC):
            p.add("vector", lambda e, dc=dc, nt=nt:
                  e.tensor_copy(Bp(0, 128, MB_XN + dc * 520, nt),
                                A(MF_XF + dc * 520, nt)),
                  reads=RA(MF_XF + dc * 520, nt),
                  writes=RB(MB_XN + dc * 520, nt))
        for dc in range(DC):
            p.add("scalar", lambda e, dc=dc, nt=nt:
                  e.activation(Bp(0, 128, MB_ZBF + dc * 520, nt),
                               A(MF_XF + dc * 520, nt), AF.Square),
                  reads=RA(MF_XF + dc * 520, nt),
                  writes=RB(MB_ZBF + dc * 520, nt))
        for dc in range(DC):
            p.add("tensor", lambda e, dc=dc: e.matmul(
                PS[5][0:1, 0:512], ones_bf.ap(), Bp(0, 128, MB_XN + dc * 520, 512),
                start=(dc == 0), stop=(dc == DC - 1)),
                reads=RB(MB_XN + dc * 520, 512) + ("ones_bf",), writes=("psb5",))
        for dc in range(DC):
            p.add("tensor", lambda e, dc=dc: e.matmul(
                PS[6][0:1, 0:512], ones_bf.ap(), Bp(0, 128, MB_ZBF + dc * 520, 512),
                start=(dc == 0), stop=(dc == DC - 1)),
                reads=RB(MB_ZBF + dc * 520, 512) + ("ones_bf",), writes=("psb6",))
        if hal:
            for dc in range(DC):
                p.add("tensor", lambda e, dc=dc, hal=hal: e.matmul(
                    PS[7][0:1, 0:hal], ones_bf.ap(),
                    Bp(0, 128, MB_XN + dc * 520 + 512, hal),
                    start=(dc == 0), stop=(dc == DC - 1)),
                    reads=RB(MB_XN + dc * 520 + 512, hal) + ("ones_bf",),
                    writes=("psb7",))
            for dc in range(DC):
                p.add("tensor", lambda e, dc=dc, hal=hal: e.matmul(
                    PS[7][0:1, 8:8 + hal], ones_bf.ap(),
                    Bp(0, 128, MB_ZBF + dc * 520 + 512, hal),
                    start=(dc == 0), stop=(dc == DC - 1)),
                    reads=RB(MB_ZBF + dc * 520 + 512, hal) + ("ones_bf",),
                    writes=("psb7",))
        p.add("vector", lambda e: e.tensor_scalar(
            Ap(0, 1, SC, 512), PS[5][0:1, 0:512], 1.0 / D, None, ALU.mult),
            reads=("psb5",), writes=RA(SC, 512))
        p.add("scalar", lambda e: e.activation(
            Ap(0, 1, SC + 512, 512), Ap(0, 1, SC, 512), AF.Square),
            reads=RA(SC, 512), writes=RA(SC + 512, 512))
        p.add("vector", lambda e: e.scalar_tensor_tensor(
            Ap(0, 1, SC + 1024, 512), PS[6][0:1, 0:512], 1.0 / D,
            Ap(0, 1, SC + 512, 512), ALU.mult, ALU.subtract),
            reads=("psb6",) + RA(SC + 512, 512), writes=RA(SC + 1024, 512))
        p.add("scalar", lambda e: e.activation(
            Ap(0, 1, SC + 1536, 512), Ap(0, 1, SC + 1024, 512), AF.Ln,
            bias=small[0:1, 0:1]),
            reads=RA(SC + 1024, 512) + ("eps",), writes=RA(SC + 1536, 512))
        p.add("scalar", lambda e: e.activation(
            Ap(0, 1, SC + 1024, 512), Ap(0, 1, SC + 1536, 512), AF.Exp,
            scale=-0.5),
            reads=RA(SC + 1536, 512), writes=RA(SC + 1024, 512))
        p.add("vector", lambda e: e.tensor_mul(
            Ap(0, 1, SC, 512), Ap(0, 1, SC, 512), Ap(0, 1, SC + 1024, 512)),
            reads=RA(SC, 512) + RA(SC + 1024, 512), writes=RA(SC, 512))
        if hal:
            p.add("vector", lambda e, hal=hal: e.tensor_scalar(
                Ap(0, 1, SC + 2048, hal), PS[7][0:1, 0:hal], 1.0 / D, None,
                ALU.mult), reads=("psb7",), writes=RA(SC + 2048, hal))
            p.add("scalar", lambda e, hal=hal: e.activation(
                Ap(0, 1, SC + 2056, hal), Ap(0, 1, SC + 2048, hal), AF.Square),
                reads=RA(SC + 2048, hal), writes=RA(SC + 2056, hal))
            p.add("vector", lambda e, hal=hal: e.scalar_tensor_tensor(
                Ap(0, 1, SC + 2064, hal), PS[7][0:1, 8:8 + hal], 1.0 / D,
                Ap(0, 1, SC + 2056, hal), ALU.mult, ALU.subtract),
                reads=("psb7",) + RA(SC + 2056, hal), writes=RA(SC + 2064, hal))
            p.add("scalar", lambda e, hal=hal: e.activation(
                Ap(0, 1, SC + 2072, hal), Ap(0, 1, SC + 2064, hal), AF.Ln,
                bias=small[0:1, 0:1]),
                reads=RA(SC + 2064, hal) + ("eps",), writes=RA(SC + 2072, hal))
            p.add("scalar", lambda e, hal=hal: e.activation(
                Ap(0, 1, SC + 2064, hal), Ap(0, 1, SC + 2072, hal), AF.Exp,
                scale=-0.5),
                reads=RA(SC + 2072, hal), writes=RA(SC + 2064, hal))
            p.add("vector", lambda e, hal=hal: e.tensor_mul(
                Ap(0, 1, SC + 2048, hal), Ap(0, 1, SC + 2048, hal),
                Ap(0, 1, SC + 2064, hal)),
                reads=RA(SC + 2048, hal) + RA(SC + 2064, hal),
                writes=RA(SC + 2048, hal))
        p.add("tensor", lambda e: e.matmul(
            PS[5][:, 0:512], onesrow_f[0:1, :], Ap(0, 1, SC + 1024, 512),
            start=True, stop=True),
            reads=RA(SC + 1024, 512) + ("onesrow_f",), writes=("psb5",))
        p.add("tensor", lambda e: e.matmul(
            PS[6][:, 0:512], onesrow_f[0:1, :], Ap(0, 1, SC, 512),
            start=True, stop=True),
            reads=RA(SC, 512) + ("onesrow_f",), writes=("psb6",))
        if hal:
            p.add("tensor", lambda e, hal=hal: e.matmul(
                PS[7][:, 16:16 + hal], onesrow_f[0:1, :], Ap(0, 1, SC + 2064, hal),
                start=True, stop=True),
                reads=RA(SC + 2064, hal) + ("onesrow_f",), writes=("psb7",))
            p.add("tensor", lambda e, hal=hal: e.matmul(
                PS[7][:, 24:24 + hal], onesrow_f[0:1, :], Ap(0, 1, SC + 2048, hal),
                start=True, stop=True),
                reads=RA(SC + 2048, hal) + ("onesrow_f",), writes=("psb7",))
        for dc in range(DC):
            p.add("vector", lambda e, dc=dc: e.tensor_mul(
                A(MF_XF + dc * 520, 512), A(MF_XF + dc * 520, 512),
                PS[5][:, 0:512]),
                reads=("psb5",) + RA(MF_XF + dc * 520, 512),
                writes=RA(MF_XF + dc * 520, 512))
            p.add("vector", lambda e, dc=dc: e.tensor_tensor(
                Bp(0, 128, MB_XN + dc * 520, 512), A(MF_XF + dc * 520, 512),
                PS[6][:, 0:512], ALU.subtract),
                reads=("psb6",) + RA(MF_XF + dc * 520, 512),
                writes=RB(MB_XN + dc * 520, 512))
            if hal:
                p.add("vector", lambda e, dc=dc, hal=hal: e.tensor_mul(
                    A(MF_XF + dc * 520 + 512, hal),
                    A(MF_XF + dc * 520 + 512, hal), PS[7][:, 16:16 + hal]),
                    reads=("psb7",) + RA(MF_XF + dc * 520 + 512, hal),
                    writes=RA(MF_XF + dc * 520 + 512, hal))
                p.add("vector", lambda e, dc=dc, hal=hal: e.tensor_tensor(
                    Bp(0, 128, MB_XN + dc * 520 + 512, hal),
                    A(MF_XF + dc * 520 + 512, hal),
                    PS[7][:, 24:24 + hal], ALU.subtract),
                    reads=("psb7",) + RA(MF_XF + dc * 520 + 512, hal),
                    writes=RB(MB_XN + dc * 520 + 512, hal))

    def m_xz_chunk(b, c):
        hal = 4 if c < NMC - 1 else 0
        if c == 0:
            for dc in range(DC):
                p.add("vector", lambda e, dc=dc: e.memset(
                    A(MF_PRE + dc * 520, 2), 0.0), writes=RA(MF_PRE + dc * 520, 2))
                p.add("vector", lambda e, dc=dc: e.memset(
                    A(MF_ZPRE + dc * 520, 2), 0.0), writes=RA(MF_ZPRE + dc * 520, 2))
        else:
            for dc in range(DC):
                p.add("vector", lambda e, dc=dc: e.tensor_copy(
                    A(MF_PRE + dc * 520, 2), A(STASH + dc * 2, 2)),
                    reads=RA(STASH + dc * 2, 2), writes=RA(MF_PRE + dc * 520, 2))
                p.add("vector", lambda e, dc=dc: e.tensor_copy(
                    A(MF_ZPRE + dc * 520, 2), A(STASH + 16 + dc * 2, 2)),
                    reads=RA(STASH + 16 + dc * 2, 2),
                    writes=RA(MF_ZPRE + dc * 520, 2))
        for o in range(16):
            slot = o % 2
            pb = o % 2
            p.dma("sync", lambda e, o=o, slot=slot: e.dma_start(
                out=B_(MB_WROT + slot * 1024, 1024),
                in_=wbf["W_in_t"][o]),
                reads=("wbf_W_in_t",), writes=RB(MB_WROT + slot * 1024, 1024))
            for dc in range(DC):
                p.add("tensor", lambda e, dc=dc, slot=slot, pb=pb: e.matmul(
                    PS[pb][:, 0:512],
                    B_(MB_WROT + slot * 1024 + dc * 128, 128),
                    Bp(0, 128, MB_XN + dc * 520, 512),
                    start=(dc == 0), stop=(dc == DC - 1)),
                    reads=RB(MB_WROT + slot * 1024, 1024) +
                          RB(MB_XN + dc * 520, 512),
                    writes=(f"psb{pb}",))
            if hal:
                for dc in range(DC):
                    p.add("tensor", lambda e, o=o, dc=dc, slot=slot, hal=hal:
                          e.matmul(PS[2][:, o * 4:o * 4 + hal],
                                   B_(MB_WROT + slot * 1024 + dc * 128, 128),
                                   Bp(0, 128, MB_XN + dc * 520 + 512, hal),
                                   start=(dc == 0), stop=(dc == DC - 1)),
                          reads=RB(MB_WROT + slot * 1024, 1024) +
                                RB(MB_XN + dc * 520 + 512, hal),
                          writes=("psb2",))
            base = MF_PRE if o < 8 else MF_ZPRE
            dcc = o % 8
            p.add("vector", lambda e, base=base, dcc=dcc, pb=pb: e.tensor_copy(
                A(base + dcc * 520 + 2, 512), PS[pb][:, 0:512]),
                reads=(f"psb{pb}",), writes=RA(base + dcc * 520 + 2, 512))
            if hal:
                p.add("vector", lambda e, base=base, dcc=dcc, o=o, hal=hal:
                      e.tensor_copy(A(base + dcc * 520 + 514, hal),
                                    PS[2][:, o * 4:o * 4 + hal]),
                      reads=("psb2",), writes=RA(base + dcc * 520 + 514, hal))
            else:
                p.add("vector", lambda e, base=base, dcc=dcc: e.memset(
                    A(base + dcc * 520 + 514, 4), 0.0),
                    writes=RA(base + dcc * 520 + 514, 4))
        for dc in range(DC):
            p.add("vector", lambda e, dc=dc: e.tensor_copy(
                A(STASH + dc * 2, 2), A(MF_PRE + dc * 520 + 512, 2)),
                reads=RA(MF_PRE + dc * 520 + 512, 2),
                writes=RA(STASH + dc * 2, 2))
            p.add("vector", lambda e, dc=dc: e.tensor_copy(
                A(STASH + 16 + dc * 2, 2), A(MF_ZPRE + dc * 520 + 512, 2)),
                reads=RA(MF_ZPRE + dc * 520 + 512, 2),
                writes=RA(STASH + 16 + dc * 2, 2))

    def m_conv_chunk(b, c):
        for dc in range(DC):
            src_ = MF_PRE + dc * 520
            dst = MF_XS + dc * 512
            p.add("vector", lambda e, src_=src_, dst=dst, dc=dc: e.tensor_scalar(
                A(dst, 512), A(src_ + 1, 512), mvecs[:, dc * 4:dc * 4 + 1],
                None, ALU.mult),
                reads=RA(src_, 520) + ("mvecs",), writes=RA(dst, 512))
            for k in range(1, 4):
                p.add("vector", lambda e, src_=src_, dst=dst, dc=dc, k=k:
                      e.scalar_tensor_tensor(
                          A(dst, 512), A(src_ + 1 + k, 512),
                          mvecs[:, dc * 4 + k:dc * 4 + k + 1],
                          A(dst, 512), ALU.mult, ALU.add),
                      reads=RA(src_, 520) + RA(dst, 512) + ("mvecs",),
                      writes=RA(dst, 512))
            p.add("scalar", lambda e, dst=dst, dc=dc: e.activation(
                A(dst, 512), A(dst, 512), AF.Silu, bias=mvecs[:, 64 + dc:65 + dc]),
                reads=RA(dst, 512) + ("mvecs",), writes=RA(dst, 512))
            p.add("vector", lambda e, dst=dst, dc=dc: e.tensor_copy(
                Bp(0, 128, MB_XSBF + (dc % 2) * 512, 512), A(dst, 512)),
                reads=RA(dst, 512), writes=RB(MB_XSBF + (dc % 2) * 512, 512))
            p.add("tensor", lambda e, dc=dc: e.matmul(
                PS[3][:, 0:512], Bp(0, 128, MB_WXD + dc * 128, 128),
                Bp(0, 128, MB_XSBF + (dc % 2) * 512, 512),
                start=(dc == 0), stop=(dc == DC - 1)),
                reads=RB(MB_WXD, 1024) + RB(MB_XSBF + (dc % 2) * 512, 512),
                writes=("psb3",))
        for dc in range(DC):
            src_ = MF_ZPRE + dc * 520
            dst = MF_RES + dc * 512
            p.add("vector", lambda e, src_=src_, dst=dst, dc=dc: e.tensor_scalar(
                A(dst, 512), A(src_ + 1, 512),
                mvecs[:, 32 + dc * 4:32 + dc * 4 + 1], None, ALU.mult),
                reads=RA(src_, 520) + ("mvecs",), writes=RA(dst, 512))
            for k in range(1, 4):
                p.add("vector", lambda e, src_=src_, dst=dst, dc=dc, k=k:
                      e.scalar_tensor_tensor(
                          A(dst, 512), A(src_ + 1 + k, 512),
                          mvecs[:, 32 + dc * 4 + k:32 + dc * 4 + k + 1],
                          A(dst, 512), ALU.mult, ALU.add),
                      reads=RA(src_, 520) + RA(dst, 512) + ("mvecs",),
                      writes=RA(dst, 512))
            p.add("scalar", lambda e, dst=dst, dc=dc: e.activation(
                Bp(0, 128, MB_ZBF + dc * 512, 512), A(dst, 512), AF.Silu,
                bias=mvecs[:, 72 + dc:73 + dc]),
                reads=RA(dst, 512) + ("mvecs",), writes=RB(MB_ZBF + dc * 512, 512))

    def m_xdbl_dt_chunk(b, c):
        p.add("vector", lambda e: e.tensor_copy(
            Ap(0, 48, MF_BC, 512), PS[3][0:48, 0:512]),
            reads=("psb3",), writes=RA(MF_BC, 512))
        p.add("scalar", lambda e: e.activation(
            Bp(0, 64, MB_DTL, 512), PS[3][64:128, 0:512], AF.Copy),
            reads=("psb3",), writes=RB(MB_DTL, 512))
        p.dma("gpsimd", lambda e, c=c: e.dma_start(
            out=bc_scr[c % 2], in_=Ap(0, 48, MF_BC, 512)),
            reads=RA(MF_BC, 512), writes=(f"bcscr{c % 2}",))
        for o in range(DC):
            p.add("tensor", lambda e, o=o: e.matmul(
                PS[4][:, 0:512], Bp(0, 64, MB_WDT + o * 128, 128),
                Bp(0, 64, MB_DTL, 512), start=True, stop=True),
                reads=RB(MB_WDT, 1024) + RB(MB_DTL, 512), writes=("psb4",))
            p.add("vector", lambda e, o=o: e.tensor_scalar(
                A(MF_PRE + o * 512, 512), PS[4][:, 0:512],
                mvecs[:, 88 + o:89 + o], None, ALU.add),
                reads=("psb4", "mvecs"), writes=RA(MF_PRE + o * 512, 512))
            p.add("scalar", lambda e, o=o: e.activation(
                A(MF_W, 512), A(MF_PRE + o * 512, 512), AF.Abs),
                reads=RA(MF_PRE + o * 512, 512), writes=RA(MF_W, 512))
            p.add("scalar", lambda e, o=o: e.activation(
                A(MF_W, 512), A(MF_W, 512), AF.Exp, scale=-1.0),
                reads=RA(MF_W, 512), writes=RA(MF_W, 512))
            p.add("scalar", lambda e, o=o: e.activation(
                A(MF_W, 512), A(MF_W, 512), AF.Ln, bias=small[:, 2:3]),
                reads=RA(MF_W, 512) + ("one1",), writes=RA(MF_W, 512))
            p.add("vector", lambda e, o=o: e.tensor_scalar(
                A(MF_P, 512), A(MF_PRE + o * 512, 512), 0.0, None, ALU.max),
                reads=RA(MF_PRE + o * 512, 512), writes=RA(MF_P, 512))
            p.add("vector", lambda e, o=o: e.tensor_add(
                A(MF_PRE + o * 512, 512), A(MF_P, 512), A(MF_W, 512)),
                reads=RA(MF_P, 512) + RA(MF_W, 512),
                writes=RA(MF_PRE + o * 512, 512))
            p.add("vector", lambda e, o=o: e.tensor_mul(
                A(MF_ZPRE + o * 512, 512), A(MF_PRE + o * 512, 512),
                A(MF_XS + o * 512, 512)),
                reads=RA(MF_PRE + o * 512, 512) + RA(MF_XS + o * 512, 512),
                writes=RA(MF_ZPRE + o * 512, 512))

    def m_scan_chunk(b, c):
        for n in range(16):
            bco = MF_BREP0 if n % 2 == 0 else MF_BREP1
            p.dma("scalar", lambda e, c=c, n=n, bco=bco: e.dma_start(
                out=A(bco, 1024).rearrange("p (x t) -> p x t", x=2),
                in_=bc_scr[c % 2, n:n + 33:32, :].partition_broadcast(128)),
                reads=(f"bcscr{c % 2}",), writes=RA(bco, 1024),
                fam=f"brep{n % 2}")

            def brep4(base):
                return A(base, 512).rearrange("p (x t) -> p x t", x=1) \
                    .broadcast_to([128, 4, 512])

            for hf in range(2):
                po = MF_PALL + hf * 2048
                wo = MF_WALL + hf * 2048
                do = MF_PRE + hf * 2048
                uo = MF_ZPRE + hf * 2048
                ro = MF_RES + hf * 2048
                p.add("scalar", lambda e, n=n, po=po, do=do: e.activation(
                    A(po, 2048), A(do, 2048), AF.Exp, scale=-(n + 1.0)),
                    reads=RA(do, 2048), writes=RA(po, 2048))
                p.add("vector", lambda e, wo=wo, uo=uo, bco=bco: e.tensor_tensor(
                    A(wo, 2048).rearrange("p (x t) -> p x t", x=4),
                    A(uo, 2048).rearrange("p (x t) -> p x t", x=4),
                    brep4(bco), ALU.mult),
                    reads=RA(uo, 2048) + RA(bco, 512), writes=RA(wo, 2048))
                for i in range(4):
                    dc = hf * 4 + i
                    p.add("vector", lambda e, n=n, dc=dc, po=po, wo=wo, i=i:
                          e.tensor_tensor_scan(
                              A(MF_RES + dc * 512, 512), A(po + i * 512, 512),
                              A(wo + i * 512, 512),
                              af[:, MF_STATE + n * 8 + dc:
                                 MF_STATE + n * 8 + dc + 1],
                              ALU.mult, ALU.add),
                          reads=RA(po + i * 512, 512) + RA(wo + i * 512, 512) +
                                RA(MF_STATE, 128),
                          writes=RA(MF_RES + dc * 512, 512))
                p.add("vector", lambda e, ro=ro, hf=hf, bco=bco: e.tensor_tensor(
                    Bp(0, 128, MB_XN + hf * 2048, 2048).rearrange(
                        "p (x t) -> p x t", x=4),
                    A(ro, 2048).rearrange("p (x t) -> p x t", x=4),
                    brep4(bco + 512), ALU.mult),
                    reads=RA(ro, 2048) + RA(bco + 512, 512),
                    writes=RB(MB_XN + hf * 2048, 2048))
                for i in range(4):
                    dc = hf * 4 + i
                    p.add("tensor", lambda e, n=n, dc=dc, hf=hf, i=i: e.matmul(
                        PS[dc][:, 0:512], ident_bf.ap(),
                        Bp(0, 128, MB_XN + hf * 2048 + i * 512, 512),
                        start=(n == 0), stop=False),
                        reads=RB(MB_XN + hf * 2048 + i * 512, 512) + ("ident_bf",),
                        writes=(f"psb{dc}",))
            p.add("vector", lambda e, n=n: e.tensor_copy(
                af[:, MF_STATE + n * 8:MF_STATE + n * 8 + 8],
                af[:, MF_RES:MF_RES + 4096].rearrange(
                    "p (ci t) -> p ci t", t=512)[:, :, 511]),
                reads=RA(MF_RES, 4096),
                writes=RA(MF_STATE, 128))
        for dc in range(DC):
            p.add("vector", lambda e, dc=dc: e.tensor_scalar(
                Bp(0, 128, MB_XSBF + (dc % 2) * 512, 512),
                A(MF_XS + dc * 512, 512),
                mvecs[:, 96 + dc:97 + dc], None, ALU.mult),
                reads=RA(MF_XS + dc * 512, 512) + ("mvecs",),
                writes=RB(MB_XSBF + (dc % 2) * 512, 512))
            p.add("tensor", lambda e, dc=dc: e.matmul(
                PS[dc][:, 0:512], ident_bf.ap(),
                Bp(0, 128, MB_XSBF + (dc % 2) * 512, 512),
                start=False, stop=True),
                reads=RB(MB_XSBF + (dc % 2) * 512, 512) + ("ident_bf",),
                writes=(f"psb{dc}",))
            p.add("scalar", lambda e, dc=dc: e.activation(
                Bp(0, 128, MB_YBF + dc * 512, 512), PS[dc][:, 0:512], AF.Copy),
                reads=(f"psb{dc}",), writes=RB(MB_YBF + dc * 512, 512))

    def m_wout_chunk(b, c):
        t0 = c * MCH
        for k in range(16):
            slot = k % 2
            p.dma("sync", lambda e, k=k, slot=slot: e.dma_start(
                out=B_(MB_WROT + slot * 1024, 1024),
                in_=wbf["W_out"][k * 128:(k + 1) * 128, :]),
                reads=("wbf_W_out",), writes=RB(MB_WROT + slot * 1024, 1024))
            rhs_off = (MB_YBF + k * 512) if k < 8 else (MB_ZBF + (k - 8) * 512)
            for o in range(DC):
                p.add("tensor", lambda e, k=k, o=o, slot=slot, rhs_off=rhs_off:
                      e.matmul(PS[o][:, 0:512],
                               B_(MB_WROT + slot * 1024 + o * 128, 128),
                               Bp(0, 128, rhs_off, 512),
                               start=(k == 0), stop=(k == 15)),
                      reads=RB(MB_WROT + slot * 1024, 1024) + RB(rhs_off, 512),
                      writes=(f"psb{o}",))
        for o in range(DC):
            p.add("vector", lambda e, o=o: e.tensor_scalar(
                A(MF_RES + o * 512, 512), PS[o][:, 0:512],
                mvecs[:, 104 + o:105 + o], None, ALU.add),
                reads=(f"psb{o}", "mvecs"), writes=RA(MF_RES + o * 512, 512))
            p.dma("gpsimd", lambda e, o=o, b=b, t0=t0: e.dma_start(
                out=s_scr[o * 128:(o + 1) * 128, b * T + t0:b * T + t0 + 512],
                in_=A(MF_RES + o * 512, 512)),
                reads=RA(MF_RES + o * 512, 512), writes=(f"sst{b}_{c}_{o}",))

    def phase_mamba():
        p.dma("sync", lambda e: e.dma_start(
            out=Bp(0, 64, MB_WDT, 1024), in_=wbf["W_dt"]),
            reads=("wbf_W_dt",), writes=RB(MB_WDT, 1024))
        for k in range(DC):
            p.dma("sync", lambda e, k=k: e.dma_start(
                out=Bp(0, 128, MB_WXD + k * 128, 128),
                in_=wbf["W_xdbl"][k * 128:(k + 1) * 128, :]),
                reads=("wbf_W_xdbl",), writes=RB(MB_WXD + k * 128, 128),
                fam=f"wxd{k % 2}")
        for b in range(B_PER_CORE):
            p.add("vector", lambda e: e.memset(
                af[:, MF_STATE:MF_STATE + 128], 0.0),
                writes=RA(MF_STATE, 128))
            for c in range(NMC):
                m_ln_chunk(b, c)
                m_xz_chunk(b, c)
                m_conv_chunk(b, c)
                m_xdbl_dt_chunk(b, c)
                m_scan_chunk(b, c)
                m_wout_chunk(b, c)

    def phase_out_mamba():
        outs = []
        for b in range(B_PER_CORE):
            for o in range(DC):
                op = p.dma("gpsimd", lambda e, b=b, o=o: e.dma_start(
                    out=out_fm[b, o * 128:(o + 1) * 128, :],
                    in_=s_scr[o * 128:(o + 1) * 128, b * T:(b + 1) * T]),
                    reads=tuple(f"sst{b}_{c}_{o}" for c in range(NMC)),
                    writes=(f"oo{b}_{o}",))
                outs.append(op)
        p.add("gpsimd", lambda e: e.engine_nop(), deps=outs)



    BB_LAYERS = int(globals().get("BB_NLAYERS", 6))

    def phase_w_bb():
        for l in range(BB_LAYERS):
            cast_weight(Wqkv_t_d[l].rearrange("o p m -> (o p) m"),
                        wbf["Wqkv_t"][l].rearrange("o p m -> (o p) m"),
                        16 * 128, 1024, f"wbf_Wqkv{l}")
            cast_weight(Wqkv_v_d[l].rearrange("v c p m -> (v c p) m"),
                        wbf["Wqkv_v"][l].rearrange("v c p m -> (v c p) m"),
                        2 * 8 * 128, 512, f"wbf_Wqkvv{l}")
            cast_weight(Wo_d[l], wbf["Wo"][l], D, D, f"wbf_Wo{l}")
            cast_weight(W1_t_d[l].rearrange("ci p m -> (ci p) m"),
                        wbf["W1_t"][l].rearrange("ci p m -> (ci p) m"),
                        HC * 128, 2048, f"wbf_W1{l}")
            cast_weight(W2_d[l], wbf["W2"][l], HIDH, D, f"wbf_W2{l}")

    XBO = BF_XB

    def xb_off(dc, t):
        return XBO + dc * 2064 + 1 + t

    def bb_init(b):
        for dc in range(DC):
            p.add("vector", lambda e, dc=dc: e.memset(
                A(XBO + dc * 2064, 1), 0.0), writes=RA(XBO + dc * 2064, 1))
        for tt in range(4):
            t0 = tt * 512
            for dc in range(DC):
                p.dma("sync", lambda e, b=b, dc=dc, t0=t0: e.dma_start(
                    out=A(xb_off(dc, t0), 512),
                    in_=x_fm[b, dc * 128:(dc + 1) * 128, t0:t0 + 512]),
                    writes=RA(xb_off(dc, t0), 512))
            for dc in range(DC):
                p.add("vector", lambda e, dc=dc, t0=t0: e.tensor_copy(
                    Bp(0, 128, FB_M + dc * 512, 512), A(xb_off(dc, t0), 512)),
                    reads=RA(xb_off(dc, t0), 512),
                    writes=RB(FB_M + dc * 512, 512))
                p.add("scalar", lambda e, dc=dc, t0=t0: e.activation(
                    Bp(0, 128, FB_XN + dc * 512, 512), A(xb_off(dc, t0), 512),
                    AF.Square),
                    reads=RA(xb_off(dc, t0), 512),
                    writes=RB(FB_XN + dc * 512, 512))
            for dc in range(DC):
                p.add("tensor", lambda e, dc=dc: e.matmul(
                    PS[5][0:1, 0:512], ones_bf.ap(), Bp(0, 128, FB_M + dc * 512, 512),
                    start=(dc == 0), stop=(dc == DC - 1)),
                    reads=RB(FB_M + dc * 512, 512) + ("ones_bf",), writes=("psb5",))
                p.add("tensor", lambda e, dc=dc: e.matmul(
                    PS[6][0:1, 0:512], ones_bf.ap(), Bp(0, 128, FB_XN + dc * 512, 512),
                    start=(dc == 0), stop=(dc == DC - 1)),
                    reads=RB(FB_XN + dc * 512, 512) + ("ones_bf",), writes=("psb6",))
            ln_rows_512()
            p.add("tensor", lambda e: e.matmul(
                PS[5][:, 0:512], onesrow_f[0:1, :], Ap(0, 1, SC + 1024, 512),
                start=True, stop=True),
                reads=RA(SC + 1024, 512) + ("onesrow_f",), writes=("psb5",))
            p.add("tensor", lambda e: e.matmul(
                PS[6][:, 0:512], onesrow_f[0:1, :], Ap(0, 1, SC, 512),
                start=True, stop=True),
                reads=RA(SC, 512) + ("onesrow_f",), writes=("psb6",))
            for dc in range(DC):
                p.add("vector", lambda e, dc=dc, t0=t0: e.tensor_mul(
                    A(xb_off(dc, t0), 512), A(xb_off(dc, t0), 512), PS[5][:, 0:512]),
                    reads=("psb5",) + RA(xb_off(dc, t0), 512),
                    writes=RA(xb_off(dc, t0), 512))
                p.add("vector", lambda e, dc=dc, t0=t0: e.tensor_tensor(
                    A(xb_off(dc, t0), 512), A(xb_off(dc, t0), 512), PS[6][:, 0:512],
                    ALU.subtract),
                    reads=("psb6",) + RA(xb_off(dc, t0), 512),
                    writes=RA(xb_off(dc, t0), 512))

    def ln_rows_512():
        p.add("vector", lambda e: e.tensor_scalar(
            Ap(0, 1, SC, 512), PS[5][0:1, 0:512], 1.0 / D, None, ALU.mult),
            reads=("psb5",), writes=RA(SC, 512))
        p.add("scalar", lambda e: e.activation(
            Ap(0, 1, SC + 512, 512), Ap(0, 1, SC, 512), AF.Square),
            reads=RA(SC, 512), writes=RA(SC + 512, 512))
        p.add("vector", lambda e: e.scalar_tensor_tensor(
            Ap(0, 1, SC + 1024, 512), PS[6][0:1, 0:512], 1.0 / D,
            Ap(0, 1, SC + 512, 512), ALU.mult, ALU.subtract),
            reads=("psb6",) + RA(SC + 512, 512), writes=RA(SC + 1024, 512))
        p.add("scalar", lambda e: e.activation(
            Ap(0, 1, SC + 1536, 512), Ap(0, 1, SC + 1024, 512), AF.Ln,
            bias=small[0:1, 0:1]),
            reads=RA(SC + 1024, 512) + ("eps",), writes=RA(SC + 1536, 512))
        p.add("scalar", lambda e: e.activation(
            Ap(0, 1, SC + 1024, 512), Ap(0, 1, SC + 1536, 512), AF.Exp,
            scale=-0.5),
            reads=RA(SC + 1536, 512), writes=RA(SC + 1024, 512))
        p.add("vector", lambda e: e.tensor_mul(
            Ap(0, 1, SC, 512), Ap(0, 1, SC, 512), Ap(0, 1, SC + 1024, 512)),
            reads=RA(SC, 512) + RA(SC + 1024, 512), writes=RA(SC, 512))

    def bb_layer_setup(l):
        p.dma("sync", lambda e, l=l: e.dma_start(
            out=lvecs[:, :], in_=lvecs_d[l]), writes=("lvecs",))
        p.dma("sync", lambda e, l=l: e.dma_start(
            out=b1ab[:, 0:HC], in_=b1a_d[l]), writes=("b1ab_a",))
        p.dma("sync", lambda e, l=l: e.dma_start(
            out=b1ab[:, HC:2 * HC], in_=b1g_d[l]), writes=("b1ab_g",))
        p.dma("sync", lambda e, l=l: e.dma_start(
            out=lrows2[:, :], in_=lrows2_d[l]), writes=("lrows2",))
        p.add("vector", lambda e: e.tensor_copy(lrows2_bf[:, :], lrows2[:, :]),
              reads=("lrows2",), writes=("lrows2_bf",))
        for i, nm in enumerate(("cs_q", "ss_q", "cs_k", "ss_k")):
            p.dma("sync", lambda e, nm=nm, i=i: e.dma_start(
                out=ab[:, AB_TAB + i * T:AB_TAB + (i + 1) * T], in_=tabs_d[nm]),
                writes=RB(AB_TAB + i * T, T))
        p.add("vector", lambda e: e.tensor_scalar(
            ab[:, AB_TAB:AB_TAB + T], ab[:, AB_TAB:AB_TAB + T],
            lvecs[:, 0:1], 8.0, ALU.mult, ALU.mult),
            reads=RB(AB_TAB, T) + ("lvecs",), writes=RB(AB_TAB, T))
        p.add("vector", lambda e: e.tensor_scalar(
            ab[:, AB_TAB + 2 * T:AB_TAB + 3 * T], ab[:, AB_TAB + 2 * T:AB_TAB + 3 * T],
            lvecs[:, 8:9], None, ALU.mult),
            reads=RB(AB_TAB + 2 * T, T) + ("lvecs",), writes=RB(AB_TAB + 2 * T, T))
        p.add("vector", lambda e: e.tensor_scalar(
            rq_bf[:, :], rperm[:, :], lvecs[:, 0:1], 8.0, ALU.mult, ALU.mult),
            reads=("rperm", "lvecs"), writes=("rq_bf",))
        p.add("vector", lambda e: e.tensor_scalar(
            rk_bf[:, :], rperm[:, :], lvecs[:, 8:9], None, ALU.mult),
            reads=("rperm", "lvecs"), writes=("rk_bf",))

    def xbf_slice(base, dc, shifted_halo=True):
        off = base + dc * 520 + (0 if dc >= 4 else 1)
        return off

    def bb_attn_group(b, l, g):
        t0 = g * 512
        if g == 0:
            p.add("vector", lambda e: e.memset(
                af[0:1, 0:1] if False else ab[:, BB_XBF:BB_XBF + 1], 0.0),
                writes=RB(BB_XBF, 1))
            for dc in range(DC):
                p.add("vector", lambda e, dc=dc: e.memset(
                    ab[:, BB_XBF + dc * 520:BB_XBF + dc * 520 + 1], 0.0),
                    writes=RB(BB_XBF + dc * 520, 1))
        else:
            p.add("vector", lambda e: e.tensor_copy(
                ab[:, BB_XBF:BB_XBF + 8 * 520].rearrange(
                    "p (c t) -> p c t", t=520)[:, :, 0:1].squeeze(2)
                if False else
                ab[:, BB_XBF:BB_XBF + 8 * 520].rearrange(
                    "p (c t) -> p c t", t=520)[:, :, 0],
                ab[:, BB_XBF:BB_XBF + 8 * 520].rearrange(
                    "p (c t) -> p c t", t=520)[:, :, 512]),
                reads=RB(BB_XBF, 8 * 520), writes=RB(BB_XBF, 8 * 520))
        for dc in range(DC):
            p.add("vector", lambda e, dc=dc, t0=t0: e.tensor_copy(
                Bp(0, 128, BB_XBF + dc * 520 + 1, 512),
                A(xb_off(dc, t0), 512)),
                reads=RA(xb_off(dc, t0), 512),
                writes=RB(BB_XBF + dc * 520 + 1, 512))
        SCR = BB_AFM

        def qk_mms(o):
            slot = o % 2
            pb = o % 2
            p.dma("sync", lambda e, o=o, l=l, slot=slot: e.dma_start(
                out=B_(BB_WROT + slot * 1024, 1024),
                in_=wbf["Wqkv_t"][l, o]),
                reads=(f"wbf_Wqkv{l}",), writes=RB(BB_WROT + slot * 1024, 1024))
            for dc in range(DC):
                p.add("tensor", lambda e, dc=dc, slot=slot, pb=pb: e.matmul(
                    PS[pb][:, 0:512],
                    B_(BB_WROT + slot * 1024 + dc * 128, 128),
                    Bp(0, 128, xbf_slice(BB_XBF, dc), 512),
                    start=(dc == 0), stop=(dc == DC - 1)),
                    reads=RB(BB_WROT + slot * 1024, 1024) +
                          RB(BB_XBF + dc * 520, 520),
                    writes=(f"psb{pb}",))

        def qk_post(o):
            slot = o % 2
            pb = o % 2
            isq = o < 8
            sqo = BB_PT + slot * 1024
            qro = BB_PT + slot * 1024 + 512
            sb_ = 2 if slot == 0 else 4
            rb_ = 3 if slot == 0 else 5
            lnro = SC + 1536 + slot * 512
            p.add("scalar", lambda e, pb=pb, sqo=sqo: e.activation(
                Bp(0, 128, sqo, 512), PS[pb][:, 0:512], AF.Square),
                reads=(f"psb{pb}",), writes=RB(sqo, 512))
            p.add("scalar", lambda e, pb=pb, qro=qro: e.activation(
                Bp(0, 128, qro, 512), PS[pb][:, 0:512], AF.Copy),
                reads=(f"psb{pb}",), writes=RB(qro, 512))
            p.add("tensor", lambda e, sqo=sqo, sb_=sb_: e.matmul(
                PS[sb_][0:2, 0:512], h2.ap(), Bp(0, 128, sqo, 512),
                start=True, stop=True),
                reads=RB(sqo, 512) + ("h2",), writes=(f"psb{sb_}",))
            p.add("scalar", lambda e, sb_=sb_, lnro=lnro: e.activation(
                Ap(0, 2, lnro, 512), PS[sb_][0:2, 0:512], AF.Ln),
                reads=(f"psb{sb_}",), writes=RA(lnro, 512))
            p.add("scalar", lambda e, slot=slot, lnro=lnro: e.activation(
                qkrs[0:2, slot * 512:slot * 512 + 512], Ap(0, 2, lnro, 512),
                AF.Exp, scale=-0.5),
                reads=RA(lnro, 512), writes=(f"qkrs{slot}",))
            p.add("tensor", lambda e, slot=slot, rb_=rb_: e.matmul(
                PS[rb_][:, 0:512], h2t.ap(), qkrs[0:2, slot * 512:slot * 512 + 512],
                start=True, stop=True),
                reads=(f"qkrs{slot}", "h2t"), writes=(f"psb{rb_}",))
            p.add("vector", lambda e, rb_=rb_, qro=qro: e.tensor_mul(
                Bp(0, 128, qro, 512), Bp(0, 128, qro, 512), PS[rb_][:, 0:512]),
                reads=(f"psb{rb_}",) + RB(qro, 512), writes=RB(qro, 512))
            rot = rq_bf if isq else rk_bf
            rotkey = "rq_bf" if isq else "rk_bf"
            p.add("tensor", lambda e, rot=rot, rb_=rb_, qro=qro: e.matmul(
                PS[rb_][:, 0:512], rot.ap(), Bp(0, 128, qro, 512),
                start=True, stop=True),
                reads=RB(qro, 512) + (rotkey,), writes=(f"psb{rb_}",))
            tab = AB_TAB + (0 if isq else 2 * T) + t0
            p.add("vector", lambda e, tab=tab, qro=qro, slot=slot: e.tensor_mul(
                Bp(0, 128, SCR + slot * 1024, 512), Bp(0, 128, qro, 512),
                B_(tab, 512)),
                reads=RB(qro, 512) + RB(tab, 512),
                writes=RB(SCR + slot * 1024, 512))
            stab = AB_TAB + (T if isq else 3 * T) + t0
            p.add("vector", lambda e, stab=stab, rb_=rb_, slot=slot: e.tensor_mul(
                Bp(0, 128, SCR + slot * 1024 + 512, 512), PS[rb_][:, 0:512],
                B_(stab, 512)),
                reads=(f"psb{rb_}",) + RB(stab, 512),
                writes=RB(SCR + slot * 1024 + 512, 512))
            dst = (BB_Q + o * 512) if isq else (BB_KCUR + (o - 8) * 512)
            p.add("vector", lambda e, dst=dst, slot=slot: e.tensor_add(
                Bp(0, 128, dst, 512), Bp(0, 128, SCR + slot * 1024, 512),
                Bp(0, 128, SCR + slot * 1024 + 512, 512)),
                reads=RB(SCR + slot * 1024, 1024),
                writes=RB(dst, 512))

        for o in range(16):
            qk_mms(o)
            if o > 0:
                qk_post(o - 1)
        qk_post(15)
        for vb in range(2):
            for dc in range(DC):
                slot = dc % 2
                p.dma("sync", lambda e, vb=vb, dc=dc, l=l, slot=slot: e.dma_start(
                    out=B_(BB_WROT + slot * 1024, 512),
                    in_=wbf["Wqkv_v"][l, vb, dc]),
                    reads=(f"wbf_Wqkvv{l}",), writes=RB(BB_WROT + slot * 1024, 512))
                for ts_ in range(4):
                    p.add("tensor", lambda e, vb=vb, dc=dc, ts_=ts_, slot=slot:
                          e.matmul(PS[ts_ * 2 + vb][:, 0:512],
                                   Bp(0, 128, xbf_slice(BB_XBF, dc) + ts_ * 128, 128),
                                   B_(BB_WROT + slot * 1024, 512),
                                   start=(dc == 0), stop=(dc == DC - 1)),
                          reads=RB(BB_WROT + slot * 1024, 512) +
                                RB(BB_XBF + dc * 520, 520),
                          writes=(f"psb{ts_ * 2 + vb}",))
        for ts_ in range(4):
            for vb in range(2):
                p.add("scalar", lambda e, ts_=ts_, vb=vb: e.activation(
                    Bp(0, 128, BB_VCUR + ts_ * 1024 + vb * 512, 512),
                    PS[ts_ * 2 + vb][:, 0:512], AF.Copy),
                    reads=(f"psb{ts_ * 2 + vb}",),
                    writes=RB(BB_VCUR + ts_ * 1024 + vb * 512, 512))
        for wloc in range(2):
            w = 2 * g + wloc
            bb_window(b, l, g, wloc, w)
        p.add("vector", lambda e: e.tensor_copy(
            ab[:, BB_K:BB_K + 2048].rearrange("p (c t) -> p c t", t=256),
            ab[:, BB_KCUR:BB_KCUR + 4096].rearrange(
                "p (c t) -> p c t", t=512)[:, :, 256:512]),
            reads=RB(BB_KCUR, 4096), writes=RB(BB_K, 2048))
        p.add("vector", lambda e: e.tensor_copy(
            B_(BB_V, 2048), B_(BB_VCUR + 2048, 2048)),
            reads=RB(BB_VCUR + 2048, 2048), writes=RB(BB_V, 2048))

    def bb_window(b, l, g, wloc, w):
        t0 = w * 256
        kts = (2, 3) if w == 0 else (0, 1, 2, 3)

        def ksrc(kt):
            if kt < 2:
                if wloc == 0:
                    return BB_K + kt * 128, 256, RB(BB_K, 2048)
                return BB_KCUR + kt * 128, 512, RB(BB_KCUR, 4096)
            tloc = 2 * wloc + (kt - 2) - (0 if wloc == 0 else 2)
            tsub = 2 * wloc + (kt - 2)
            return BB_KCUR + tsub * 128, 512, RB(BB_KCUR, 4096)

        def vsrc(kt):
            if kt < 2:
                if wloc == 0:
                    return BB_V + kt * 1024, RB(BB_V, 2048)
                return BB_VCUR + kt * 1024, RB(BB_VCUR, 4096)
            tsub = 2 * wloc + (kt - 2)
            return BB_VCUR + tsub * 1024, RB(BB_VCUR, 4096)

        def w_scores(h):
            par = h % 2
            ch = h // 2
            sp_lo = PS[0] if par == 0 else PS[2]
            sp_hi = PS[1] if par == 0 else PS[3]
            for kt in kts:
                koff, kstride, kregs = ksrc(kt)
                sp = sp_lo if kt < 2 else sp_hi
                scol = (kt % 2) * 256
                p.add("tensor", lambda e, koff=koff, kstride=kstride, par=par,
                      ch=ch, wloc=wloc, sp=sp, scol=scol, kt=kt: e.matmul(
                          sp[:, scol:scol + 256],
                          ab[par * 64:par * 64 + 64,
                             koff + ch * (kstride if True else 0) * 1:
                             koff + ch * kstride + 128]
                          if False else
                          ab[par * 64:par * 64 + 64,
                             koff + ch * kstride:koff + ch * kstride + 128],
                          ab[par * 64:par * 64 + 64,
                             BB_Q + ch * 512 + wloc * 256:
                             BB_Q + ch * 512 + wloc * 256 + 256],
                          start=True, stop=True),
                      reads=kregs + RB(BB_Q + ch * 512, 512),
                      writes=(f"psb{0 if sp is PS[0] else (1 if sp is PS[1] else (2 if sp is PS[2] else 3))}",))
            for sp, bank, lohi in ((sp_lo, 0 if par == 0 else 2, 0),
                                   (sp_hi, 1 if par == 0 else 3, 1)):
                if w == 0 and lohi == 0:
                    continue
                p.add("vector", lambda e, sp=sp, lohi=lohi: e.tensor_tensor(
                    sp[:, 0:512], sp[:, 0:512],
                    maskT[:, lohi * 512:lohi * 512 + 512], ALU.add),
                    reads=(f"psb{bank}", "maskT"), writes=(f"psb{bank}",))
            pts = BB_PT + par * 1024
            if w == 0:
                p.add("vector", lambda e, pts=pts: e.memset(
                    B_(pts, 512), 0.0), writes=RB(pts, 512))
                p.add("scalar", lambda e, sp=sp_hi, pts=pts,
                      bank=1 if par == 0 else 3: e.activation(
                          B_(pts + 512, 512), sp[:, 0:512], AF.Exp),
                      reads=(f"psb{1 if par == 0 else 3}",),
                      writes=RB(pts + 512, 512))
            else:
                p.add("scalar", lambda e, sp=sp_lo, pts=pts,
                      bank=0 if par == 0 else 2: e.activation(
                          B_(pts, 512), sp[:, 0:512], AF.Exp),
                      reads=(f"psb{0 if par == 0 else 2}",), writes=RB(pts, 512))
                p.add("scalar", lambda e, sp=sp_hi, pts=pts,
                      bank=1 if par == 0 else 3: e.activation(
                          B_(pts + 512, 512), sp[:, 0:512], AF.Exp),
                      reads=(f"psb{1 if par == 0 else 3}",),
                      writes=RB(pts + 512, 512))
        def w_sumspv(h):
            par = h % 2
            ch = h // 2
            pts = BB_PT + par * 1024
            for i, kt in enumerate(kts):
                p.add("tensor", lambda e, kt=kt, par=par, pts=pts,
                      first=(i == 0), last=(i == len(kts) - 1): e.matmul(
                          PS[7][0:1, par * 256:par * 256 + 256], ones_bf.ap(),
                          B_(pts + kt * 256, 256),
                          start=first, stop=last),
                      reads=RB(pts, 1024) + ("ones_bf",), writes=("psb7",))
            for i, kt in enumerate(kts):
                voff, vregs = vsrc(kt)
                p.add("tensor", lambda e, voff=voff, h=h, par=par, pts=pts, kt=kt,
                      first=(i == 0), last=(i == len(kts) - 1): e.matmul(
                          PS[4 + par][par * 64:par * 64 + 64, 0:256],
                          B_(voff + h * 64, 64), B_(pts + kt * 256, 256),
                          start=first, stop=last),
                      reads=vregs + RB(pts, 1024), writes=(f"psb{4 + par}",))
        def w_div(h):
            par = h % 2
            ch = h // 2
            if True:
                p.add("scalar", lambda e: e.activation(
                    Ap(0, 1, SC + 1536, 512), PS[7][0:1, 0:512], AF.Ln),
                    reads=("psb7",), writes=RA(SC + 1536, 512))
                p.add("scalar", lambda e: e.activation(
                    rs_row[0:1, 0:512], Ap(0, 1, SC + 1536, 512), AF.Exp,
                    scale=-1.0),
                    reads=RA(SC + 1536, 512), writes=("rs_row",))
                p.add("tensor", lambda e: e.matmul(
                    PS[6][0:64, 0:256], onesrow_bf[0:1, 0:64],
                    rs_row[0:1, 0:256], start=True, stop=True),
                    reads=("rs_row", "onesrow_bf"), writes=("psb6",))
                p.add("tensor", lambda e: e.matmul(
                    PS[6][64:128, 0:256], onesrow_bf[0:1, 0:64],
                    rs_row[0:1, 256:512], start=True, stop=True),
                    reads=("rs_row", "onesrow_bf"), writes=("psb6",))
                p.add("scalar", lambda e: e.activation(
                    Bp(0, 128, BB_MISC, 256), PS[6][:, 0:256], AF.Copy),
                    reads=("psb6",), writes=RB(BB_MISC, 256))
                for pp in range(2):
                    p.add("vector", lambda e, pp=pp, ch=ch: e.tensor_mul(
                        Bp(pp * 64, 64, BB_AFM + ch * 256, 256),
                        PS[4 + pp][pp * 64:pp * 64 + 64, 0:256],
                        Bp(pp * 64, 64, BB_MISC, 256)),
                        reads=(f"psb{4 + pp}",) + RB(BB_MISC, 256),
                        writes=RB(BB_AFM + ch * 256, 256))

        for h in range(HEADS):
            w_scores(h)
            if h > 0:
                w_sumspv(h - 1)
                if h % 2 == 0 and h >= 2:
                    w_div(h - 1)
        w_sumspv(HEADS - 1)
        w_div(HEADS - 1)
        for k in range(DC):
            slot = k % 2
            p.dma("sync", lambda e, k=k, l=l, slot=slot: e.dma_start(
                out=B_(BB_WOROT + slot * 1024, 1024),
                in_=wbf["Wo"][l][k * 128:(k + 1) * 128, :]),
                reads=(f"wbf_Wo{l}",), writes=RB(BB_WOROT + slot * 1024, 1024))
            for o in range(DC):
                p.add("tensor", lambda e, k=k, o=o, slot=slot: e.matmul(
                    PS[o][:, 256:512],
                    B_(BB_WOROT + slot * 1024 + o * 128, 128),
                    Bp(0, 128, BB_AFM + k * 256, 256),
                    start=(k == 0), stop=(k == DC - 1)),
                    reads=RB(BB_WOROT + slot * 1024, 1024) +
                          RB(BB_AFM + k * 256, 256),
                    writes=(f"psb{o}",))
        for o in range(DC):
            p.add("vector", lambda e, o=o, t0=t0: e.tensor_add(
                A(xb_off(o, t0), 256), A(xb_off(o, t0), 256), PS[o][:, 256:512]),
                reads=(f"psb{o}",) + RA(xb_off(o, t0), 256),
                writes=RA(xb_off(o, t0), 256))

    def bb_ffn_tt(b, l, tt):
        t0 = tt * 512
        if tt == 0:
            for dc in range(DC):
                p.add("vector", lambda e, dc=dc: e.memset(
                    ab[:, FB_XBF2 + dc * 520:FB_XBF2 + dc * 520 + 1], 0.0),
                    writes=RB(FB_XBF2 + dc * 520, 1))
        else:
            p.add("vector", lambda e: e.tensor_copy(
                ab[:, FB_XBF2:FB_XBF2 + 8 * 520].rearrange(
                    "p (c t) -> p c t", t=520)[:, :, 0],
                ab[:, FB_XBF2:FB_XBF2 + 8 * 520].rearrange(
                    "p (c t) -> p c t", t=520)[:, :, 512]),
                reads=RB(FB_XBF2, 8 * 520), writes=RB(FB_XBF2, 8 * 520))
        for dc in range(DC):
            p.add("vector", lambda e, dc=dc, t0=t0: e.tensor_copy(
                Bp(0, 128, FB_XBF2 + dc * 520 + 1, 512), A(xb_off(dc, t0), 512)),
                reads=RA(xb_off(dc, t0), 512),
                writes=RB(FB_XBF2 + dc * 520 + 1, 512))
        for dc in range(DC):
            p.add("scalar", lambda e, dc=dc: e.activation(
                Bp(0, 128, FB_M + dc * 512, 512),
                Bp(0, 128, xbf_slice(FB_XBF2, dc), 512), AF.Square),
                reads=RB(FB_XBF2 + dc * 520, 520), writes=RB(FB_M + dc * 512, 512))
        for dc in range(DC):
            p.add("tensor", lambda e, dc=dc: e.matmul(
                PS[5][0:1, 0:512], ones_bf.ap(),
                Bp(0, 128, xbf_slice(FB_XBF2, dc), 512),
                start=(dc == 0), stop=(dc == DC - 1)),
                reads=RB(FB_XBF2 + dc * 520, 520) + ("ones_bf",), writes=("psb5",))
            p.add("tensor", lambda e, dc=dc: e.matmul(
                PS[6][0:1, 0:512], ones_bf.ap(), Bp(0, 128, FB_M + dc * 512, 512),
                start=(dc == 0), stop=(dc == DC - 1)),
                reads=RB(FB_M + dc * 512, 512) + ("ones_bf",), writes=("psb6",))
        ln_rows_512()
        p.add("scalar", lambda e: e.activation(
            murs_row[0:1, 0:512], Ap(0, 1, SC, 512), AF.Copy),
            reads=RA(SC, 512), writes=("mursrow0",))
        p.add("scalar", lambda e: e.activation(
            rs_row[0:1, 0:512], Ap(0, 1, SC + 1024, 512), AF.Copy),
            reads=RA(SC + 1024, 512), writes=("rs_row",))
        p.add("tensor", lambda e: e.matmul(
            PS[5][:, 0:512], onesrow_bf[0:1, :], rs_row[0:1, 0:512],
            start=True, stop=True),
            reads=("rs_row", "onesrow_bf"), writes=("psb5",))
        for dc in range(DC):
            p.add("tensor", lambda e, dc=dc: e.matmul(
                PS[6][:, 0:512], lrows2_bf[0:2, dc * 128:(dc + 1) * 128],
                murs_row[0:2, 0:512], start=True, stop=True),
                reads=("mursrow0", "mursrow1", "lrows2_bf"), writes=("psb6",))
            p.add("vector", lambda e, dc=dc: e.tensor_mul(
                Ap(0, 128, BF_QN, 512), Bp(0, 128, xbf_slice(FB_XBF2, dc), 512),
                PS[5][:, 0:512]),
                reads=("psb5",) + RB(FB_XBF2 + dc * 520, 520),
                writes=RA(BF_QN, 512))
            p.add("vector", lambda e, dc=dc: e.scalar_tensor_tensor(
                Bp(0, 128, FB_XN + dc * 512, 512), Ap(0, 128, BF_QN, 512),
                lvecs[:, 16 + dc:17 + dc], PS[6][:, 0:512],
                ALU.mult, ALU.subtract),
                reads=RA(BF_QN, 512) + ("psb6", "lvecs"),
                writes=RB(FB_XN + dc * 512, 512))
        def w1_pair(ci):
            slot = ci % 2
            ab_ = ci % 2
            p.dma("sync", lambda e, ci=ci, l=l, slot=slot: e.dma_start(
                out=B_(FB_W1 + slot * 2048, 2048),
                in_=wbf["W1_t"][l, ci]),
                reads=(f"wbf_W1{l}",), writes=RB(FB_W1 + slot * 2048, 2048))
            for dc in range(DC):
                p.add("tensor", lambda e, ci=ci, dc=dc, slot=slot, ab_=ab_:
                      e.matmul(
                          PS[ab_ * 2][:, 0:512],
                          B_(FB_W1 + slot * 2048 + dc * 256, 128),
                          Bp(0, 128, FB_XN + dc * 512, 512),
                          start=(dc == 0), stop=(dc == DC - 1)),
                      reads=RB(FB_W1 + slot * 2048, 2048) +
                            RB(FB_XN + dc * 512, 512),
                      writes=(f"psb{ab_ * 2}",))
                p.add("tensor", lambda e, ci=ci, dc=dc, slot=slot, ab_=ab_:
                      e.matmul(
                          PS[ab_ * 2 + 1][:, 0:512],
                          B_(FB_W1 + slot * 2048 + dc * 256 + 128, 128),
                          Bp(0, 128, FB_XN + dc * 512, 512),
                          start=(dc == 0), stop=(dc == DC - 1)),
                      reads=RB(FB_W1 + slot * 2048, 2048) +
                            RB(FB_XN + dc * 512, 512),
                      writes=(f"psb{ab_ * 2 + 1}",))

        def w1_evac(ci):
            ab_ = ci % 2
            gsl = ci % 2
            p.add("scalar", lambda e, ci=ci, gsl=gsl, ab_=ab_: e.activation(
                Bp(0, 128, FB_GELU + gsl * 512, 512), PS[ab_ * 2 + 1][:, 0:512],
                AF.Gelu, bias=b1ab[:, HC + ci:HC + ci + 1]),
                reads=(f"psb{ab_ * 2 + 1}", "b1ab_g"),
                writes=RB(FB_GELU + gsl * 512, 512))
            p.add("vector", lambda e, ci=ci, gsl=gsl, ab_=ab_:
                  e.scalar_tensor_tensor(
                      Bp(0, 128, FB_M + ci * 512, 512), PS[ab_ * 2][:, 0:512],
                      b1ab[:, ci:ci + 1], Bp(0, 128, FB_GELU + gsl * 512, 512),
                      ALU.add, ALU.mult),
                  reads=(f"psb{ab_ * 2}", "b1ab_a") + RB(FB_GELU + gsl * 512, 512),
                  writes=RB(FB_M + ci * 512, 512))

        def w2_half(ci, orange):
            slot = ci % 2
            p.dma("sync", lambda e, ci=ci, l=l, slot=slot, tag=orange[0]:
                  e.dma_start(
                      out=B_(FB_W2 + slot * 1024, 1024),
                      in_=wbf["W2"][l][ci * 128:(ci + 1) * 128, :]),
                  reads=(f"wbf_W2{l}",),
                  writes=RB(FB_W2 + slot * 1024, 1024), fam=f"w2h{orange[0]}{slot}")
            for o in orange:
                bank = 4 + (o % 4)
                p.add("tensor", lambda e, ci=ci, o=o, slot=slot, bank=bank:
                      e.matmul(
                          PS[bank][:, 0:512],
                          B_(FB_W2 + slot * 1024 + o * 128, 128),
                          Bp(0, 128, FB_M + ci * 512, 512),
                          start=(ci == 0), stop=(ci == HC - 1)),
                      reads=RB(FB_W2 + slot * 1024, 1024) +
                            RB(FB_M + ci * 512, 512),
                      writes=(f"psb{bank}",))

        for ci in range(HC):
            w1_pair(ci)
            w1_evac(ci)
            if ci > 0:
                w2_half(ci - 1, range(0, 4))
        w2_half(HC - 1, range(0, 4))
        for o in range(0, 4):
            p.add("vector", lambda e, o=o, t0=t0: e.scalar_tensor_tensor(
                A(xb_off(o, t0), 512), PS[4 + (o % 4)][:, 0:512],
                lvecs[:, 32 + o:33 + o], A(xb_off(o, t0), 512),
                ALU.add, ALU.add),
                reads=(f"psb{4 + (o % 4)}", "lvecs") + RA(xb_off(o, t0), 512),
                writes=RA(xb_off(o, t0), 512))
        for ci in range(HC):
            w2_half(ci, range(4, 8))
        for o in range(4, 8):
            p.add("vector", lambda e, o=o, t0=t0: e.scalar_tensor_tensor(
                A(xb_off(o, t0), 512), PS[4 + (o % 4)][:, 0:512],
                lvecs[:, 32 + o:33 + o], A(xb_off(o, t0), 512),
                ALU.add, ALU.add),
                reads=(f"psb{4 + (o % 4)}", "lvecs") + RA(xb_off(o, t0), 512),
                writes=RA(xb_off(o, t0), 512))

    def bb_combine(b):
        outs = []
        for tt in range(4):
            t0 = tt * 512
            for dc in range(DC):
                p.dma("sync", lambda e, b=b, dc=dc, t0=t0: e.dma_start(
                    out=Ap(0, 128, BF_QN, 512),
                    in_=x_fm[b, dc * 128:(dc + 1) * 128, t0:t0 + 512]),
                    writes=RA(BF_QN, 512))
                p.dma("scalar", lambda e, b=b, dc=dc, t0=t0: e.dma_start(
                    out=Ap(0, 128, BF_QN + 512, 512),
                    in_=s_scr[dc * 128:(dc + 1) * 128, b * T + t0:b * T + t0 + 512]),
                    reads=tuple(f"sst{b}_{c}_{dc}" for c in range(NMC)),
                    writes=RA(BF_QN + 512, 512))
                p.add("scalar", lambda e, dc=dc, t0=t0: e.activation(
                    Ap(0, 128, BF_QN + 1024, 512), A(xb_off(dc, t0), 512), AF.Silu),
                    reads=RA(xb_off(dc, t0), 512), writes=RA(BF_QN + 1024, 512))
                p.add("vector", lambda e: e.tensor_mul(
                    Ap(0, 128, BF_QN + 1024, 512), Ap(0, 128, BF_QN + 1024, 512),
                    Ap(0, 128, BF_QN + 512, 512)),
                    reads=RA(BF_QN + 512, 512) + RA(BF_QN + 1024, 512),
                    writes=RA(BF_QN + 1024, 512))
                p.add("vector", lambda e: e.tensor_add(
                    Ap(0, 128, BF_QN + 1024, 512), Ap(0, 128, BF_QN + 1024, 512),
                    Ap(0, 128, BF_QN, 512)),
                    reads=RA(BF_QN, 512) + RA(BF_QN + 1024, 512),
                    writes=RA(BF_QN + 1024, 512))
                op = p.dma("gpsimd", lambda e, b=b, dc=dc, t0=t0: e.dma_start(
                    out=out_fm[b, dc * 128:(dc + 1) * 128, t0:t0 + 512],
                    in_=Ap(0, 128, BF_QN + 1024, 512)),
                    reads=RA(BF_QN + 1024, 512), writes=(f"oo{b}_{dc}",))
                outs.append(op)
        return outs

    def bb_out_xb(b):
        outs = []
        for dc in range(DC):
            op = p.dma("gpsimd", lambda e, b=b, dc=dc: e.dma_start(
                out=out_fm[b, dc * 128:(dc + 1) * 128, :],
                in_=A(xb_off(dc, 0), T)),
                reads=RA(xb_off(dc, 0), T), writes=(f"oo{b}_{dc}",))
            outs.append(op)
        return outs

    def phase_branchB(b, milestone_out):
        bb_init(b)
        for l in range(BB_LAYERS):
            bb_layer_setup(l)
            for g in range(4):
                bb_attn_group(b, l, g)
            for tt in range(4):
                bb_ffn_tt(b, l, tt)
        if milestone_out == "xb":
            return bb_out_xb(b)
        return bb_combine(b)

    phase_consts()
    if MILESTONE == "mamba":
        phase_w()
        phase_mamba()
        phase_out_mamba()
    elif MILESTONE == "bb":
        phase_w_bb()
        fin = []
        for b in range(B_PER_CORE):
            fin += phase_branchB(b, "xb")
        p.add("gpsimd", lambda e: e.engine_nop(), deps=fin)
    else:
        phase_w()
        phase_mamba()
        phase_w_bb()
        fin = []
        for b in range(B_PER_CORE):
            fin += phase_branchB(b, "combine")
        p.add("gpsimd", lambda e: e.engine_nop(), deps=fin)

    n_ops = p.finalize_and_emit()
    return nc, n_ops, (dbg_state["map"], dbg_state["mapb"])


def kernel(**inputs):
    consts = _host_consts()
    w = _prep_weights(inputs)
    x = np.asarray(inputs["L0_attn_x"])
    nc, n_ops, dbg_maps = build_program()
    kernel.dbg_map, kernel.dbg_mapb = dbg_maps

    in_maps = []
    for core in range(N_CORES):
        xs = x[core * B_PER_CORE:(core + 1) * B_PER_CORE]
        x_fmh = np.ascontiguousarray(xs.transpose(0, 2, 1))
        m = dict(x_fm=x_fmh,
                 W_in=w["W_in"], W_out=w["W_out"], W_dt=w["W_dt"],
                 W_xdbl=w["W_xdbl"], mvecs=w["mvecs"],
                 Wqkv=w["Wqkv"], Wo=w["Wo"], W1=w["W1"], W2=w["W2"],
                 b1a=w["b1a"], b1g=w["b1g"], lvecs=w["lvecs"],
                 lrows2=w["lrows2"], Wqkv_t=w["Wqkv_t"], Wqkv_v=w["Wqkv_v"],
                 W1_t=w["W1_t"], W_in_t=w["W_in_t"],
                 **consts)
        in_maps.append(m)

    res = run_bass_kernel_spmd(nc, in_maps, core_ids=list(range(N_CORES)))
    outs = []
    for core in range(N_CORES):
        o = np.asarray(res.results[core]["out_fm"])
        outs.append(np.ascontiguousarray(o.transpose(0, 2, 1)))
    kernel.dbg = [np.asarray(res.results[c]["dbg"]) for c in range(N_CORES)]
    kernel.dbgb = [np.asarray(res.results[c]["dbgb"]) for c in range(N_CORES)]
    return np.concatenate(outs, axis=0).astype(np.float32)
